# version 08
# baseline (speedup 1.0000x reference)
import math
from contextlib import ExitStack
import numpy as np
import ml_dtypes
import concourse.bass as bass
import concourse.mybir as mybir
from concourse.bass_utils import run_bass_kernel_spmd

F32 = mybir.dt.float32
BF16 = mybir.dt.bfloat16
I32 = mybir.dt.int32
AF = mybir.ActivationFunctionType
ALU = mybir.AluOpType
AX = mybir.AxisListType

T = 4096
D = 1024
NB = D // 128
NT = T // 128
NCH = T // 512
HEADS = 16
DH = 64
PM = 256
RD = 32
QK = 96
FF0 = 3072
FF1 = 4096
RATIO = PM ** (-0.5)
KEPS = 1e-4
NZ = DH ** (-0.25)
LN_EPS = 1e-5
RMS_EPS = 1e-6
CQ_MARGIN = 20.0
QSCALE = 1.0 / math.sqrt(QK)
LNRATIO = math.log(RATIO)
DEQ = 1.0 / 127.0


class Sem:
    def __init__(self, handle):
        self.h = handle
        self.val = 0

    def step(self, n=1):
        self.val += n
        return n


def _rope_tables():
    inv_freq = 1.0 / 10000.0 ** (np.arange(0, RD, 2, dtype=np.float32) / RD)
    t = np.arange(T, dtype=np.float32)
    freqs = np.outer(t, inv_freq)
    emb = np.concatenate([freqs, freqs], -1)
    cos4 = np.tile(np.cos(emb).T.astype(np.float32), (4, 1))
    sin4 = np.tile(np.sin(emb).T.astype(np.float32), (4, 1))
    return (np.ascontiguousarray(cos4.astype(ml_dtypes.bfloat16)),
            np.ascontiguousarray(sin4.astype(ml_dtypes.bfloat16)))


def _rot128():
    R = np.zeros((32, 32), np.float32)
    for d in range(16):
        R[d, d + 16] = -1.0
        R[d + 16, d] = 1.0
    R128 = np.kron(np.eye(4, dtype=np.float32), R)
    return np.ascontiguousarray(R128.T.astype(ml_dtypes.bfloat16))


def _diag_masks():
    q = np.arange(512)[None, :]
    k = np.arange(128)[:, None]
    out = np.concatenate([(q - k >= 128 * v).astype(np.float32) for v in range(4)], axis=1)
    return np.ascontiguousarray(out.astype(ml_dtypes.bfloat16))


def build_program(upto=None):
    nc = bass.Bass("TRN2", target_bir_lowering=False)

    x_in = nc.dram_tensor("x", [T, D], F32, kind="ExternalInput").ap()
    wq = nc.dram_tensor("wq", [D, D], F32, kind="ExternalInput").ap()
    wk = nc.dram_tensor("wk", [D, D], F32, kind="ExternalInput").ap()
    wv = nc.dram_tensor("wv", [D, D], F32, kind="ExternalInput").ap()
    wo0 = nc.dram_tensor("wo0", [D, D], F32, kind="ExternalInput").ap()
    bo0 = nc.dram_tensor("bo0", [128, 8], F32, kind="ExternalInput").ap()
    projT = nc.dram_tensor("projT", [DH, PM], F32, kind="ExternalInput").ap()
    wgT = nc.dram_tensor("wgT", [D, FF0], I32, kind="ExternalInput").ap()
    wuT = nc.dram_tensor("wuT", [D, FF0], I32, kind="ExternalInput").ap()
    wdT = nc.dram_tensor("wdT", [FF0, D], I32, kind="ExternalInput").ap()
    wkv = nc.dram_tensor("wkv", [D, 256], F32, kind="ExternalInput").ap()
    gkv = nc.dram_tensor("gkv", [128, 2], F32, kind="ExternalInput").ap()
    wkdec = nc.dram_tensor("wkdec", [256, D], F32, kind="ExternalInput").ap()
    wvdec = nc.dram_tensor("wvdec", [256, D], F32, kind="ExternalInput").ap()
    wq1 = nc.dram_tensor("wq1", [D, 512], F32, kind="ExternalInput").ap()
    wqdec = nc.dram_tensor("wqdec", [512, D], F32, kind="ExternalInput").ap()
    wkrope = nc.dram_tensor("wkrope", [D, 512], F32, kind="ExternalInput").ap()
    wqrope = nc.dram_tensor("wqrope", [512, 512], F32, kind="ExternalInput").ap()
    wo1 = nc.dram_tensor("wo1", [D, D], F32, kind="ExternalInput").ap()
    wfcT = nc.dram_tensor("wfcT", [D, FF1], F32, kind="ExternalInput").ap()
    wprT = nc.dram_tensor("wprT", [FF1, D], F32, kind="ExternalInput").ap()

    y_out = nc.dram_tensor("y", [T, D], F32, kind="ExternalOutput").ap()

    def scratch(name, shape, dt):
        kind = "ExternalOutput" if upto == name else "Internal"
        return nc.dram_tensor(name, shape, dt, kind=kind).ap()

    xn0_tm = scratch("xn0_tm", [T, D], BF16)
    g0fm = scratch("g0fm", [NB, 128, T], F32)
    h0sp = scratch("h0sp", [FF0 // 128, 128, T], BF16)
    qfm_sp = scratch("qfm_sp", [NB, 128, T], BF16)
    x1_tm = scratch("x1_tm", [T, D], F32)
    xn1_tm = scratch("xn1_tm", [T, D], BF16)
    s1fm = scratch("s1fm", [NB, 128, T], F32)
    kaug_sp = scratch("kaug_sp", [HEADS, QK + 1, T], BF16)
    qaug_sp = scratch("qaug_sp", [HEADS, QK + 1, T], BF16)
    h1sp = scratch("h1sp", [FF1 // 128, 128, T], BF16)
    cc_in = nc.dram_tensor("cc_in", [128, 128], F32, kind="Internal").ap()
    cc_out = nc.dram_tensor("cc_out", [128, 128], F32, kind="Internal").ap()
    dbg = None
    if upto is not None and upto.startswith("dbg_"):
        ddt = BF16 if upto in ("dbg_ctx",) else F32
        dbg = nc.dram_tensor("dbg", [128, 8192], ddt, kind="ExternalOutput").ap()

    identf_d = nc.inline_tensor(np.eye(128, dtype=np.float32), name="c_identf").ap()
    identb_d = nc.inline_tensor(np.eye(128, dtype=np.float32).astype(ml_dtypes.bfloat16), name="c_identb").ap()
    onesrow_d = nc.inline_tensor(np.ones((1, 128), np.float32), name="c_onesrow").ap()
    onescol_d = nc.inline_tensor(np.ones((128, 1), np.float32), name="c_onescol").ap()
    dcol_d = nc.inline_tensor(np.full((128, 1), 0.5 * NZ * NZ, np.float32), name="c_dcol").ap()
    cos4_np, sin4_np = _rope_tables()
    cos4_d = nc.inline_tensor(cos4_np, name="c_cos4").ap()
    sin4_d = nc.inline_tensor(sin4_np, name="c_sin4").ap()
    rot_d = nc.inline_tensor(_rot128(), name="c_rot").ap()
    masks_d = nc.inline_tensor(_diag_masks(), name="c_masks").ap()

    es = ExitStack()
    with es:
        identf = es.enter_context(nc.sbuf_tensor("identf", [128, 128], F32))
        identb = es.enter_context(nc.sbuf_tensor("identb", [128, 128], BF16))
        onesrow = es.enter_context(nc.sbuf_tensor("onesrow", [1, 128], F32))
        onescol = es.enter_context(nc.sbuf_tensor("onescol", [128, 1], F32))
        dcol = es.enter_context(nc.sbuf_tensor("dcol", [128, 1], F32))
        cst = es.enter_context(nc.sbuf_tensor("cst", [128, 8], F32))

        PF = [es.enter_context(nc.psum_tensor(f"pf{i}", [128, 512], F32)) for i in range(6)]
        PB = [es.enter_context(nc.psum_tensor(f"pb{i}", [128, 1024], BF16)) for i in range(2)]

        sems = {}
        for name in ["ld", "wl", "mm", "ve", "sc", "gp", "st"]:
            sems[name] = Sem(es.enter_context(nc.semaphore(name)))
        ld, wl, mm, ve, sc, gp, st = (sems[k] for k in ["ld", "wl", "mm", "ve", "sc", "gp", "st"])
        ccs = Sem(es.enter_context(nc.semaphore("ccs")))

        block = es.enter_context(nc.Block())

        import os as _os0
        if _os0.environ.get("K_CC_NOCONSTLD"):
            pass
        else:
            _emit_consts = True

        @block.sync
        def _(s):
            if _os0.environ.get("K_CC_NOCONSTLD"):
                return
            s.dma_start(out=identf[:], in_=identf_d).then_inc(ld.h, ld.step(16))
            s.dma_start(out=identb[:], in_=identb_d).then_inc(ld.h, ld.step(16))
            s.dma_start(out=onesrow[:], in_=onesrow_d).then_inc(ld.h, ld.step(16))
            s.dma_start(out=onescol[:], in_=onescol_d).then_inc(ld.h, ld.step(16))
            s.dma_start(out=dcol[:], in_=dcol_d).then_inc(ld.h, ld.step(16))

        @block.vector
        def _(v):
            if _os0.environ.get("K_CC_NOCONSTLD"):
                return
            for col, val in enumerate([LN_EPS, RMS_EPS, -0.5, -1.0, 0.5]):
                v.memset(cst[:, col:col + 1], val)
            v.nop().then_inc(ve.h, ve.step())

        if upto == "dbg_cconly":
            with nc.sbuf_tensor("cc_sb2", [128, 128], F32) as cc_sb2:
                @block.vector
                def _(v):
                    v.memset(cc_sb2[:], -1e30).then_inc(ve.h, ve.step())
                @block.sync
                def _(s):
                    s.wait_ge(ve.h, ve.val)
                    s.dma_start(out=cc_in, in_=cc_sb2[:]).then_inc(st.h, st.step(16))
                @block.gpsimd
                def _(g):
                    g.wait_ge(st.h, st.val)
                    g.collective_compute(
                        "AllReduce", ALU.max,
                        replica_groups=[[0, 1, 2, 3, 4, 5, 6, 7]],
                        ins=[cc_in.opt()], outs=[cc_out.opt()],
                    ).then_inc(ccs.h)
                    ccs.step()
                @block.sync
                def _(s):
                    s.wait_ge(ccs.h, ccs.val)
                    s.dma_start(out=cc_sb2[:], in_=cc_out).then_inc(ld.h, ld.step(16))
                    s.wait_ge(ld.h, ld.val)
                    s.dma_start(out=dbg[0:128, 0:128], in_=cc_sb2[:]).then_inc(st.h, st.step(16))
            _finish(block, sems)
            return nc, es

        wgwu_es = ExitStack()
        wg_sb = wgwu_es.enter_context(nc.sbuf_tensor("wg_sb", [128, NB * FF0], BF16))
        wu_sb = wgwu_es.enter_context(nc.sbuf_tensor("wu_sb", [128, NB * FF0], BF16))

        @block.gpsimd
        def _(g):
            for k in range(NB):
                g.dma_start(out=wg_sb[:, k * FF0:(k + 1) * FF0], in_=wgT[k * 128:(k + 1) * 128, :]).then_inc(wl.h, wl.step(16))
                g.dma_start(out=wu_sb[:, k * FF0:(k + 1) * FF0], in_=wuT[k * 128:(k + 1) * 128, :]).then_inc(wl.h, wl.step(16))
        p1_w_ld = wl.val

        p0_xc_done = {}
        p0_st_done = {}
        with (
            nc.sbuf_tensor("p0_x", [128, 2 * D], F32) as p0x,
            nc.sbuf_tensor("p0_c", [128, 2 * D], F32) as p0c,
            nc.sbuf_tensor("p0_sq", [128, D], F32) as p0sq,
            nc.sbuf_tensor("p0_st", [128, 8], F32) as p0st,
            nc.sbuf_tensor("p0_o", [128, 2 * D], BF16) as p0o,
        ):
            for i in range(NT):
                b = i % 2
                @block.sync
                def _(s, i=i, b=b):
                    s.wait_ge(ld.h, ld.val)
                    if i >= 2:
                        s.wait_ge(ve.h, p0_xc_done[i - 2])
                    s.dma_start(out=p0x[:, b * D:(b + 1) * D], in_=x_in[i * 128:(i + 1) * 128, :]).then_inc(ld.h, ld.step(16))

                @block.vector
                def _(v, i=i, b=b):
                    v.wait_ge(ld.h, ld.val)
                    xt = p0x[:, b * D:(b + 1) * D]
                    xc = p0c[:, b * D:(b + 1) * D]
                    v.reduce_sum(p0st[:, 0:1], xt, axis=AX.X)
                    v.drain()
                    v.tensor_scalar(p0st[:, 1:2], p0st[:, 0:1], 1.0 / D, None, ALU.mult)
                    v.drain()
                    v.tensor_scalar(xc, xt, p0st[:, 1:2], None, ALU.subtract).then_inc(ve.h, ve.step())
                    p0_xc_done[i] = ve.val

                @block.vector
                def _(v, i=i, b=b):
                    v.drain()
                    v.tensor_tensor(p0sq[:], p0c[:, b * D:(b + 1) * D], p0c[:, b * D:(b + 1) * D], ALU.mult)
                    v.drain()
                    v.reduce_sum(p0st[:, 2:3], p0sq[:], axis=AX.X)
                    v.drain()
                    v.tensor_scalar(p0st[:, 3:4], p0st[:, 2:3], 1.0 / D, None, ALU.mult).then_inc(ve.h, ve.step())

                @block.scalar
                def _(scl, i=i, b=b):
                    scl.wait_ge(ve.h, ve.val)
                    scl.activation(p0st[:, 4:5], p0st[:, 3:4], AF.Ln, bias=cst[:, 0:1])
                    scl.drain()
                    scl.activation(p0st[:, 5:6], p0st[:, 4:5], AF.Exp, scale=-0.5).then_inc(sc.h, sc.step())

                @block.vector
                def _(v, i=i, b=b):
                    v.wait_ge(sc.h, sc.val)
                    v.nop().then_inc(ve.h, ve.step())
                    if i >= 2:
                        v.wait_ge(st.h, st.val)
                    v.tensor_scalar(p0o[:, b * D:(b + 1) * D], p0c[:, b * D:(b + 1) * D],
                                    p0st[:, 5:6], None, ALU.mult).then_inc(ve.h, ve.step())

                @block.sync
                def _(s, i=i, b=b):
                    s.wait_ge(st.h, st.val)
                    s.wait_ge(ve.h, ve.val)
                    s.dma_start(out=xn0_tm[i * 128:(i + 1) * 128, :], in_=p0o[:, b * D:(b + 1) * D]).then_inc(st.h, st.step(16))
                    p0_st_done[i] = st.val

        if upto == "xn0_tm":
            _finish(block, sems)
            return nc, es
        p0_store_total = st.val
        import os as _os

        phase_barrier(block, sems)
        with (
            nc.sbuf_tensor("p1_xnf", [128, 2 * NB * 512], BF16) as p1_xnf,
            nc.sbuf_tensor("p1_sg", [128, 2 * 512], F32) as p1_sg,
            nc.sbuf_tensor("p1_h", [128, 2 * 512], BF16) as p1_h,
        ):
            xnf_free = {}
            silu_done = {}
            stt_done = {}
            hst_done = {}
            _p1n = int(_os.environ.get("K_P1_CHUNKS", str(NCH)))
            _p1mt = int(_os.environ.get("K_P1_MT", str(FF0 // 128)))
            for c in range(_p1n):
                cb = c % 2
                @block.sync
                def _(s, c=c, cb=cb):
                    s.wait_ge(ld.h, ld.val)
                    if c == 0:
                        s.wait_ge(st.h, p0_store_total)
                    if c >= 2:
                        s.wait_ge(mm.h, xnf_free[c - 2])
                    for j in range(NB):
                        if _os.environ.get("K_P1_NOTR"):
                            s.dma_start(out=p1_xnf[:, (cb * NB + j) * 512:(cb * NB + j + 1) * 512],
                                        in_=xn0_tm[j * 128:(j + 1) * 128, 0:512]).then_inc(ld.h, ld.step(16))
                        else:
                            s.dma_start(out=p1_xnf[:, (cb * NB + j) * 512:(cb * NB + j + 1) * 512],
                                        in_=xn0_tm[c * 512:(c + 1) * 512, j * 128:(j + 1) * 128],
                                        transpose=True).then_inc(ld.h, ld.step(16))

                ld_chunk = ld.val
                for mt in range(_p1mt):
                    a = mt % 2
                    g_idx = c * 24 + mt
                    @block.tensor
                    def _(te, c=c, cb=cb, mt=mt, a=a, g_idx=g_idx, ld_chunk=ld_chunk):
                        if g_idx == 0:
                            te.wait_ge(wl.h, p1_w_ld)
                        if mt == 0:
                            te.wait_ge(ld.h, ld_chunk)
                        if g_idx >= 2:
                            te.wait_ge(sc.h, silu_done[g_idx - 2])
                            te.wait_ge(ve.h, stt_done[g_idx - 2])
                        for k in range(NB):
                            ins = te.matmul(PF[a][:, 0:512], wg_sb[:, k * FF0 + mt * 128: k * FF0 + (mt + 1) * 128],
                                            p1_xnf[:, (cb * NB + k) * 512:(cb * NB + k + 1) * 512],
                                            start=(k == 0), stop=(k == NB - 1))
                            if k == NB - 1:
                                ins.then_inc(mm.h, mm.step())
                        for k in range(NB):
                            ins = te.matmul(PF[2 + a][:, 0:512], wu_sb[:, k * FF0 + mt * 128: k * FF0 + (mt + 1) * 128],
                                            p1_xnf[:, (cb * NB + k) * 512:(cb * NB + k + 1) * 512],
                                            start=(k == 0), stop=(k == NB - 1))
                            if k == NB - 1:
                                ins.then_inc(mm.h, mm.step())
                    mm_gate = mm.val - 1
                    mm_up = mm.val
                    @block.scalar
                    def _(scl, a=a, g_idx=g_idx, mm_gate=mm_gate):
                        scl.wait_ge(mm.h, mm_gate)
                        if g_idx >= 2:
                            scl.wait_ge(ve.h, stt_done[g_idx - 2])
                        if _os.environ.get("K_P1_NOSILU"):
                            scl.activation(p1_sg[:, a * 512:(a + 1) * 512], PF[a][:, 0:512], AF.Square, scale=DEQ).then_inc(sc.h, sc.step())
                        else:
                            scl.activation(p1_sg[:, a * 512:(a + 1) * 512], PF[a][:, 0:512], AF.Silu, scale=DEQ).then_inc(sc.h, sc.step())
                    silu_done[g_idx] = sc.val
                    @block.vector
                    def _(v, a=a, g_idx=g_idx, mm_up=mm_up):
                        v.wait_ge(mm.h, mm_up)
                        v.wait_ge(sc.h, silu_done[g_idx])
                        if g_idx >= 2:
                            v.wait_ge(gp.h, gp.val)
                        v.scalar_tensor_tensor(p1_h[:, a * 512:(a + 1) * 512],
                                               p1_sg[:, a * 512:(a + 1) * 512], DEQ,
                                               PF[2 + a][:, 0:512], ALU.mult, ALU.mult).then_inc(ve.h, ve.step())
                    stt_done[g_idx] = ve.val
                    if _os.environ.get("K_P1_NOST"):
                        @block.sync
                        def _(s, g_idx=g_idx):
                            s.wait_ge(ve.h, stt_done[g_idx])
                            s.nop().then_inc(gp.h, gp.step(16))
                    else:
                        @block.gpsimd
                        def _(g, c=c, mt=mt, a=a, g_idx=g_idx):
                            g.wait_ge(gp.h, gp.val)
                            g.wait_ge(ve.h, stt_done[g_idx])
                            g.dma_start(out=h0sp[mt, :, c * 512:(c + 1) * 512],
                                        in_=p1_h[:, a * 512:(a + 1) * 512]).then_inc(gp.h, gp.step(16))
                    hst_done[g_idx] = gp.val
                xnf_free[c] = mm.val

        wgwu_es.close()
        if upto == "h0sp":
            _finish(block, sems)
            return nc, es

        phase_barrier(block, sems)
        with (
            nc.sbuf_tensor("wd_sb", [128, (FF0 // 128) * D], BF16) as wd_sb,
            nc.sbuf_tensor("p2_h", [128, 2 * (FF0 // 128) * 512], BF16) as p2_h,
            nc.sbuf_tensor("p2_g", [128, 2 * 512], F32) as p2_g,
        ):
            @block.gpsimd
            def _(g):
                for k in range(FF0 // 128):
                    g.dma_start(out=wd_sb[:, k * D:(k + 1) * D], in_=wdT[k * 128:(k + 1) * 128, :]).then_inc(wl.h, wl.step(16))
            p2_w_ld = wl.val

            h_free = {}
            g_done = {}
            g_store = {}
            for c in range(NCH):
                cb = c % 2
                @block.sync
                def _(s, c=c, cb=cb):
                    s.wait_ge(ld.h, ld.val)
                    if c >= 2:
                        s.wait_ge(mm.h, h_free[c - 2])
                    for part in range(4):
                        s.dma_start(out=p2_h[:, (cb * 24 + part * 6) * 512:(cb * 24 + (part + 1) * 6) * 512].rearrange("p (k t) -> p k t", k=6),
                                    in_=h0sp[part * 6:(part + 1) * 6, :, c * 512:(c + 1) * 512].rearrange("k p t -> p k t")).then_inc(ld.h, ld.step(16))

                ld_chunk = ld.val
                for mt in range(NB):
                    a = mt % 2
                    g_idx = c * NB + mt
                    @block.tensor
                    def _(te, cb=cb, mt=mt, a=a, g_idx=g_idx, ld_chunk=ld_chunk):
                        if g_idx == 0:
                            te.wait_ge(wl.h, p2_w_ld)
                        if mt == 0:
                            te.wait_ge(ld.h, ld_chunk)
                        if g_idx >= 2:
                            te.wait_ge(sc.h, g_done[g_idx - 2])
                        for k in range(FF0 // 128):
                            ins = te.matmul(PF[a][:, 0:512], wd_sb[:, k * D + mt * 128: k * D + (mt + 1) * 128],
                                            p2_h[:, (cb * 24 + k) * 512:(cb * 24 + k + 1) * 512],
                                            start=(k == 0), stop=(k == FF0 // 128 - 1))
                            if k == FF0 // 128 - 1:
                                ins.then_inc(mm.h, mm.step())
                    mm_y = mm.val
                    @block.scalar
                    def _(scl, a=a, g_idx=g_idx, mm_y=mm_y):
                        scl.wait_ge(mm.h, mm_y)
                        if g_idx >= 2:
                            scl.wait_ge(st.h, st.val)
                        scl.activation(p2_g[:, a * 512:(a + 1) * 512], PF[a][:, 0:512], AF.Silu, scale=DEQ).then_inc(sc.h, sc.step())
                    g_done[g_idx] = sc.val
                    @block.sync
                    def _(s, c=c, mt=mt, a=a, g_idx=g_idx):
                        s.wait_ge(st.h, st.val)
                        s.wait_ge(sc.h, g_done[g_idx])
                        s.dma_start(out=g0fm[mt, :, c * 512:(c + 1) * 512],
                                    in_=p2_g[:, a * 512:(a + 1) * 512]).then_inc(st.h, st.step(16))
                    g_store[g_idx] = st.val
                h_free[c] = mm.val

        if upto == "g0fm":
            _finish(block, sems)
            return nc, es

        phase_barrier(block, sems)
        with (
            nc.sbuf_tensor("ctx_sb", [128, HEADS * 130], BF16) as ctx_sb,
            nc.sbuf_tensor("diagk_sb", [128, HEADS * NT], F32) as diagk_sb,
            nc.sbuf_tensor("stt0", [128, 16], F32) as stt0,
            nc.sbuf_tensor("projTs", [128, PM], BF16) as projTs,
            nc.sbuf_tensor("gmax_sb", [1, 8], F32) as gmax_sb,
            nc.sbuf_tensor("cc_sb", [128, 128], F32) as cc_sb,
        ):
            with (
                nc.sbuf_tensor("k_fm", [128, NB * T], BF16) as k_fm,
                nc.sbuf_tensor("v0_tm", [128, NT * HEADS * 65], BF16) as v0_tm,
            ):
                with (
                    nc.sbuf_tensor("wq_sb", [128, NB * D], BF16) as wq_sb,
                    nc.sbuf_tensor("wk_sb", [128, NB * D], BF16) as wk_sb,
                    nc.sbuf_tensor("p3_xnf", [128, 2 * NB * 512], BF16) as p3_xnf,
                    nc.sbuf_tensor("p3_q", [128, 2 * 512], BF16) as p3_q,
                    nc.sbuf_tensor("projTf", [128, PM], F32) as projTf,
                ):
                    @block.gpsimd
                    def _(g):
                        for k in range(NB):
                            g.dma_start(out=wq_sb[:, k * D:(k + 1) * D], in_=wq[k * 128:(k + 1) * 128, :]).then_inc(wl.h, wl.step(16))
                            g.dma_start(out=wk_sb[:, k * D:(k + 1) * D], in_=wk[k * 128:(k + 1) * 128, :]).then_inc(wl.h, wl.step(16))
                    p3_w_ld = wl.val

                    @block.sync
                    def _(s):
                        s.dma_start(out=projTf[0:64, :], in_=projT).then_inc(ld.h, ld.step(16))
                        s.dma_start(out=projTf[64:128, :], in_=projT).then_inc(ld.h, ld.step(16))
                    @block.vector
                    def _(v):
                        v.wait_ge(ld.h, ld.val)
                        v.tensor_scalar(projTs[:], projTf[:], NZ, None, ALU.mult).then_inc(ve.h, ve.step())
                        v.memset(v0_tm[:], 1.0).then_inc(ve.h, ve.step())

                    qev_done = {}
                    qst_done = {}
                    for c in range(NCH):
                        cb = c % 2
                        ld_before = ld.val
                        @block.sync
                        def _(s, c=c, cb=cb):
                            s.wait_ge(ld.h, ld.val)
                            if c >= 2:
                                s.wait_ge(mm.h, qev_done[(c - 2, "mmend")])
                            for j in range(NB):
                                s.dma_start(out=p3_xnf[:, (cb * NB + j) * 512:(cb * NB + j + 1) * 512],
                                            in_=xn0_tm[c * 512:(c + 1) * 512, j * 128:(j + 1) * 128],
                                            transpose=True).then_inc(ld.h, ld.step(16))
                        ld_chunk = ld.val
                        for mt in range(NB):
                            a = mt % 2
                            g_idx = c * NB + mt
                            @block.tensor
                            def _(te, c=c, cb=cb, mt=mt, a=a, g_idx=g_idx, ld_chunk=ld_chunk):
                                if g_idx == 0:
                                    te.wait_ge(wl.h, p3_w_ld)
                                if mt == 0:
                                    te.wait_ge(ld.h, ld_chunk)
                                if g_idx >= 1:
                                    te.wait_ge(ve.h, qev_done[g_idx - 1])
                                for k in range(NB):
                                    ins = te.matmul(PF[a][:, 0:512], wq_sb[:, k * D + mt * 128: k * D + (mt + 1) * 128],
                                                    p3_xnf[:, (cb * NB + k) * 512:(cb * NB + k + 1) * 512],
                                                    start=(k == 0), stop=(k == NB - 1))
                                    if k == NB - 1:
                                        ins.then_inc(mm.h, mm.step())
                                for k in range(NB):
                                    ins = te.matmul(PF[2 + a][:, 0:512], wk_sb[:, k * D + mt * 128: k * D + (mt + 1) * 128],
                                                    p3_xnf[:, (cb * NB + k) * 512:(cb * NB + k + 1) * 512],
                                                    start=(k == 0), stop=(k == NB - 1))
                                    if k == NB - 1:
                                        ins.then_inc(mm.h, mm.step())
                            mm_q = mm.val - 1
                            mm_k = mm.val
                            @block.vector
                            def _(v, c=c, mt=mt, a=a, g_idx=g_idx, mm_q=mm_q, mm_k=mm_k):
                                v.wait_ge(mm.h, mm_q)
                                v.wait_ge(gp.h, gp.val)
                                v.tensor_copy(p3_q[:, a * 512:(a + 1) * 512], PF[a][:, 0:512]).then_inc(ve.h, ve.step())
                                v.wait_ge(mm.h, mm_k)
                                v.tensor_copy(k_fm[:, mt * T + c * 512: mt * T + (c + 1) * 512], PF[2 + a][:, 0:512]).then_inc(ve.h, ve.step())
                            qev_done[g_idx] = ve.val
                            @block.gpsimd
                            def _(g, c=c, mt=mt, a=a, g_idx=g_idx):
                                g.wait_ge(gp.h, gp.val)
                                g.wait_ge(ve.h, qev_done[g_idx] - 1)
                                g.dma_start(out=qfm_sp[mt, :, c * 512:(c + 1) * 512],
                                            in_=p3_q[:, a * 512:(a + 1) * 512]).then_inc(gp.h, gp.step(16))
                            qst_done[g_idx] = gp.val
                        qev_done[(c, "mmend")] = mm.val

                phase_barrier(block, sems)
                with (
                    nc.sbuf_tensor("wv_sb", [128, NB * D], BF16) as wv_sb,
                    nc.sbuf_tensor("p3b_xnf", [128, 2 * NB * 512], BF16) as p3b_xnf,
                ):
                    @block.gpsimd
                    def _(g):
                        for k in range(NB):
                            g.dma_start(out=wv_sb[:, k * D:(k + 1) * D], in_=wv[k * 128:(k + 1) * 128, :]).then_inc(wl.h, wl.step(16))
                    p3b_w_ld = wl.val

                    vev_done = {}
                    for c in range(NCH):
                        cb = c % 2
                        @block.sync
                        def _(s, c=c, cb=cb):
                            s.wait_ge(ld.h, ld.val)
                            if c >= 2:
                                s.wait_ge(mm.h, vev_done[(c - 2, "mmend")])
                            for j in range(NB):
                                s.dma_start(out=p3b_xnf[:, (cb * NB + j) * 512:(cb * NB + j + 1) * 512],
                                            in_=xn0_tm[c * 512:(c + 1) * 512, j * 128:(j + 1) * 128],
                                            transpose=True).then_inc(ld.h, ld.step(16))
                        ld_chunk = ld.val
                        for tt in range(4):
                            for nh in range(2):
                                a = (tt * 2 + nh) % 2
                                g_idx = c * 8 + tt * 2 + nh
                                @block.tensor
                                def _(te, c=c, cb=cb, tt=tt, nh=nh, a=a, g_idx=g_idx, ld_chunk=ld_chunk):
                                    if g_idx == 0:
                                        te.wait_ge(wl.h, p3b_w_ld)
                                    if tt == 0 and nh == 0:
                                        te.wait_ge(ld.h, ld_chunk)
                                    if g_idx >= 2:
                                        te.wait_ge(ve.h, vev_done[g_idx - 2])
                                    for k in range(NB):
                                        ins = te.matmul(PF[a][:, 0:512],
                                                        p3b_xnf[:, (cb * NB + k) * 512 + tt * 128: (cb * NB + k) * 512 + (tt + 1) * 128],
                                                        wv_sb[:, k * D + nh * 512: k * D + (nh + 1) * 512],
                                                        start=(k == 0), stop=(k == NB - 1))
                                        if k == NB - 1:
                                            ins.then_inc(mm.h, mm.step())
                                mm_v = mm.val
                                @block.vector
                                def _(v, c=c, tt=tt, nh=nh, a=a, mm_v=mm_v):
                                    v.wait_ge(mm.h, mm_v)
                                    base = (c * 4 + tt) * HEADS * 65 + nh * 8 * 65
                                    dst = v0_tm[:, base: base + 8 * 65].rearrange("p (h e) -> p h e", h=8)[:, :, 0:64]
                                    v.tensor_copy(dst, PF[a][:, 0:512].rearrange("p (h e) -> p h e", h=8)).then_inc(ve.h, ve.step())
                                vev_done[g_idx] = ve.val
                        vev_done[(c, "mmend")] = mm.val

                phase_barrier(block, sems)
                with (
                    nc.sbuf_tensor("k2_sb", [128, T], F32) as k2_sb,
                    nc.sbuf_tensor("kp_f", [128, 2 * PM], F32) as kp_f,
                    nc.sbuf_tensor("kp_b", [128, 2 * PM], BF16) as kp_b,
                    nc.sbuf_tensor("ctxe", [128, PM], BF16) as ctxe,
                ):
                    @block.vector
                    def _(v):
                        v.memset(cc_sb[:], -1e30).then_inc(ve.h, ve.step())
                    for h in range(HEADS):
                        r0 = (h % 2) * 64
                        blk = h // 2
                        @block.vector
                        def _(v, h=h, r0=r0, blk=blk):
                            if h > 0:
                                v.wait_ge(mm.h, mm.val)
                            v.drain()
                            v.tensor_tensor(k2_sb[r0:r0 + 64, :], k_fm[r0:r0 + 64, blk * T:(blk + 1) * T],
                                            k_fm[r0:r0 + 64, blk * T:(blk + 1) * T], ALU.mult).then_inc(ve.h, ve.step())
                        sq_done = ("ve", ve.val)
                        for tt in range(NT):
                            a = tt % 2
                            dbank = 4 + (tt // 8) % 2
                            @block.tensor
                            def _(te, h=h, r0=r0, blk=blk, tt=tt, a=a, dbank=dbank, sq_done=sq_done):
                                if tt == 0:
                                    te.wait_ge(ve.h, sq_done[1])
                                te.wait_ge(ve.h, ve.val)
                                te.matmul(PF[a][:, 0:PM], k_fm[r0:r0 + 64, blk * T + tt * 128: blk * T + (tt + 1) * 128],
                                          projTs[r0:r0 + 64, :], start=True, stop=True).then_inc(mm.h, mm.step())
                                if tt % 8 == 0 and (h * NT + tt) >= 16:
                                    te.wait_ge(ve.h, ve.val)
                                te.matmul(PF[dbank][:, (tt % 8):(tt % 8) + 1], k2_sb[r0:r0 + 64, tt * 128:(tt + 1) * 128],
                                          dcol[r0:r0 + 64, :], start=True, stop=True).then_inc(mm.h, mm.step())
                            mm_dd = mm.val - 1
                            mm_dg = mm.val
                            @block.vector
                            def _(v, h=h, tt=tt, a=a, dbank=dbank, mm_dd=mm_dd, mm_dg=mm_dg):
                                v.wait_ge(mm.h, mm_dd)
                                v.reduce_max(stt0[:, 2 + a:3 + a], PF[a][:, 0:PM], axis=AX.X)
                                v.drain()
                                if h == 0 and tt == 0:
                                    v.tensor_copy(stt0[:, 0:1], stt0[:, 2:3]).then_inc(ve.h, ve.step())
                                else:
                                    v.tensor_tensor(stt0[:, 0:1], stt0[:, 0:1], stt0[:, 2 + a:3 + a], ALU.max).then_inc(ve.h, ve.step())
                                if tt % 8 == 7:
                                    v.wait_ge(mm.h, mm_dg)
                                    v.tensor_copy(diagk_sb[:, h * NT + tt - 7: h * NT + tt + 1], PF[dbank][:, 0:8]).then_inc(ve.h, ve.step())
                    @block.tensor
                    def _(te):
                        te.wait_ge(ve.h, ve.val)
                        te.transpose(PF[0][0:1, 0:128], stt0[:, 0:1], identf[:]).then_inc(mm.h, mm.step())
                    @block.vector
                    def _(v):
                        v.wait_ge(mm.h, mm.val)
                        v.reduce_max(gmax_sb[0:1, 0:1], PF[0][0:1, 0:128], axis=AX.X)
                        v.drain()
                        v.tensor_copy(cc_sb[0:1, 0:1], gmax_sb[0:1, 0:1]).then_inc(ve.h, ve.step())
                    @block.sync
                    def _(s):
                        s.wait_ge(ve.h, ve.val)
                        s.dma_start(out=cc_in, in_=cc_sb[:]).then_inc(st.h, st.step(16))
                    if _os.environ.get("K_NOCC"):
                        @block.gpsimd
                        def _(g):
                            g.wait_ge(st.h, st.val)
                            g.dma_start(out=cc_out, in_=cc_in).then_inc(ccs.h, 16)
                            ccs.step(16)
                    else:
                        @block.gpsimd
                        def _(g):
                            g.wait_ge(st.h, st.val)
                            g.collective_compute(
                                "AllReduce", ALU.max,
                                replica_groups=[[0, 1, 2, 3, 4, 5, 6, 7]],
                                ins=[cc_in.opt()], outs=[cc_out.opt()],
                            ).then_inc(ccs.h)
                            ccs.step()
                    @block.sync
                    def _(s):
                        s.wait_ge(ccs.h, ccs.val)
                        s.dma_start(out=gmax_sb[0:1, 0:8], in_=cc_out[0:1, 0:8]).then_inc(ld.h, ld.step(16))
                    @block.tensor
                    def _(te):
                        te.wait_ge(ld.h, ld.val)
                        te.matmul(PF[1][:, 0:1], onesrow[:], gmax_sb[0:1, 0:1], start=True, stop=True).then_inc(mm.h, mm.step())
                    @block.vector
                    def _(v):
                        v.wait_ge(mm.h, mm.val)
                        v.tensor_scalar(stt0[:, 1:2], PF[1][:, 0:1], -1.0, LNRATIO, ALU.mult, ALU.add).then_inc(ve.h, ve.step())
                        v.drain()

                    if upto == "dbg_gmax":
                        @block.sync
                        def _(s):
                            s.wait_ge(ve.h, ve.val)
                            s.dma_start(out=dbg[0:1, 0:8], in_=gmax_sb[0:1, 0:8]).then_inc(st.h, st.step(16))
                        _finish(block, sems)
                        return nc, es

                    kp_done = {}
                    ctxev = {}
                    prev_ctx_ev = None
                    for h in range(HEADS):
                        r0 = (h % 2) * 64
                        blk = h // 2
                        mm0, sc0, ve0 = mm.val, sc.val, ve.val
                        dd_of = lambda tt: mm0 + 2 * tt + 1
                        ctx_of = lambda tt: mm0 + 2 * tt + 2
                        bias_of = lambda tt: ve0 + 2 * tt + 1
                        cvt_of = lambda tt: ve0 + 2 * tt + 2
                        exp_of = lambda tt: sc0 + tt + 1
                        @block.tensor
                        def _(te, h=h, r0=r0, blk=blk, pce=prev_ctx_ev, ve0=ve0):
                            for tt in range(NT):
                                a = tt % 2
                                if tt == 0:
                                    te.wait_ge(ve.h, ve0)
                                if tt >= 2:
                                    te.wait_ge(sc.h, exp_of(tt - 2))
                                te.matmul(PF[a][:, 0:PM], k_fm[r0:r0 + 64, blk * T + tt * 128: blk * T + (tt + 1) * 128],
                                          projTs[r0:r0 + 64, :], start=True, stop=True).then_inc(mm.h, mm.step())
                                te.wait_ge(ve.h, cvt_of(tt))
                                if tt == 0 and pce is not None:
                                    te.wait_ge(ve.h, pce)
                                te.matmul(PF[2][0:65, 0:PM], v0_tm[:, (tt * HEADS + h) * 65: (tt * HEADS + h) * 65 + 65],
                                          kp_b[:, a * PM:(a + 1) * PM],
                                          start=(tt == 0), stop=(tt == NT - 1)).then_inc(mm.h, mm.step())
                        @block.vector
                        def _(v, h=h):
                            for tt in range(NT):
                                a = tt % 2
                                if tt >= 2:
                                    v.wait_ge(sc.h, exp_of(tt - 2))
                                v.tensor_scalar(stt0[:, 4 + a:5 + a], diagk_sb[:, h * NT + tt: h * NT + tt + 1],
                                                -1.0, stt0[:, 1:2], ALU.mult, ALU.add).then_inc(ve.h, ve.step())
                                v.wait_ge(sc.h, exp_of(tt))
                                if tt >= 2:
                                    v.wait_ge(mm.h, ctx_of(tt - 2))
                                v.tensor_scalar(kp_b[:, a * PM:(a + 1) * PM], kp_f[:, a * PM:(a + 1) * PM],
                                                RATIO * KEPS, None, ALU.add).then_inc(ve.h, ve.step())
                        @block.scalar
                        def _(scl, h=h):
                            for tt in range(NT):
                                a = tt % 2
                                scl.wait_ge(mm.h, dd_of(tt))
                                scl.wait_ge(ve.h, bias_of(tt))
                                if tt >= 2:
                                    scl.wait_ge(ve.h, cvt_of(tt - 2))
                                scl.activation(kp_f[:, a * PM:(a + 1) * PM], PF[a][:, 0:PM], AF.Exp,
                                               bias=stt0[:, 4 + a:5 + a]).then_inc(sc.h, sc.step())
                        assert mm.val == mm0 + 2 * NT and ve.val == ve0 + 2 * NT and sc.val == sc0 + NT
                        mm_ctx = mm.val
                        @block.vector
                        def _(v, h=h, mm_ctx=mm_ctx):
                            v.wait_ge(mm.h, mm_ctx)
                            v.tensor_copy(ctxe[0:65, 0:PM], PF[2][0:65, 0:PM]).then_inc(ve.h, ve.step())
                        ctx_ev = ve.val
                        @block.tensor
                        def _(te, h=h, ctx_ev=ctx_ev):
                            te.wait_ge(ve.h, ctx_ev)
                            te.transpose(PB[0][0:128, 0:65], ctxe[0:65, 0:128], identb[0:65, 0:65]).then_inc(mm.h, mm.step())
                            te.transpose(PB[1][0:128, 0:65], ctxe[0:65, 128:256], identb[0:65, 0:65]).then_inc(mm.h, mm.step())
                        mm_tr = mm.val
                        @block.vector
                        def _(v, h=h, mm_tr=mm_tr):
                            v.wait_ge(mm.h, mm_tr)
                            v.tensor_copy(ctx_sb[:, h * 130: h * 130 + 65], PB[0][:, 0:65])
                            v.drain()
                            v.tensor_copy(ctx_sb[:, h * 130 + 65: h * 130 + 130], PB[1][:, 0:65]).then_inc(ve.h, ve.step())
                        prev_ctx_ev = ve.val

                    if upto == "dbg_ctx":
                        @block.sync
                        def _(s):
                            s.wait_ge(ve.h, ve.val)
                            s.dma_start(out=dbg[:, 0:HEADS * 130], in_=ctx_sb[:]).then_inc(st.h, st.step(16))
                        _finish(block, sems)
                        return nc, es

            _emit_p6(dict(nc=nc, block=block, sems=sems, PF=PF, PB=PB, identf=identf,
                          identb=identb, onesrow=onesrow, dcol=dcol, cst=cst, projTs=projTs,
                          ctx_sb=ctx_sb, stt0=stt0, x_in=x_in, wo0=wo0, bo0=bo0,
                          qfm_sp=qfm_sp, g0fm=g0fm, x1_tm=x1_tm, xn1_tm=xn1_tm))

        if upto in ("x1_tm", "xn1_tm"):
            _finish(block, sems)
            return nc, es

        phase_barrier(block, sems)
        v1_es = ExitStack()
        v1_tm = v1_es.enter_context(nc.sbuf_tensor("v1_tm", [128, NT * HEADS * 65], BF16, side="right"))
        _emit_p7(dict(nc=nc, block=block, sems=sems, ccs=ccs, PF=PF, PB=PB, identf=identf,
                      identb=identb, onesrow=onesrow, onescol=onescol, cst=cst,
                      xn1_tm=xn1_tm, kaug_sp=kaug_sp, qaug_sp=qaug_sp, v1_tm=v1_tm,
                      wkv=wkv, gkv=gkv, wkdec=wkdec, wvdec=wvdec, wq1=wq1, wqdec=wqdec,
                      wkrope=wkrope, wqrope=wqrope, cos4_d=cos4_d, sin4_d=sin4_d, rot_d=rot_d,
                      upto=upto, dbg=dbg))
        if upto in ("kaug_sp", "qaug_sp"):
            _finish(block, sems)
            return nc, es

        E2 = dict(nc=nc, block=block, sems=sems, PF=PF, PB=PB, identf=identf, identb=identb,
                  onesrow=onesrow, onescol=onescol, cst=cst, masks_d=masks_d,
                  xn1_tm=xn1_tm, x1_tm=x1_tm, kaug_sp=kaug_sp, qaug_sp=qaug_sp, v1_tm=v1_tm,
                  s1fm=s1fm, h1sp=h1sp, wo1=wo1, wfcT=wfcT, wprT=wprT, y_out=y_out,
                  upto=upto, dbg=dbg, es=es, v1_es=v1_es)
        r = _emit_p8_p11(E2)
        if r:
            _finish(block, sems)
            return nc, es

        _finish(block, sems)
    return nc, es


def _emit_p8_p11(E):
    (nc, block, sems, PF, PB, identf, identb, onesrow, onescol, cst, masks_d,
     xn1_tm, x1_tm, kaug_sp, qaug_sp, v1_tm, s1fm, h1sp, wo1, wfcT, wprT, y_out,
     upto, dbg, es) = (
        E["nc"], E["block"], E["sems"], E["PF"], E["PB"], E["identf"], E["identb"],
        E["onesrow"], E["onescol"], E["cst"], E["masks_d"], E["xn1_tm"], E["x1_tm"],
        E["kaug_sp"], E["qaug_sp"], E["v1_tm"], E["s1fm"], E["h1sp"], E["wo1"],
        E["wfcT"], E["wprT"], E["y_out"], E["upto"], E["dbg"], E["es"])
    ld, wl, mm, ve, sc, gp, st = (sems[k] for k in ["ld", "wl", "mm", "ve", "sc", "gp", "st"])

    phase_barrier(block, sems)
    attn1_es = ExitStack()
    attn1 = attn1_es.enter_context(nc.sbuf_tensor("attn1", [128, NB * T], BF16))
    p8 = ExitStack()
    _a = p8.enter_context
    ka_sb = _a(nc.sbuf_tensor("p8_ka", [97, 2 * T], BF16))
    qa_sb = _a(nc.sbuf_tensor("p8_qa", [97, 2 * T], BF16))
    pT = _a(nc.sbuf_tensor("p8_pT", [128, 3 * 512], BF16))
    mask_sb = _a(nc.sbuf_tensor("p8_mask", [128, 4 * 512], BF16))
    dinv = _a(nc.sbuf_tensor("p8_dinv", [1, 512], F32))
    tmpo = _a(nc.sbuf_tensor("p8_tmpo", [64, 512], F32))

    @block.sync
    def _(s):
        s.wait_ge(ld.h, ld.val)
        s.dma_start(out=mask_sb[:], in_=masks_d).then_inc(ld.h, ld.step(16))
        s.dma_start(out=ka_sb[:, 0:T], in_=kaug_sp[0]).then_inc(ld.h, ld.step(16))
        s.dma_start(out=qa_sb[:, 0:T], in_=qaug_sp[0]).then_inc(ld.h, ld.step(16))
    hd_ld = {0: ld.val}
    hd_mm_done = {}
    for h in range(HEADS):
        hb = h % 2
        r0 = (h % 2) * 64
        if h + 1 < HEADS:
            @block.sync
            def _(s, h=h):
                s.wait_ge(ld.h, ld.val)
                if h >= 1:
                    s.wait_ge(mm.h, hd_mm_done[h - 1])
                s.dma_start(out=ka_sb[:, ((h + 1) % 2) * T:((h + 1) % 2 + 1) * T], in_=kaug_sp[h + 1]).then_inc(ld.h, ld.step(16))
                s.dma_start(out=qa_sb[:, ((h + 1) % 2) * T:((h + 1) % 2 + 1) * T], in_=qaug_sp[h + 1]).then_inc(ld.h, ld.step(16))
            hd_ld[h + 1] = ld.val
        if h == 0:
            prev_evac = None
        for c in range(NCH):
            nt8 = 4 * c + 4
            mm0, sc0, ve0 = mm.val, ve.val, sc.val
            sc0 = sc.val
            ve0 = ve.val
            mm0 = mm.val
            mm_s_of = lambda i: mm0 + (1 if i == 0 else 2 * i)
            pv_of = lambda i: mm0 + 2 * i + 3 if i < nt8 - 1 else mm0 + 2 * nt8
            exp_of = lambda i: sc0 + i + 1
            mask_of = lambda i: ve0 + (i - 4 * c) + 1
            @block.tensor
            def _(te, h=h, hb=hb, c=c, nt8=nt8, prev_ev=prev_evac):
                def s_mm(i):
                    sb3 = [0, 1, 4][i % 3]
                    if i == 0 and c == 0:
                        te.wait_ge(ld.h, hd_ld[h])
                    if i >= 3:
                        te.wait_ge(sc.h, exp_of(i - 3))
                    te.matmul(PF[sb3][:, 0:512], ka_sb[:, hb * T + i * 128: hb * T + (i + 1) * 128],
                              qa_sb[:, hb * T + c * 512: hb * T + (c + 1) * 512],
                              start=True, stop=True).then_inc(mm.h, mm.step())
                def pv_mm(i):
                    if i >= 4 * c:
                        te.wait_ge(ve.h, mask_of(i))
                    else:
                        te.wait_ge(sc.h, exp_of(i))
                    if i == 0 and prev_ev is not None:
                        te.wait_ge(ve.h, max(prev_ev))
                    te.matmul(PF[2][0:65, 0:512], v1_tm[:, i * HEADS * 65 + h * 65: i * HEADS * 65 + (h + 1) * 65],
                              pT[:, (i % 3) * 512:(i % 3 + 1) * 512],
                              start=(i == 0), stop=(i == nt8 - 1)).then_inc(mm.h, mm.step())
                s_mm(0)
                for i in range(1, nt8):
                    s_mm(i)
                    pv_mm(i - 1)
                pv_mm(nt8 - 1)
            @block.scalar
            def _(scl, h=h, c=c, nt8=nt8):
                for i in range(nt8):
                    scl.wait_ge(mm.h, mm_s_of(i))
                    if i >= 3:
                        scl.wait_ge(mm.h, pv_of(i - 3))
                    scl.activation(pT[:, (i % 3) * 512:(i % 3 + 1) * 512], PF[[0, 1, 4][i % 3]][:, 0:512],
                                   AF.Exp).then_inc(sc.h, sc.step())
            assert sc.val == sc0 + nt8
            @block.vector
            def _(v, c=c, nt8=nt8):
                for i in range(4 * c, nt8):
                    v.wait_ge(sc.h, exp_of(i))
                    vv = i - 4 * c
                    v.tensor_tensor(pT[:, (i % 3) * 512:(i % 3 + 1) * 512], pT[:, (i % 3) * 512:(i % 3 + 1) * 512],
                                    mask_sb[:, vv * 512:(vv + 1) * 512], ALU.mult)
                    v.drain()
                    v.nop().then_inc(ve.h, ve.step())
            assert ve.val == ve0 + 4
            assert mm.val == mm0 + 2 * nt8
            mm_o = mm.val
            @block.vector
            def _(v, mm_o=mm_o):
                v.wait_ge(mm.h, mm_o)
                v.reciprocal(dinv[0:1, :], PF[2][64:65, 0:512])
                v.drain()
                v.nop().then_inc(ve.h, ve.step())
            rec_done = ve.val
            @block.tensor
            def _(te, rec_done=rec_done):
                te.wait_ge(ve.h, rec_done)
                te.matmul(PF[3][0:64, 0:512], onesrow[0:1, 0:64], dinv[0:1, :], start=True, stop=True).then_inc(mm.h, mm.step())
            mm_b = mm.val
            @block.vector
            def _(v, mm_o=mm_o):
                v.wait_ge(mm.h, mm_o)
                v.tensor_copy(tmpo[:], PF[2][0:64, 0:512])
                v.drain()
                v.nop().then_inc(ve.h, ve.step())
            cp_done = ve.val
            @block.vector
            def _(v, h=h, r0=r0, c=c, mm_b=mm_b, cp_done=cp_done):
                v.wait_ge(mm.h, mm_b)
                v.tensor_tensor(attn1[r0:r0 + 64, (h // 2) * T + c * 512:(h // 2) * T + (c + 1) * 512],
                                tmpo[:], PF[3][0:64, 0:512], ALU.mult).then_inc(ve.h, ve.step())
            prev_evac = (cp_done, rec_done)
        hd_mm_done[h] = mm.val
    p8.close()
    E["v1_es"].close()
    del ka_sb, qa_sb, pT, mask_sb, dinv, tmpo

    phase_barrier(block, sems)
    p9 = ExitStack()
    wo1_sb = p9.enter_context(nc.sbuf_tensor("wo1_sb", [128, NB * D], BF16))
    s1_ev = p9.enter_context(nc.sbuf_tensor("p9_s1", [128, NB * 512], F32))
    @block.gpsimd
    def _(g):
        g.wait_ge(wl.h, wl.val)
        for k in range(NB):
            g.dma_start(out=wo1_sb[:, k * D:(k + 1) * D], in_=wo1[k * 128:(k + 1) * 128, :]).then_inc(wl.h, wl.step(16))
    p9_w_ld = wl.val
    for c in range(NCH):
        mm0, ve0 = mm.val, ve.val
        mm_of = lambda mt: mm0 + mt + 1
        ev_of = lambda mt: ve0 + mt + 1
        @block.tensor
        def _(te, c=c, mm0=mm0, ve0=ve0):
            for mt in range(NB):
                a = mt % 2
                if c == 0 and mt == 0:
                    te.wait_ge(wl.h, p9_w_ld)
                if mt >= 2:
                    te.wait_ge(ve.h, ev_of(mt - 2))
                elif c > 0:
                    te.wait_ge(ve.h, ve0)
                for k in range(NB):
                    ins = te.matmul(PF[a][:, 0:512], wo1_sb[:, k * D + mt * 128: k * D + (mt + 1) * 128],
                                    attn1[:, k * T + c * 512:(k * T) + (c + 1) * 512],
                                    start=(k == 0), stop=(k == NB - 1))
                    if k == NB - 1:
                        ins.then_inc(mm.h, mm.step())
        @block.vector
        def _(v, c=c, stv=st.val):
            for mt in range(NB):
                a = mt % 2
                v.wait_ge(mm.h, mm_of(mt))
                if mt == 0:
                    v.wait_ge(st.h, stv)
                v.drain()
                v.tensor_copy(s1_ev[:, mt * 512:(mt + 1) * 512], PF[a][:, 0:512]).then_inc(ve.h, ve.step())
        @block.sync
        def _(s, c=c):
            s.wait_ge(st.h, st.val)
            for mt in range(NB):
                a = mt % 2
                s.wait_ge(ve.h, ev_of(mt))
                s.dma_start(out=s1fm[mt, :, c * 512:(c + 1) * 512], in_=s1_ev[:, mt * 512:(mt + 1) * 512]).then_inc(st.h, st.step(16))
        assert mm.val == mm0 + NB and ve.val == ve0 + NB
    p9.close()
    attn1_es.close()
    if upto == "s1fm":
        return True

    phase_barrier(block, sems)
    p10 = ExitStack()
    _a = p10.enter_context
    wfc_sb = _a(nc.sbuf_tensor("wfc_sb", [128, NB * FF1], BF16))
    xnf1 = _a(nc.sbuf_tensor("p10_xnf", [128, NB * 512], BF16))
    r_f = _a(nc.sbuf_tensor("p10_r", [128, 2 * 512], F32))
    h_b = _a(nc.sbuf_tensor("p10_h", [128, 2 * 512], BF16))
    @block.gpsimd
    def _(g):
        g.wait_ge(wl.h, wl.val)
        for k in range(NB):
            g.dma_start(out=wfc_sb[:, k * FF1:(k + 1) * FF1], in_=wfcT[k * 128:(k + 1) * 128, :]).then_inc(wl.h, wl.step(16))
    p10_w_ld = wl.val
    for c in range(NCH):
        @block.sync
        def _(s, c=c):
            s.wait_ge(ld.h, ld.val)
            s.wait_ge(mm.h, mm.val)
            for j in range(NB):
                s.dma_start(out=xnf1[:, j * 512:(j + 1) * 512],
                            in_=xn1_tm[c * 512:(c + 1) * 512, j * 128:(j + 1) * 128],
                            transpose=True).then_inc(ld.h, ld.step(16))
        ld_chunk = ld.val
        for mt in range(FF1 // 128):
            a = mt % 2
            g_idx = c * 32 + mt
            @block.tensor
            def _(te, c=c, mt=mt, a=a, g_idx=g_idx, ld_chunk=ld_chunk):
                if g_idx == 0:
                    te.wait_ge(wl.h, p10_w_ld)
                if mt == 0:
                    te.wait_ge(ld.h, ld_chunk)
                if g_idx >= 2:
                    te.wait_ge(sc.h, sc.val)
                for k in range(NB):
                    ins = te.matmul(PF[a][:, 0:512], wfc_sb[:, k * FF1 + mt * 128: k * FF1 + (mt + 1) * 128],
                                    xnf1[:, k * 512:(k + 1) * 512], start=(k == 0), stop=(k == NB - 1))
                    if k == NB - 1:
                        ins.then_inc(mm.h, mm.step())
            mm_pre = mm.val
            @block.scalar
            def _(scl, a=a, g_idx=g_idx, mm_pre=mm_pre):
                scl.wait_ge(mm.h, mm_pre)
                if g_idx >= 2:
                    scl.wait_ge(ve.h, ve.val)
                scl.activation(r_f[:, a * 512:(a + 1) * 512], PF[a][:, 0:512], AF.Relu).then_inc(sc.h, sc.step())
            relu_done = sc.val
            @block.vector
            def _(v, a=a, g_idx=g_idx, relu_done=relu_done):
                v.wait_ge(sc.h, relu_done)
                if g_idx >= 2:
                    v.wait_ge(gp.h, gp.val)
                v.tensor_tensor(h_b[:, a * 512:(a + 1) * 512], r_f[:, a * 512:(a + 1) * 512],
                                r_f[:, a * 512:(a + 1) * 512], ALU.mult).then_inc(ve.h, ve.step())
            sqr_done = ve.val
            @block.gpsimd
            def _(g, c=c, mt=mt, a=a, sqr_done=sqr_done):
                g.wait_ge(gp.h, gp.val)
                g.wait_ge(ve.h, sqr_done)
                g.dma_start(out=h1sp[mt, :, c * 512:(c + 1) * 512], in_=h_b[:, a * 512:(a + 1) * 512]).then_inc(gp.h, gp.step(16))
    p10.close()
    if upto == "h1sp":
        return True

    phase_barrier(block, sems)
    p11 = ExitStack()
    _a = p11.enter_context
    wpr_sb = _a(nc.sbuf_tensor("wpr_sb", [128, (FF1 // 128) * D], BF16))
    h1f = _a(nc.sbuf_tensor("p11_h", [128, (FF1 // 128) * 512], BF16))
    yb = _a(nc.sbuf_tensor("p11_y", [128, 2 * 512], F32))
    memb = _a(nc.sbuf_tensor("p11_mem", [128, 2 * 512], F32))
    sb_ = _a(nc.sbuf_tensor("p11_s", [128, 2 * 512], F32))
    accb = _a(nc.sbuf_tensor("p11_acc", [128, 2 * 512], F32))
    g1b = _a(nc.sbuf_tensor("p11_g1", [128, 2 * 512], F32))
    s1c = _a(nc.sbuf_tensor("p11_s1", [128, NB * 512], F32))
    sgb = _a(nc.sbuf_tensor("p11_sg", [128, 2 * 512], F32))
    x2t = _a(nc.sbuf_tensor("p11_x2", [128, 4 * D], F32))
    xrt = _a(nc.sbuf_tensor("p11_xr", [128, 4 * D], F32))
    @block.gpsimd
    def _(g):
        g.wait_ge(wl.h, wl.val)
        for k in range(FF1 // 128):
            g.dma_start(out=wpr_sb[:, k * D:(k + 1) * D], in_=wprT[k * 128:(k + 1) * 128, :]).then_inc(wl.h, wl.step(16))
    p11_w_ld = wl.val
    for c in range(NCH):
        @block.sync
        def _(s, c=c):
            s.wait_ge(ld.h, ld.val)
            s.wait_ge(mm.h, mm.val)
            s.wait_ge(ve.h, ve.val)
            for part in range(4):
                s.dma_start(out=h1f[:, part * 8 * 512:(part + 1) * 8 * 512].rearrange("p (k t) -> p k t", k=8),
                            in_=h1sp[part * 8:(part + 1) * 8, :, c * 512:(c + 1) * 512].rearrange("k p t -> p k t")).then_inc(ld.h, ld.step(16))
            s.dma_start(out=s1c[:].rearrange("p (k t) -> p k t", k=NB),
                        in_=s1fm[:, :, c * 512:(c + 1) * 512].rearrange("k p t -> p k t")).then_inc(ld.h, ld.step(16))
            for tt in range(4):
                s.dma_start(out=xrt[:, tt * D:(tt + 1) * D],
                            in_=x1_tm[(c * 4 + tt) * 128:(c * 4 + tt + 1) * 128, :]).then_inc(ld.h, ld.step(16))
        ld_chunk = ld.val
        for mt in range(NB):
            a = mt % 2
            g_idx = c * NB + mt
            @block.tensor
            def _(te, c=c, mt=mt, a=a, g_idx=g_idx, ld_chunk=ld_chunk):
                if g_idx == 0:
                    te.wait_ge(wl.h, p11_w_ld)
                if mt == 0:
                    te.wait_ge(ld.h, ld_chunk)
                if g_idx >= 2:
                    te.wait_ge(ve.h, ve.val)
                for k in range(FF1 // 128):
                    ins = te.matmul(PF[a][:, 0:512], wpr_sb[:, k * D + mt * 128: k * D + (mt + 1) * 128],
                                    h1f[:, k * 512:(k + 1) * 512], start=(k == 0), stop=(k == FF1 // 128 - 1))
                    if k == FF1 // 128 - 1:
                        ins.then_inc(mm.h, mm.step())
            mm_y = mm.val
            @block.vector
            def _(v, a=a, mm_y=mm_y):
                v.wait_ge(mm.h, mm_y)
                v.tensor_copy(yb[:, a * 512:(a + 1) * 512], PF[a][:, 0:512])
                v.drain()
                v.tensor_copy(memb[:, a * 512:(a + 1) * 512], yb[:, a * 512:(a + 1) * 512]).then_inc(ve.h, ve.step())
                for step in range(5):
                    v.drain()
                    v.tensor_scalar(sb_[:, a * 512:(a + 1) * 512], memb[:, a * 512:(a + 1) * 512],
                                    1.0, None, ALU.is_gt)
                    v.drain()
                    if step == 0:
                        v.tensor_copy(accb[:, a * 512:(a + 1) * 512], sb_[:, a * 512:(a + 1) * 512])
                    else:
                        v.tensor_tensor(accb[:, a * 512:(a + 1) * 512], accb[:, a * 512:(a + 1) * 512],
                                        sb_[:, a * 512:(a + 1) * 512], ALU.add)
                    if step < 4:
                        v.scalar_tensor_tensor(g1b[:, a * 512:(a + 1) * 512], sb_[:, a * 512:(a + 1) * 512], -1.0,
                                               yb[:, a * 512:(a + 1) * 512], ALU.mult, ALU.add)
                        v.drain()
                        v.scalar_tensor_tensor(memb[:, a * 512:(a + 1) * 512], memb[:, a * 512:(a + 1) * 512], 0.9,
                                               g1b[:, a * 512:(a + 1) * 512], ALU.mult, ALU.add)
                v.drain()
                v.nop().then_inc(ve.h, ve.step())
            snn_v = ve.val
            @block.scalar
            def _(scl, a=a, snn_v=snn_v):
                scl.wait_ge(ve.h, snn_v)
                scl.activation(g1b[:, a * 512:(a + 1) * 512], accb[:, a * 512:(a + 1) * 512], AF.Silu,
                               scale=0.2).then_inc(sc.h, sc.step())
            g1_done = sc.val
            @block.vector
            def _(v, mt=mt, a=a, g1_done=g1_done):
                v.wait_ge(sc.h, g1_done)
                v.tensor_tensor(sgb[:, a * 512:(a + 1) * 512], g1b[:, a * 512:(a + 1) * 512],
                                s1c[:, mt * 512:(mt + 1) * 512], ALU.mult).then_inc(ve.h, ve.step())
            sg_done = ve.val
            for tt in range(4):
                tb = 2 + tt % 2
                @block.tensor
                def _(te, mt=mt, a=a, tt=tt, tb=tb, sg_done=sg_done):
                    te.wait_ge(ve.h, sg_done if tt == 0 else ve.val)
                    te.transpose(PF[tb][:, 0:128], sgb[:, a * 512 + tt * 128: a * 512 + (tt + 1) * 128],
                                 identf[:]).then_inc(mm.h, mm.step())
                trd = mm.val
                @block.vector
                def _(v, mt=mt, tt=tt, tb=tb, trd=trd, stv=st.val):
                    if mt == 0 and tt == 0:
                        v.wait_ge(st.h, stv)
                    v.wait_ge(mm.h, trd)
                    v.tensor_copy(x2t[:, tt * D + mt * 128: tt * D + (mt + 1) * 128], PF[tb][:, 0:128]).then_inc(ve.h, ve.step())
        for tt in range(4):
            i_tile = c * 4 + tt
            @block.vector
            def _(v, tt=tt):
                v.wait_ge(ve.h, ve.val)
                v.tensor_tensor(x2t[:, tt * D:(tt + 1) * D], x2t[:, tt * D:(tt + 1) * D],
                                xrt[:, tt * D:(tt + 1) * D], ALU.add).then_inc(ve.h, ve.step())
            add_done = ve.val
            @block.sync
            def _(s, i_tile=i_tile, tt=tt, add_done=add_done):
                s.wait_ge(st.h, st.val)
                s.wait_ge(ve.h, add_done)
                s.dma_start(out=y_out[i_tile * 128:(i_tile + 1) * 128, :], in_=x2t[:, tt * D:(tt + 1) * D]).then_inc(st.h, st.step(16))
    p11.close()
    return False


def _emit_p7(E):
    (nc, block, sems, PF, PB, identf, identb, onesrow, onescol, cst,
     xn1_tm, kaug_sp, qaug_sp, v1_tm) = (
        E["nc"], E["block"], E["sems"], E["PF"], E["PB"], E["identf"], E["identb"],
        E["onesrow"], E["onescol"], E["cst"], E["xn1_tm"], E["kaug_sp"], E["qaug_sp"], E["v1_tm"])
    wkv, gkv, wkdec, wvdec, wq1, wqdec, wkrope, wqrope = (
        E["wkv"], E["gkv"], E["wkdec"], E["wvdec"], E["wq1"], E["wqdec"], E["wkrope"], E["wqrope"])
    cos4_d, sin4_d, rot_d = E["cos4_d"], E["sin4_d"], E["rot_d"]
    ld, wl, mm, ve, sc, gp, st = (sems[k] for k in ["ld", "wl", "mm", "ve", "sc", "gp", "st"])

    p7 = ExitStack()
    _a = p7.enter_context
    xnf = _a(nc.sbuf_tensor("p7_xnf", [128, NB * 512], BF16))
    wkv_sb = _a(nc.sbuf_tensor("wkv_sb", [128, NB * 256], BF16))
    wkdec_sb = _a(nc.sbuf_tensor("wkdec_sb", [128, 2 * D], BF16))
    wvdec_sb = _a(nc.sbuf_tensor("wvdec_sb", [128, 2 * D], BF16))
    wq1_sb = _a(nc.sbuf_tensor("wq1_sb", [128, NB * 512], BF16))
    wqdec_sb = _a(nc.sbuf_tensor("wqdec_sb", [128, 4 * D], BF16))
    wkrope_sb = _a(nc.sbuf_tensor("wkrope_sb", [128, NB * 512], BF16))
    wqrope_sb = _a(nc.sbuf_tensor("wqrope_sb", [128, 4 * 512], BF16))
    gkv_sb = _a(nc.sbuf_tensor("gkv_sb", [128, 2], F32))
    cos_sb = _a(nc.sbuf_tensor("cos_sb", [128, T], BF16))
    sin_sb = _a(nc.sbuf_tensor("sin_sb", [128, T], BF16))
    rot_sb = _a(nc.sbuf_tensor("rot_sb", [128, 128], BF16))
    kvc_f = _a(nc.sbuf_tensor("p7_kvc", [128, 2 * 512], F32))
    sqbuf = _a(nc.sbuf_tensor("p7_sq", [128, 2 * 512], F32))
    kvcn_b = _a(nc.sbuf_tensor("p7_kvcn", [128, 2 * 512], BF16))
    qc_b = _a(nc.sbuf_tensor("p7_qc", [128, 4 * 512], BF16))
    kr_b = _a(nc.sbuf_tensor("p7_kr", [128, 512], BF16))
    t1_f = _a(nc.sbuf_tensor("p7_t1", [128, 512], F32))
    t2_f = _a(nc.sbuf_tensor("p7_t2", [128, 512], F32))
    kstage = _a(nc.sbuf_tensor("p7_kst", [128, HEADS * 512], BF16))
    qstage = _a(nc.sbuf_tensor("p7_qst", [128, HEADS * 512], BF16))
    rows = _a(nc.sbuf_tensor("p7_rows", [1, 2 * 512], F32))
    maxk = _a(nc.sbuf_tensor("p7_maxk", [1, 64], F32))

    @block.gpsimd
    def _(g):
        g.wait_ge(wl.h, wl.val)
        for k in range(NB):
            g.dma_start(out=wkv_sb[:, k * 256:(k + 1) * 256], in_=wkv[k * 128:(k + 1) * 128, :]).then_inc(wl.h, wl.step(16))
            g.dma_start(out=wq1_sb[:, k * 512:(k + 1) * 512], in_=wq1[k * 128:(k + 1) * 128, :]).then_inc(wl.h, wl.step(16))
            g.dma_start(out=wkrope_sb[:, k * 512:(k + 1) * 512], in_=wkrope[k * 128:(k + 1) * 128, :]).then_inc(wl.h, wl.step(16))
        for k in range(2):
            g.dma_start(out=wkdec_sb[:, k * D:(k + 1) * D], in_=wkdec[k * 128:(k + 1) * 128, :]).then_inc(wl.h, wl.step(16))
            g.dma_start(out=wvdec_sb[:, k * D:(k + 1) * D], in_=wvdec[k * 128:(k + 1) * 128, :]).then_inc(wl.h, wl.step(16))
        for k in range(4):
            g.dma_start(out=wqdec_sb[:, k * D:(k + 1) * D], in_=wqdec[k * 128:(k + 1) * 128, :]).then_inc(wl.h, wl.step(16))
            g.dma_start(out=wqrope_sb[:, k * 512:(k + 1) * 512], in_=wqrope[k * 128:(k + 1) * 128, :]).then_inc(wl.h, wl.step(16))
    p7_w_ld = wl.val
    @block.sync
    def _(s):
        s.wait_ge(ld.h, ld.val)
        s.dma_start(out=gkv_sb[:], in_=gkv).then_inc(ld.h, ld.step(16))
        s.dma_start(out=cos_sb[:], in_=cos4_d).then_inc(ld.h, ld.step(16))
        s.dma_start(out=sin_sb[:], in_=sin4_d).then_inc(ld.h, ld.step(16))
        s.dma_start(out=rot_sb[:], in_=rot_d).then_inc(ld.h, ld.step(16))
    consts_ld = ld.val

    @block.vector
    def _(v):
        v.memset(v1_tm[:], 1.0).then_inc(ve.h, ve.step())
        v.memset(maxk[:], -1e30).then_inc(ve.h, ve.step())

    for c in range(NCH):
        @block.sync
        def _(s, c=c):
            s.wait_ge(ld.h, ld.val)
            s.wait_ge(mm.h, mm.val)
            for j in range(NB):
                s.dma_start(out=xnf[:, j * 512:(j + 1) * 512],
                            in_=xn1_tm[c * 512:(c + 1) * 512, j * 128:(j + 1) * 128],
                            transpose=True).then_inc(ld.h, ld.step(16))
        ld_chunk = ld.val

        for m in range(2):
            @block.tensor
            def _(te, c=c, m=m, ld_chunk=ld_chunk):
                if c == 0 and m == 0:
                    te.wait_ge(wl.h, p7_w_ld)
                    te.wait_ge(ld.h, consts_ld)
                if m == 0:
                    te.wait_ge(ld.h, ld_chunk)
                te.wait_ge(ve.h, ve.val)
                for k in range(NB):
                    ins = te.matmul(PF[m][:, 0:512], wkv_sb[:, k * 256 + m * 128: k * 256 + (m + 1) * 128],
                                    xnf[:, k * 512:(k + 1) * 512], start=(k == 0), stop=(k == NB - 1))
                    if k == NB - 1:
                        ins.then_inc(mm.h, mm.step())
            mm_kv = mm.val
            @block.vector
            def _(v, m=m, mm_kv=mm_kv):
                v.wait_ge(mm.h, mm_kv)
                v.tensor_copy(kvc_f[:, m * 512:(m + 1) * 512], PF[m][:, 0:512]).then_inc(ve.h, ve.step())
        kv_ev = ve.val
        @block.vector
        def _(v, kv_ev=kv_ev):
            v.drain()
            v.tensor_tensor(sqbuf[:], kvc_f[:], kvc_f[:], ALU.mult).then_inc(ve.h, ve.step())
        sq_done = ve.val
        @block.tensor
        def _(te, sq_done=sq_done):
            te.wait_ge(ve.h, sq_done)
            te.matmul(PF[2][0:1, 0:512], onescol[:], sqbuf[:, 0:512], start=True, stop=False)
            te.matmul(PF[2][0:1, 0:512], onescol[:], sqbuf[:, 512:1024], start=False, stop=True).then_inc(mm.h, mm.step())
        ms_done = mm.val
        @block.scalar
        def _(scl, ms_done=ms_done):
            scl.wait_ge(mm.h, ms_done)
            scl.activation(rows[0:1, 0:512], PF[2][0:1, 0:512], AF.Ln, bias=cst[0:1, 1:2], scale=1.0 / 256)
            scl.drain()
            scl.activation(rows[0:1, 512:1024], rows[0:1, 0:512], AF.Exp, scale=-0.5).then_inc(sc.h, sc.step())
        msr_done = sc.val
        @block.vector
        def _(v, msr_done=msr_done):
            v.wait_ge(sc.h, msr_done)
            v.nop().then_inc(ve.h, ve.step())
        rstd_done = ve.val
        @block.tensor
        def _(te, rstd_done=rstd_done):
            te.wait_ge(ve.h, rstd_done)
            te.matmul(PF[3][:, 0:512], onesrow[:], rows[0:1, 512:1024], start=True, stop=True).then_inc(mm.h, mm.step())
        bc_done = mm.val
        for m in range(2):
            @block.vector
            def _(v, m=m, bc_done=bc_done):
                v.wait_ge(mm.h, bc_done)
                v.scalar_tensor_tensor(kvcn_b[:, m * 512:(m + 1) * 512], kvc_f[:, m * 512:(m + 1) * 512],
                                       gkv_sb[:, m:m + 1], PF[3][:, 0:512], ALU.mult, ALU.mult).then_inc(ve.h, ve.step())
        kvcn_done = ve.val

        for mt in range(NB):
            a = mt % 2
            @block.tensor
            def _(te, mt=mt, a=a, kvcn_done=kvcn_done):
                te.wait_ge(ve.h, kvcn_done if mt < 2 else ve.val)
                for k in range(2):
                    ins = te.matmul(PF[a][:, 0:512], wkdec_sb[:, k * D + mt * 128: k * D + (mt + 1) * 128],
                                    kvcn_b[:, k * 512:(k + 1) * 512], start=(k == 0), stop=(k == 1))
                    if k == 1:
                        ins.then_inc(mm.h, mm.step())
            mm_kd = mm.val
            @block.vector
            def _(v, mt=mt, a=a, mm_kd=mm_kd, stv=st.val):
                if mt == 0:
                    v.wait_ge(st.h, stv)
                v.wait_ge(mm.h, mm_kd)
                v.tensor_copy(kstage[0:64, (2 * mt) * 512:(2 * mt + 1) * 512], PF[a][0:64, 0:512])
                v.tensor_copy(kstage[0:64, (2 * mt + 1) * 512:(2 * mt + 2) * 512], PF[a][64:128, 0:512]).then_inc(ve.h, ve.step())
        for tt in range(4):
            for nh in range(2):
                a = (tt * 2 + nh) % 2
                @block.tensor
                def _(te, c=c, tt=tt, nh=nh, a=a):
                    te.wait_ge(ve.h, ve.val)
                    for k in range(2):
                        ins = te.matmul(PF[a][:, 0:512], kvcn_b[:, k * 512 + tt * 128: k * 512 + (tt + 1) * 128],
                                        wvdec_sb[:, k * D + nh * 512: k * D + (nh + 1) * 512],
                                        start=(k == 0), stop=(k == 1))
                        if k == 1:
                            ins.then_inc(mm.h, mm.step())
                mm_v = mm.val
                @block.vector
                def _(v, c=c, tt=tt, nh=nh, a=a, mm_v=mm_v):
                    v.wait_ge(mm.h, mm_v)
                    base = (c * 4 + tt) * HEADS * 65 + nh * 8 * 65
                    dst = v1_tm[:, base: base + 8 * 65].rearrange("p (h e) -> p h e", h=8)[:, :, 0:64]
                    v.tensor_copy(dst, PF[a][:, 0:512].rearrange("p (h e) -> p h e", h=8)).then_inc(ve.h, ve.step())

        for m in range(4):
            a = m % 2
            @block.tensor
            def _(te, m=m, a=a):
                te.wait_ge(ve.h, ve.val)
                for k in range(NB):
                    ins = te.matmul(PF[a][:, 0:512], wq1_sb[:, k * 512 + m * 128: k * 512 + (m + 1) * 128],
                                    xnf[:, k * 512:(k + 1) * 512], start=(k == 0), stop=(k == NB - 1))
                    if k == NB - 1:
                        ins.then_inc(mm.h, mm.step())
            mm_qc = mm.val
            @block.vector
            def _(v, m=m, a=a, mm_qc=mm_qc):
                v.wait_ge(mm.h, mm_qc)
                v.tensor_scalar(qc_b[:, m * 512:(m + 1) * 512], PF[a][:, 0:512], QSCALE, None, ALU.mult).then_inc(ve.h, ve.step())
        qc_done = ve.val
        for mt in range(NB):
            a = mt % 2
            @block.tensor
            def _(te, mt=mt, a=a, qc_done=qc_done):
                te.wait_ge(ve.h, qc_done if mt < 2 else ve.val)
                for k in range(4):
                    ins = te.matmul(PF[a][:, 0:512], wqdec_sb[:, k * D + mt * 128: k * D + (mt + 1) * 128],
                                    qc_b[:, k * 512:(k + 1) * 512], start=(k == 0), stop=(k == 3))
                    if k == 3:
                        ins.then_inc(mm.h, mm.step())
            mm_qd = mm.val
            @block.vector
            def _(v, mt=mt, a=a, mm_qd=mm_qd):
                v.wait_ge(mm.h, mm_qd)
                v.tensor_copy(qstage[0:64, (2 * mt) * 512:(2 * mt + 1) * 512], PF[a][0:64, 0:512])
                v.tensor_copy(qstage[0:64, (2 * mt + 1) * 512:(2 * mt + 2) * 512], PF[a][64:128, 0:512]).then_inc(ve.h, ve.step())

        for which, wsb, nk, stage in (("k", wkrope_sb, NB, kstage), ("q", wqrope_sb, 4, qstage)):
            src_b = xnf if which == "k" else qc_b
            for m in range(4):
                a = m % 2
                @block.tensor
                def _(te, which=which, wsb=wsb, nk=nk, src_b=src_b, m=m, a=a):
                    te.wait_ge(ve.h, ve.val)
                    for k in range(nk):
                        ins = te.matmul(PF[a][:, 0:512], wsb[:, k * 512 + m * 128: k * 512 + (m + 1) * 128],
                                        src_b[:, k * 512:(k + 1) * 512], start=(k == 0), stop=(k == nk - 1))
                        if k == nk - 1:
                            ins.then_inc(mm.h, mm.step())
                mm_r = mm.val
                @block.vector
                def _(v, a=a, mm_r=mm_r):
                    v.wait_ge(mm.h, mm_r)
                    v.tensor_copy(kr_b[:], PF[a][:, 0:512]).then_inc(ve.h, ve.step())
                kr_done = ve.val
                @block.tensor
                def _(te, a=a, kr_done=kr_done):
                    te.wait_ge(ve.h, kr_done)
                    te.matmul(PF[4][:, 0:512], rot_sb[:], kr_b[:], start=True, stop=True).then_inc(mm.h, mm.step())
                rot_done = mm.val
                @block.vector
                def _(v, c=c, m=m, stage=stage, rot_done=rot_done):
                    v.tensor_tensor(t1_f[:], kr_b[:], cos_sb[:, c * 512:(c + 1) * 512], ALU.mult)
                    v.wait_ge(mm.h, rot_done)
                    v.tensor_tensor(t2_f[:], PF[4][:, 0:512], sin_sb[:, c * 512:(c + 1) * 512], ALU.mult)
                    v.drain()
                    for hh in range(4):
                        h = 4 * m + hh
                        v.tensor_tensor(stage[64:96, h * 512:(h + 1) * 512],
                                        t1_f[32 * hh:32 * (hh + 1), :], t2_f[32 * hh:32 * (hh + 1), :], ALU.add)
                    v.drain()
                    v.nop().then_inc(ve.h, ve.step())

        @block.vector
        def _(v):
            v.memset(kstage[96:97, :], 1.0).then_inc(ve.h, ve.step())
        for which, stage in (("k", kstage), ("q", qstage)):
            mm0, sc0, ve0 = mm.val, sc.val, ve.val
            nrm_of = lambda h: mm0 + h + 1
            nrow_of = lambda h: sc0 + h + 1
            order7 = []
            for _h in range(HEADS):
                order7.append(("sq", _h))
                if _h >= 4:
                    order7.append(("post", _h - 4))
            for _h in range(HEADS - 4, HEADS):
                order7.append(("post", _h))
            sq_pos = {}
            post_pos = {}
            for _j, (_kind, _h) in enumerate(order7):
                (sq_pos if _kind == "sq" else post_pos)[_h] = ve0 + _j + 1
            sq_of = lambda h: sq_pos[h]
            post_of = lambda h: post_pos[h]
            @block.vector
            def _(v, which=which, stage=stage, ve0=ve0, order7=tuple(order7)):
                for kind, h in order7:
                    if kind == "sq":
                        if h >= 2:
                            v.wait_ge(mm.h, nrm_of(h - 2))
                        v.drain()
                        v.tensor_tensor(sqbuf[0:96, (h % 2) * 512:(h % 2) * 512 + 512], stage[0:96, h * 512:(h + 1) * 512],
                                        stage[0:96, h * 512:(h + 1) * 512], ALU.mult).then_inc(ve.h, ve.step())
                    else:
                        v.wait_ge(sc.h, nrow_of(h))
                        if which == "k":
                            v.reduce_max(maxk[0:1, 32 + h:33 + h], rows[0:1, (h % 2) * 512:(h % 2) * 512 + 512], axis=AX.X)
                            v.drain()
                            v.tensor_tensor(maxk[0:1, h:h + 1], maxk[0:1, h:h + 1], maxk[0:1, 32 + h:33 + h], ALU.max)
                            v.drain()
                            v.tensor_scalar(maxk[0:1, 48 + h:49 + h], maxk[0:1, h:h + 1], -1.0, None, ALU.mult)
                            v.drain()
                            v.nop().then_inc(ve.h, ve.step())
                        else:
                            v.tensor_scalar(qstage[96:97, h * 512:(h + 1) * 512], rows[0:1, (h % 2) * 512:(h % 2) * 512 + 512],
                                            maxk[0:1, 48 + h:49 + h], CQ_MARGIN, ALU.mult, ALU.add).then_inc(ve.h, ve.step())
            @block.tensor
            def _(te, which=which, ve0=ve0, sc0=sc0):
                for h in range(HEADS):
                    te.wait_ge(ve.h, sq_of(h))
                    if h >= 2:
                        te.wait_ge(sc.h, nrow_of(h - 2))
                    else:
                        te.wait_ge(sc.h, sc0)
                    te.matmul(PF[2 + h % 2][0:1, 0:512], onescol[0:96, :],
                              sqbuf[0:96, (h % 2) * 512:(h % 2) * 512 + 512], start=True, stop=True).then_inc(mm.h, mm.step())
            @block.scalar
            def _(scl, which=which, ve0=ve0):
                for h in range(HEADS):
                    scl.wait_ge(mm.h, nrm_of(h))
                    if h >= 2:
                        scl.wait_ge(ve.h, post_of(h - 2))
                    scl.activation(rows[0:1, (h % 2) * 512:(h % 2) * 512 + 512], PF[2 + h % 2][0:1, 0:512], AF.Ln)
                    scl.drain()
                    scl.activation(rows[0:1, (h % 2) * 512:(h % 2) * 512 + 512],
                                   rows[0:1, (h % 2) * 512:(h % 2) * 512 + 512], AF.Exp, scale=0.5).then_inc(sc.h, sc.step())
            assert ve.val == ve0 + 32 and mm.val == mm0 + 16 and sc.val == sc0 + 16
        stage_done = ve.val
        @block.sync
        def _(s, c=c, stage_done=stage_done):
            s.wait_ge(st.h, st.val)
            s.wait_ge(ve.h, stage_done)
            for h in range(HEADS):
                s.dma_start(out=kaug_sp[h, :, c * 512:(c + 1) * 512], in_=kstage[0:QK + 1, h * 512:(h + 1) * 512]).then_inc(st.h, st.step(16))
                s.dma_start(out=qaug_sp[h, :, c * 512:(c + 1) * 512], in_=qstage[0:QK + 1, h * 512:(h + 1) * 512]).then_inc(st.h, st.step(16))
    p7.close()


def _emit_p6(E):
    (nc, block, sems, PF, PB, identf, identb, onesrow, dcol, cst,
     projTs, ctx_sb, stt0, x_in, wo0, bo0, qfm_sp, g0fm, x1_tm, xn1_tm) = (
        E["nc"], E["block"], E["sems"], E["PF"], E["PB"], E["identf"], E["identb"],
        E["onesrow"], E["dcol"], E["cst"], E["projTs"], E["ctx_sb"], E["stt0"],
        E["x_in"], E["wo0"], E["bo0"], E["qfm_sp"], E["g0fm"], E["x1_tm"], E["xn1_tm"])
    ld, wl, mm, ve, sc, gp, st = (sems[k] for k in ["ld", "wl", "mm", "ve", "sc", "gp", "st"])
    phase_barrier(block, sems)
    p6es = ExitStack()
    _a = p6es.enter_context
    p6_qf = _a(nc.sbuf_tensor("p6_qf", [128, 2 * NB * 512], BF16))
    p6_q2 = _a(nc.sbuf_tensor("p6_q2", [128, 512], F32))
    p6_qpf = _a(nc.sbuf_tensor("p6_qpf", [128, 2 * PM], F32))
    p6_qpb = _a(nc.sbuf_tensor("p6_qpb", [128, 2 * PM], BF16))
    p6_qTs = _a(nc.sbuf_tensor("p6_qTs", [128, 2 * HEADS * 512], BF16))
    p6_dinv = _a(nc.sbuf_tensor("p6_dinv", [1, 1024], F32))
    p6_tmpo = _a(nc.sbuf_tensor("p6_tmpo", [64, 1024], F32))
    p6_attn = _a(nc.sbuf_tensor("p6_attn", [128, NB * 512], BF16))
    wo_sb = _a(nc.sbuf_tensor("wo_sb", [128, NB * D], BF16))
    bo_sb = _a(nc.sbuf_tensor("bo_sb", [128, 8], F32))
    p6_g0 = _a(nc.sbuf_tensor("p6_g0", [128, NB * 512], F32))
    p6_s0 = _a(nc.sbuf_tensor("p6_s0", [128, 2 * 512], F32))
    p6_sg = _a(nc.sbuf_tensor("p6_sg", [128, 2 * 512], F32))
    p6_x1 = _a(nc.sbuf_tensor("p6_x1", [128, 4 * D], F32))
    p6_xr = _a(nc.sbuf_tensor("p6_xr", [128, 4 * D], F32))
    p6_xc = _a(nc.sbuf_tensor("p6_xc", [128, D], F32))
    p6_sq = _a(nc.sbuf_tensor("p6_sq", [128, D], F32))
    p6_xn = _a(nc.sbuf_tensor("p6_xn", [128, 2 * D], BF16))
    p6_ln = _a(nc.sbuf_tensor("p6_ln", [128, 8], F32))
    if True:
        @block.gpsimd
        def _(g):
            for k in range(NB):
                g.dma_start(out=wo_sb[:, k * D:(k + 1) * D], in_=wo0[k * 128:(k + 1) * 128, :]).then_inc(wl.h, wl.step(16))
        p6_w_ld = wl.val
        @block.sync
        def _(s):
            s.dma_start(out=bo_sb[:], in_=bo0).then_inc(ld.h, ld.step(16))

        for c in range(NCH):
            cb = c % 2
            chunk_start_mm = mm.val
            @block.sync
            def _(s, c=c, cb=cb):
                s.wait_ge(ld.h, ld.val)
                if c > 0:
                    s.wait_ge(ve.h, ve.val)
                s.dma_start(out=p6_qf[:, cb * NB * 512:(cb + 1) * NB * 512].rearrange("p (k t) -> p k t", k=NB),
                            in_=qfm_sp[:, :, c * 512:(c + 1) * 512].rearrange("k p t -> p k t")).then_inc(ld.h, ld.step(16))
                s.dma_start(out=p6_g0[:].rearrange("p (k t) -> p k t", k=NB),
                            in_=g0fm[:, :, c * 512:(c + 1) * 512].rearrange("k p t -> p k t")).then_inc(ld.h, ld.step(16))
                for tt in range(4):
                    s.dma_start(out=p6_xr[:, tt * D:(tt + 1) * D],
                                in_=x_in[(c * 4 + tt) * 128:(c * 4 + tt + 1) * 128, :]).then_inc(ld.h, ld.step(16))
            ld_chunk = ld.val

            for h in range(HEADS):
                r0 = (h % 2) * 64
                blk = h // 2
                dbank = 4 + h % 2
                mm0, sc0, ve0 = mm.val, sc.val, ve.val
                sq_of = ve0 + 1
                bias_of = lambda tt: ve0 + 1 + tt + 1
                cvt_of = lambda tt: ve0 + 5 + tt + 1
                copy_of = lambda tt: ve0 + 9 + tt + 1
                diag_of = mm0 + 4
                dd_of = lambda tt: mm0 + 4 + tt + 1
                tr_of = lambda tt: mm0 + 8 + tt + 1
                exp_of = lambda tt: sc0 + tt + 1
                @block.tensor
                def _(te, c=c, h=h, r0=r0, dbank=dbank, ve0=ve0, sq_of=sq_of, sc0=sc0):
                    te.wait_ge(ve.h, sq_of)
                    te.wait_ge(sc.h, sc0)
                    for tt in range(4):
                        ins = te.matmul(PF[dbank][:, tt:tt + 1], p6_q2[r0:r0 + 64, tt * 128:(tt + 1) * 128],
                                        dcol[r0:r0 + 64, :], start=True, stop=True)
                        if tt == 3:
                            ins.then_inc(mm.h, mm.step(4))
                    for tt in range(4):
                        a = tt % 2
                        if tt >= 2:
                            te.wait_ge(sc.h, exp_of(tt - 2))
                        te.matmul(PF[a][:, 0:PM], p6_qf[r0:r0 + 64, (cb * NB + blk) * 512 + tt * 128:(cb * NB + blk) * 512 + (tt + 1) * 128],
                                  projTs[r0:r0 + 64, :], start=True, stop=True).then_inc(mm.h, mm.step())
                    for tt in range(4):
                        a = tt % 2
                        te.wait_ge(ve.h, cvt_of(tt))
                        if tt >= 2:
                            te.wait_ge(ve.h, copy_of(tt - 2))
                        te.transpose(PB[a][:, 0:128], p6_qpb[:, a * PM: a * PM + 128], identb[:])
                        te.transpose(PB[a][:, 128:256], p6_qpb[:, a * PM + 128: (a + 1) * PM], identb[:]).then_inc(mm.h, mm.step())
                @block.vector
                def _(v, c=c, cb=cb, h=h, r0=r0, blk=blk, dbank=dbank, ld_chunk=ld_chunk):
                    if h == 0:
                        v.wait_ge(ld.h, ld_chunk)
                    else:
                        v.wait_ge(mm.h, mm0)
                    v.drain()
                    v.tensor_tensor(p6_q2[r0:r0 + 64, :], p6_qf[r0:r0 + 64, (cb * NB + blk) * 512:(cb * NB + blk + 1) * 512],
                                    p6_qf[r0:r0 + 64, (cb * NB + blk) * 512:(cb * NB + blk + 1) * 512], ALU.mult).then_inc(ve.h, ve.step())
                    for tt in range(4):
                        a = tt % 2
                        v.wait_ge(mm.h, max(dd_of(tt), diag_of))
                        if tt >= 2:
                            v.wait_ge(sc.h, exp_of(tt - 2))
                        v.reduce_max(stt0[:, 6 + a:7 + a], PF[a][:, 0:PM], axis=AX.X, negate=True)
                        v.tensor_scalar(stt0[:, 4 + a:5 + a], PF[dbank][:, tt:tt + 1], -1.0, LNRATIO, ALU.mult, ALU.add).then_inc(ve.h, ve.step())
                    for tt in range(4):
                        a = tt % 2
                        v.wait_ge(sc.h, exp_of(tt))
                        if tt >= 2:
                            v.wait_ge(mm.h, tr_of(tt - 2))
                        v.tensor_scalar(p6_qpb[:, a * PM:(a + 1) * PM], p6_qpf[:, a * PM:(a + 1) * PM],
                                        stt0[:, 8 + a:9 + a], RATIO * KEPS, ALU.mult, ALU.add).then_inc(ve.h, ve.step())
                    for tt in range(4):
                        a = tt % 2
                        v.wait_ge(mm.h, tr_of(tt))
                        v.tensor_copy(p6_qTs[:, h * 512 + tt * 128: h * 512 + (tt + 1) * 128], PB[a][:, 0:128])
                        v.drain()
                        v.tensor_copy(p6_qTs[:, (HEADS + h) * 512 + tt * 128: (HEADS + h) * 512 + (tt + 1) * 128],
                                      PB[a][:, 128:256]).then_inc(ve.h, ve.step())
                @block.scalar
                def _(scl, h=h):
                    for tt in range(4):
                        a = tt % 2
                        scl.wait_ge(ve.h, bias_of(tt))
                        if tt >= 2:
                            scl.wait_ge(ve.h, cvt_of(tt - 2))
                        scl.activation(stt0[:, 8 + a:9 + a], stt0[:, 6 + a:7 + a], AF.Exp)
                        scl.activation(p6_qpf[:, a * PM:(a + 1) * PM], PF[a][:, 0:PM], AF.Exp,
                                       bias=stt0[:, 4 + a:5 + a]).then_inc(sc.h, sc.step())
                assert mm.val == mm0 + 12 and ve.val == ve0 + 13 and sc.val == sc0 + 4

            stageA_ve = ve.val
            mm0b, ve0b = mm.val, ve.val
            orderB = []
            for _h in range(HEADS):
                orderB.append(("out", _h))
                if _h >= 1:
                    orderB.append(("bc", _h - 1))
            orderB.append(("bc", HEADS - 1))
            out_pos, bc_pos = {}, {}
            for _j, (_k, _h) in enumerate(orderB):
                (out_pos if _k == "out" else bc_pos)[_h] = mm0b + _j + 1
            rec_of = lambda h: ve0b + 3 * h + 1
            cp_of = lambda h: ve0b + 3 * h + 2
            ttb_of = lambda h: ve0b + 3 * h + 3
            @block.tensor
            def _(te, c=c, ve0b=ve0b, orderB=tuple(orderB)):
                for kind, h in orderB:
                    if kind == "out":
                        ob = [2, 5][h % 2]
                        if h == 0:
                            te.wait_ge(ve.h, ve0b)
                        if h >= 2:
                            te.wait_ge(ve.h, cp_of(h - 2))
                        te.matmul(PF[ob][0:65, 0:512], ctx_sb[:, h * 130: h * 130 + 65],
                                  p6_qTs[:, h * 512:(h + 1) * 512], start=True, stop=False)
                        te.matmul(PF[ob][0:65, 0:512], ctx_sb[:, h * 130 + 65: h * 130 + 130],
                                  p6_qTs[:, (HEADS + h) * 512:(HEADS + h + 1) * 512], start=False, stop=True).then_inc(mm.h, mm.step())
                    else:
                        bb = [0, 1][h % 2]
                        te.wait_ge(ve.h, rec_of(h))
                        if h >= 2:
                            te.wait_ge(ve.h, ttb_of(h - 2))
                        te.matmul(PF[bb][0:64, 0:512], onesrow[0:1, 0:64], p6_dinv[0:1, (h % 2) * 512:(h % 2 + 1) * 512],
                                  start=True, stop=True).then_inc(mm.h, mm.step())
            @block.vector
            def _(v, c=c):
                for h in range(HEADS):
                    ob = [2, 5][h % 2]
                    bb = [0, 1][h % 2]
                    r0h = (h % 2) * 64
                    v.wait_ge(mm.h, out_pos[h])
                    if h >= 2:
                        v.wait_ge(mm.h, bc_pos[h - 2])
                    v.reciprocal(p6_dinv[0:1, (h % 2) * 512:(h % 2 + 1) * 512], PF[ob][64:65, 0:512])
                    v.drain()
                    v.nop().then_inc(ve.h, ve.step())
                    v.tensor_copy(p6_tmpo[:, (h % 2) * 512:(h % 2 + 1) * 512], PF[ob][0:64, 0:512])
                    v.drain()
                    v.nop().then_inc(ve.h, ve.step())
                    v.wait_ge(mm.h, bc_pos[h])
                    v.tensor_tensor(p6_attn[r0h:r0h + 64, (h // 2) * 512:(h // 2 + 1) * 512],
                                    p6_tmpo[:, (h % 2) * 512:(h % 2 + 1) * 512], PF[bb][0:64, 0:512],
                                    ALU.mult).then_inc(ve.h, ve.step())
            assert mm.val == mm0b + 2 * HEADS and ve.val == ve0b + 3 * HEADS

            stageB_ve = ve.val
            for mt in range(NB):
                a = mt % 2
                @block.tensor
                def _(te, c=c, mt=mt, a=a, stageB_ve=stageB_ve):
                    if mt == 0:
                        te.wait_ge(ve.h, stageB_ve)
                        te.wait_ge(wl.h, p6_w_ld)
                    if mt >= 2:
                        te.wait_ge(ve.h, ve.val)
                    for k in range(NB):
                        ins = te.matmul(PF[a][:, 0:512], wo_sb[:, k * D + mt * 128: k * D + (mt + 1) * 128],
                                        p6_attn[:, k * 512:(k + 1) * 512],
                                        start=(k == 0), stop=(k == NB - 1))
                        if k == NB - 1:
                            ins.then_inc(mm.h, mm.step())
                mm_s0 = mm.val
                @block.vector
                def _(v, c=c, mt=mt, a=a, mm_s0=mm_s0):
                    v.wait_ge(mm.h, mm_s0)
                    v.tensor_scalar(p6_s0[:, a * 512:(a + 1) * 512], PF[a][:, 0:512], bo_sb[:, mt:mt + 1], None, ALU.add)
                    v.drain()
                    if mt >= 2:
                        v.wait_ge(mm.h, mm.val)
                    v.tensor_tensor(p6_sg[:, a * 512:(a + 1) * 512], p6_s0[:, a * 512:(a + 1) * 512],
                                    p6_g0[:, mt * 512:(mt + 1) * 512], ALU.mult).then_inc(ve.h, ve.step())
                sg_done = ve.val
                for tt in range(4):
                    tb = 2 + tt % 2
                    @block.tensor
                    def _(te, mt=mt, a=a, tt=tt, tb=tb, sg_done=sg_done):
                        te.wait_ge(ve.h, sg_done if tt == 0 else ve.val)
                        te.transpose(PF[tb][:, 0:128], p6_sg[:, a * 512 + tt * 128: a * 512 + (tt + 1) * 128],
                                     identf[:]).then_inc(mm.h, mm.step())
                    trd = mm.val
                    @block.vector
                    def _(v, mt=mt, tt=tt, tb=tb, trd=trd, stv=st.val):
                        if mt == 0 and tt == 0:
                            v.wait_ge(st.h, stv)
                        v.wait_ge(mm.h, trd)
                        v.tensor_copy(p6_x1[:, tt * D + mt * 128: tt * D + (mt + 1) * 128], PF[tb][:, 0:128]).then_inc(ve.h, ve.step())

            for tt in range(4):
                i_tile = c * 4 + tt
                @block.vector
                def _(v, c=c, tt=tt):
                    v.wait_ge(ve.h, ve.val)
                    v.wait_ge(st.h, st.val)
                    v.tensor_tensor(p6_x1[:, tt * D:(tt + 1) * D], p6_x1[:, tt * D:(tt + 1) * D],
                                    p6_xr[:, tt * D:(tt + 1) * D], ALU.add).then_inc(ve.h, ve.step())
                    v.drain()
                    v.reduce_sum(p6_ln[:, 0:1], p6_x1[:, tt * D:(tt + 1) * D], axis=AX.X)
                    v.drain()
                    v.tensor_scalar(p6_ln[:, 1:2], p6_ln[:, 0:1], 1.0 / D, None, ALU.mult)
                    v.drain()
                    v.tensor_scalar(p6_xc[:], p6_x1[:, tt * D:(tt + 1) * D], p6_ln[:, 1:2], None, ALU.subtract).then_inc(ve.h, ve.step())
                xc_done = ve.val
                @block.vector
                def _(v, xc_done=xc_done):
                    v.drain()
                    v.tensor_tensor(p6_sq[:], p6_xc[:], p6_xc[:], ALU.mult)
                    v.drain()
                    v.reduce_sum(p6_ln[:, 2:3], p6_sq[:], axis=AX.X)
                    v.drain()
                    v.tensor_scalar(p6_ln[:, 3:4], p6_ln[:, 2:3], 1.0 / D, None, ALU.mult).then_inc(ve.h, ve.step())
                var_done = ve.val
                @block.scalar
                def _(scl, var_done=var_done):
                    scl.wait_ge(ve.h, var_done)
                    scl.activation(p6_ln[:, 4:5], p6_ln[:, 3:4], AF.Ln, bias=cst[:, 0:1])
                    scl.drain()
                    scl.activation(p6_ln[:, 5:6], p6_ln[:, 4:5], AF.Exp, scale=-0.5).then_inc(sc.h, sc.step())
                sd_done = sc.val
                @block.vector
                def _(v, tt=tt, sd_done=sd_done):
                    v.wait_ge(sc.h, sd_done)
                    v.tensor_scalar(p6_xn[:, (tt % 2) * D:(tt % 2 + 1) * D], p6_xc[:], p6_ln[:, 5:6], None, ALU.mult).then_inc(ve.h, ve.step())
                xn_done = ve.val
                @block.sync
                def _(s, i_tile=i_tile, tt=tt, xn_done=xn_done):
                    s.wait_ge(st.h, st.val)
                    s.wait_ge(ve.h, xn_done)
                    s.dma_start(out=x1_tm[i_tile * 128:(i_tile + 1) * 128, :], in_=p6_x1[:, tt * D:(tt + 1) * D]).then_inc(st.h, st.step(16))
                    s.dma_start(out=xn1_tm[i_tile * 128:(i_tile + 1) * 128, :], in_=p6_xn[:, (tt % 2) * D:(tt % 2 + 1) * D]).then_inc(st.h, st.step(16))
    p6es.close()


def phase_barrier(block, sems):
    vals = [(s.h, s.val) for s in sems.values() if s.val > 0]
    for eng in ["tensor", "vector", "scalar", "gpsimd", "sync"]:
        @getattr(block, eng)
        def _(e):
            for h, v in vals:
                e.wait_ge(h, v)


def _finish(block, sems):
    st, ld = sems["st"], sems["ld"]
    @block.sync
    def _(s):
        s.wait_ge(st.h, st.val)
        s.wait_ge(ld.h, ld.val)


def _prep_inputs(inputs, core):
    d = {}
    d["x"] = np.ascontiguousarray(inputs["L0_attn_x"][core])
    d["wq"] = inputs["L0_attn_wq"]
    d["wk"] = inputs["L0_attn_wk"]
    d["wv"] = inputs["L0_attn_wv"]
    d["wo0"] = inputs["L0_attn_wo"]
    d["bo0"] = np.ascontiguousarray(inputs["L0_attn_bo"].reshape(8, 128).T)
    d["projT"] = np.ascontiguousarray(inputs["L0_attn_proj"].T)
    d["wgT"] = np.ascontiguousarray(inputs["L0_ffn_w_gate"].T)
    d["wuT"] = np.ascontiguousarray(inputs["L0_ffn_w_up"].T)
    d["wdT"] = np.ascontiguousarray(inputs["L0_ffn_w_down"].T)
    d["wkv"] = inputs["L1_attn_W_kv"]
    d["gkv"] = np.ascontiguousarray(inputs["L1_attn_g_kv"].reshape(2, 128).T)
    d["wkdec"] = inputs["L1_attn_W_kdec"]
    d["wvdec"] = inputs["L1_attn_W_vdec"]
    d["wq1"] = inputs["L1_attn_W_q"]
    d["wqdec"] = inputs["L1_attn_W_qdec"]
    d["wkrope"] = inputs["L1_attn_W_krope"]
    d["wqrope"] = inputs["L1_attn_W_qrope"]
    d["wo1"] = inputs["L1_attn_W_o"]
    d["wfcT"] = np.ascontiguousarray(inputs["L1_ffn_W_fc"].T)
    d["wprT"] = np.ascontiguousarray(inputs["L1_ffn_W_proj"].T)
    return d


def kernel(**inputs):
    nc, es = build_program()
    in_maps = [_prep_inputs(inputs, core) for core in range(8)]
    res = run_bass_kernel_spmd(nc, in_maps, core_ids=list(range(8)))
    return np.stack([res.results[i]["y"] for i in range(8)], axis=0)


# version 09
# speedup vs baseline: 1.1794x; 1.1794x over previous
import math
from contextlib import ExitStack
import numpy as np
import ml_dtypes
import concourse.bass as bass
import concourse.mybir as mybir
from concourse.bass_utils import run_bass_kernel_spmd

F32 = mybir.dt.float32
BF16 = mybir.dt.bfloat16
I32 = mybir.dt.int32
AF = mybir.ActivationFunctionType
ALU = mybir.AluOpType
AX = mybir.AxisListType

T = 4096
D = 1024
NB = D // 128
NT = T // 128
NCH = T // 512
HEADS = 16
DH = 64
PM = 256
RD = 32
QK = 96
FF0 = 3072
FF1 = 4096
RATIO = PM ** (-0.5)
KEPS = 1e-4
NZ = DH ** (-0.25)
LN_EPS = 1e-5
RMS_EPS = 1e-6
CQ_MARGIN = 20.0
QSCALE = 1.0 / math.sqrt(QK)
LNRATIO = math.log(RATIO)
DEQ = 1.0 / 127.0


class Sem:
    def __init__(self, handle):
        self.h = handle
        self.val = 0

    def step(self, n=1):
        self.val += n
        return n


def _rope_tables():
    inv_freq = 1.0 / 10000.0 ** (np.arange(0, RD, 2, dtype=np.float32) / RD)
    t = np.arange(T, dtype=np.float32)
    freqs = np.outer(t, inv_freq)
    emb = np.concatenate([freqs, freqs], -1)
    cos4 = np.tile(np.cos(emb).T.astype(np.float32), (4, 1))
    sin4 = np.tile(np.sin(emb).T.astype(np.float32), (4, 1))
    return (np.ascontiguousarray(cos4.astype(ml_dtypes.bfloat16)),
            np.ascontiguousarray(sin4.astype(ml_dtypes.bfloat16)))


def _rot128():
    R = np.zeros((32, 32), np.float32)
    for d in range(16):
        R[d, d + 16] = -1.0
        R[d + 16, d] = 1.0
    R128 = np.kron(np.eye(4, dtype=np.float32), R)
    return np.ascontiguousarray(R128.T.astype(ml_dtypes.bfloat16))


def _diag_masks():
    q = np.arange(512)[None, :]
    k = np.arange(128)[:, None]
    out = np.concatenate([(q - k >= 128 * v).astype(np.float32) for v in range(4)], axis=1)
    return np.ascontiguousarray(out.astype(ml_dtypes.bfloat16))


def build_program(upto=None):
    nc = bass.Bass("TRN2", target_bir_lowering=False)

    x_in = nc.dram_tensor("x", [T, D], F32, kind="ExternalInput").ap()
    wq = nc.dram_tensor("wq", [D, D], F32, kind="ExternalInput").ap()
    wk = nc.dram_tensor("wk", [D, D], F32, kind="ExternalInput").ap()
    wv = nc.dram_tensor("wv", [D, D], F32, kind="ExternalInput").ap()
    wo0 = nc.dram_tensor("wo0", [D, D], F32, kind="ExternalInput").ap()
    bo0 = nc.dram_tensor("bo0", [128, 8], F32, kind="ExternalInput").ap()
    projT = nc.dram_tensor("projT", [DH, PM], F32, kind="ExternalInput").ap()
    wgT = nc.dram_tensor("wgT", [D, FF0], I32, kind="ExternalInput").ap()
    wuT = nc.dram_tensor("wuT", [D, FF0], I32, kind="ExternalInput").ap()
    wdT = nc.dram_tensor("wdT", [FF0, D], I32, kind="ExternalInput").ap()
    wkv = nc.dram_tensor("wkv", [D, 256], F32, kind="ExternalInput").ap()
    gkv = nc.dram_tensor("gkv", [128, 2], F32, kind="ExternalInput").ap()
    wkdec = nc.dram_tensor("wkdec", [256, D], F32, kind="ExternalInput").ap()
    wvdec = nc.dram_tensor("wvdec", [256, D], F32, kind="ExternalInput").ap()
    wq1 = nc.dram_tensor("wq1", [D, 512], F32, kind="ExternalInput").ap()
    wqdec = nc.dram_tensor("wqdec", [512, D], F32, kind="ExternalInput").ap()
    wkrope = nc.dram_tensor("wkrope", [D, 512], F32, kind="ExternalInput").ap()
    wqrope = nc.dram_tensor("wqrope", [512, 512], F32, kind="ExternalInput").ap()
    wo1 = nc.dram_tensor("wo1", [D, D], F32, kind="ExternalInput").ap()
    wfcT = nc.dram_tensor("wfcT", [D, FF1], F32, kind="ExternalInput").ap()
    wprT = nc.dram_tensor("wprT", [FF1, D], F32, kind="ExternalInput").ap()

    y_out = nc.dram_tensor("y", [T, D], F32, kind="ExternalOutput").ap()

    def scratch(name, shape, dt):
        kind = "ExternalOutput" if upto == name else "Internal"
        return nc.dram_tensor(name, shape, dt, kind=kind).ap()

    xn0_tm = scratch("xn0_tm", [T, D], BF16)
    g0fm = scratch("g0fm", [NB, 128, T], F32)
    h0sp = scratch("h0sp", [FF0 // 128, 128, T], BF16)
    qfm_sp = scratch("qfm_sp", [NB, 128, T], BF16)
    x1_tm = scratch("x1_tm", [T, D], F32)
    xn1_tm = scratch("xn1_tm", [T, D], BF16)
    s1fm = scratch("s1fm", [NB, 128, T], F32)
    kaug_sp = scratch("kaug_sp", [HEADS, QK + 1, T], BF16)
    qaug_sp = scratch("qaug_sp", [HEADS, QK + 1, T], BF16)
    h1sp = scratch("h1sp", [FF1 // 128, 128, T], BF16)
    cc_in = nc.dram_tensor("cc_in", [128, 128], F32, kind="Internal").ap()
    cc_out = nc.dram_tensor("cc_out", [128, 128], F32, kind="Internal").ap()
    dbg = None
    if upto is not None and upto.startswith("dbg_"):
        ddt = BF16 if upto in ("dbg_ctx",) else F32
        dbg = nc.dram_tensor("dbg", [128, 8192], ddt, kind="ExternalOutput").ap()

    identf_d = nc.inline_tensor(np.eye(128, dtype=np.float32), name="c_identf").ap()
    identb_d = nc.inline_tensor(np.eye(128, dtype=np.float32).astype(ml_dtypes.bfloat16), name="c_identb").ap()
    onesrow_d = nc.inline_tensor(np.ones((1, 128), np.float32), name="c_onesrow").ap()
    onescol_d = nc.inline_tensor(np.ones((128, 1), np.float32), name="c_onescol").ap()
    dcol_d = nc.inline_tensor(np.full((128, 1), 0.5 * NZ * NZ, np.float32), name="c_dcol").ap()
    cos4_np, sin4_np = _rope_tables()
    cos4_d = nc.inline_tensor(cos4_np, name="c_cos4").ap()
    sin4_d = nc.inline_tensor(sin4_np, name="c_sin4").ap()
    rot_d = nc.inline_tensor(_rot128(), name="c_rot").ap()
    masks_d = nc.inline_tensor(_diag_masks(), name="c_masks").ap()

    es = ExitStack()
    with es:
        identf = es.enter_context(nc.sbuf_tensor("identf", [128, 128], F32))
        identb = es.enter_context(nc.sbuf_tensor("identb", [128, 128], BF16))
        onesrow = es.enter_context(nc.sbuf_tensor("onesrow", [1, 128], F32))
        onescol = es.enter_context(nc.sbuf_tensor("onescol", [128, 1], F32))
        dcol = es.enter_context(nc.sbuf_tensor("dcol", [128, 1], F32))
        cst = es.enter_context(nc.sbuf_tensor("cst", [128, 8], F32))

        PF = [es.enter_context(nc.psum_tensor(f"pf{i}", [128, 512], F32)) for i in range(6)]
        PB = [es.enter_context(nc.psum_tensor(f"pb{i}", [128, 1024], BF16)) for i in range(2)]

        sems = {}
        for name in ["ld", "wl", "mm", "ve", "sc", "gp", "st"]:
            sems[name] = Sem(es.enter_context(nc.semaphore(name)))
        ld, wl, mm, ve, sc, gp, st = (sems[k] for k in ["ld", "wl", "mm", "ve", "sc", "gp", "st"])
        ccs = Sem(es.enter_context(nc.semaphore("ccs")))

        block = es.enter_context(nc.Block())

        import os as _os0
        if _os0.environ.get("K_CC_NOCONSTLD"):
            pass
        else:
            _emit_consts = True

        @block.sync
        def _(s):
            if _os0.environ.get("K_CC_NOCONSTLD"):
                return
            s.dma_start(out=identf[:], in_=identf_d).then_inc(ld.h, ld.step(16))
            s.dma_start(out=identb[:], in_=identb_d).then_inc(ld.h, ld.step(16))
            s.dma_start(out=onesrow[:], in_=onesrow_d).then_inc(ld.h, ld.step(16))
            s.dma_start(out=onescol[:], in_=onescol_d).then_inc(ld.h, ld.step(16))
            s.dma_start(out=dcol[:], in_=dcol_d).then_inc(ld.h, ld.step(16))

        @block.vector
        def _(v):
            if _os0.environ.get("K_CC_NOCONSTLD"):
                return
            for col, val in enumerate([LN_EPS, RMS_EPS, -0.5, -1.0, 0.5]):
                v.memset(cst[:, col:col + 1], val)
            v.nop().then_inc(ve.h, ve.step())

        if upto == "dbg_cconly":
            with nc.sbuf_tensor("cc_sb2", [128, 128], F32) as cc_sb2:
                @block.vector
                def _(v):
                    v.memset(cc_sb2[:], -1e30).then_inc(ve.h, ve.step())
                @block.sync
                def _(s):
                    s.wait_ge(ve.h, ve.val)
                    s.dma_start(out=cc_in, in_=cc_sb2[:]).then_inc(st.h, st.step(16))
                @block.gpsimd
                def _(g):
                    g.wait_ge(st.h, st.val)
                    g.collective_compute(
                        "AllReduce", ALU.max,
                        replica_groups=[[0, 1, 2, 3, 4, 5, 6, 7]],
                        ins=[cc_in.opt()], outs=[cc_out.opt()],
                    ).then_inc(ccs.h)
                    ccs.step()
                @block.sync
                def _(s):
                    s.wait_ge(ccs.h, ccs.val)
                    s.dma_start(out=cc_sb2[:], in_=cc_out).then_inc(ld.h, ld.step(16))
                    s.wait_ge(ld.h, ld.val)
                    s.dma_start(out=dbg[0:128, 0:128], in_=cc_sb2[:]).then_inc(st.h, st.step(16))
            _finish(block, sems)
            return nc, es

        wgwu_es = ExitStack()
        wg_sb = wgwu_es.enter_context(nc.sbuf_tensor("wg_sb", [128, NB * FF0], BF16))
        wu_sb = wgwu_es.enter_context(nc.sbuf_tensor("wu_sb", [128, NB * FF0], BF16))

        @block.gpsimd
        def _(g):
            for k in range(NB):
                g.dma_start(out=wg_sb[:, k * FF0:(k + 1) * FF0], in_=wgT[k * 128:(k + 1) * 128, :]).then_inc(wl.h, wl.step(16))
                g.dma_start(out=wu_sb[:, k * FF0:(k + 1) * FF0], in_=wuT[k * 128:(k + 1) * 128, :]).then_inc(wl.h, wl.step(16))
        p1_w_ld = wl.val

        p0_xc_done = {}
        p0_st_done = {}
        with (
            nc.sbuf_tensor("p0_x", [128, 2 * D], F32) as p0x,
            nc.sbuf_tensor("p0_c", [128, 2 * D], F32) as p0c,
            nc.sbuf_tensor("p0_sq", [128, D], F32) as p0sq,
            nc.sbuf_tensor("p0_st", [128, 8], F32) as p0st,
            nc.sbuf_tensor("p0_o", [128, 2 * D], BF16) as p0o,
        ):
            for i in range(NT):
                b = i % 2
                @block.sync
                def _(s, i=i, b=b):
                    s.wait_ge(ld.h, ld.val)
                    if i >= 2:
                        s.wait_ge(ve.h, p0_xc_done[i - 2])
                    s.dma_start(out=p0x[:, b * D:(b + 1) * D], in_=x_in[i * 128:(i + 1) * 128, :]).then_inc(ld.h, ld.step(16))

                @block.vector
                def _(v, i=i, b=b):
                    v.wait_ge(ld.h, ld.val)
                    xt = p0x[:, b * D:(b + 1) * D]
                    xc = p0c[:, b * D:(b + 1) * D]
                    v.reduce_sum(p0st[:, 0:1], xt, axis=AX.X)
                    v.drain()
                    v.tensor_scalar(p0st[:, 1:2], p0st[:, 0:1], 1.0 / D, None, ALU.mult)
                    v.drain()
                    v.tensor_scalar(xc, xt, p0st[:, 1:2], None, ALU.subtract).then_inc(ve.h, ve.step())
                    p0_xc_done[i] = ve.val

                @block.vector
                def _(v, i=i, b=b):
                    v.drain()
                    v.tensor_tensor(p0sq[:], p0c[:, b * D:(b + 1) * D], p0c[:, b * D:(b + 1) * D], ALU.mult)
                    v.drain()
                    v.reduce_sum(p0st[:, 2:3], p0sq[:], axis=AX.X)
                    v.drain()
                    v.tensor_scalar(p0st[:, 3:4], p0st[:, 2:3], 1.0 / D, None, ALU.mult).then_inc(ve.h, ve.step())

                @block.scalar
                def _(scl, i=i, b=b):
                    scl.wait_ge(ve.h, ve.val)
                    scl.activation(p0st[:, 4:5], p0st[:, 3:4], AF.Ln, bias=cst[:, 0:1])
                    scl.drain()
                    scl.activation(p0st[:, 5:6], p0st[:, 4:5], AF.Exp, scale=-0.5).then_inc(sc.h, sc.step())

                @block.vector
                def _(v, i=i, b=b):
                    v.wait_ge(sc.h, sc.val)
                    v.nop().then_inc(ve.h, ve.step())
                    if i >= 2:
                        v.wait_ge(st.h, st.val)
                    v.tensor_scalar(p0o[:, b * D:(b + 1) * D], p0c[:, b * D:(b + 1) * D],
                                    p0st[:, 5:6], None, ALU.mult).then_inc(ve.h, ve.step())

                @block.sync
                def _(s, i=i, b=b):
                    s.wait_ge(st.h, st.val)
                    s.wait_ge(ve.h, ve.val)
                    s.dma_start(out=xn0_tm[i * 128:(i + 1) * 128, :], in_=p0o[:, b * D:(b + 1) * D]).then_inc(st.h, st.step(16))
                    p0_st_done[i] = st.val

        if upto == "xn0_tm":
            _finish(block, sems)
            return nc, es
        p0_store_total = st.val
        import os as _os

        phase_barrier(block, sems)
        with (
            nc.sbuf_tensor("p1_xnf", [128, 2 * NB * 512], BF16) as p1_xnf,
            nc.sbuf_tensor("p1_sg", [128, 2 * 512], F32) as p1_sg,
            nc.sbuf_tensor("p1_h", [128, 2 * 512], BF16) as p1_h,
        ):
            xnf_free = {}
            silu_done = {}
            stt_done = {}
            hst_done = {}
            _p1n = int(_os.environ.get("K_P1_CHUNKS", str(NCH)))
            _p1mt = int(_os.environ.get("K_P1_MT", str(FF0 // 128)))
            for c in range(_p1n):
                cb = c % 2
                @block.sync
                def _(s, c=c, cb=cb):
                    s.wait_ge(ld.h, ld.val)
                    if c == 0:
                        s.wait_ge(st.h, p0_store_total)
                    if c >= 2:
                        s.wait_ge(mm.h, xnf_free[c - 2])
                    for j in range(NB):
                        if _os.environ.get("K_P1_NOTR"):
                            s.dma_start(out=p1_xnf[:, (cb * NB + j) * 512:(cb * NB + j + 1) * 512],
                                        in_=xn0_tm[j * 128:(j + 1) * 128, 0:512]).then_inc(ld.h, ld.step(16))
                        else:
                            s.dma_start(out=p1_xnf[:, (cb * NB + j) * 512:(cb * NB + j + 1) * 512],
                                        in_=xn0_tm[c * 512:(c + 1) * 512, j * 128:(j + 1) * 128],
                                        transpose=True).then_inc(ld.h, ld.step(16))

                ld_chunk = ld.val
                for mt in range(_p1mt):
                    a = mt % 2
                    g_idx = c * 24 + mt
                    @block.tensor
                    def _(te, c=c, cb=cb, mt=mt, a=a, g_idx=g_idx, ld_chunk=ld_chunk):
                        if g_idx == 0:
                            te.wait_ge(wl.h, p1_w_ld)
                        if mt == 0:
                            te.wait_ge(ld.h, ld_chunk)
                        if g_idx >= 2:
                            te.wait_ge(sc.h, silu_done[g_idx - 2])
                            te.wait_ge(ve.h, stt_done[g_idx - 2])
                        for k in range(NB):
                            ins = te.matmul(PF[a][:, 0:512], wg_sb[:, k * FF0 + mt * 128: k * FF0 + (mt + 1) * 128],
                                            p1_xnf[:, (cb * NB + k) * 512:(cb * NB + k + 1) * 512],
                                            start=(k == 0), stop=(k == NB - 1))
                            if k == NB - 1:
                                ins.then_inc(mm.h, mm.step())
                        for k in range(NB):
                            ins = te.matmul(PF[2 + a][:, 0:512], wu_sb[:, k * FF0 + mt * 128: k * FF0 + (mt + 1) * 128],
                                            p1_xnf[:, (cb * NB + k) * 512:(cb * NB + k + 1) * 512],
                                            start=(k == 0), stop=(k == NB - 1))
                            if k == NB - 1:
                                ins.then_inc(mm.h, mm.step())
                    mm_gate = mm.val - 1
                    mm_up = mm.val
                    @block.scalar
                    def _(scl, a=a, g_idx=g_idx, mm_gate=mm_gate):
                        scl.wait_ge(mm.h, mm_gate)
                        if g_idx >= 2:
                            scl.wait_ge(ve.h, stt_done[g_idx - 2])
                        if _os.environ.get("K_P1_NOSILU"):
                            scl.activation(p1_sg[:, a * 512:(a + 1) * 512], PF[a][:, 0:512], AF.Square, scale=DEQ).then_inc(sc.h, sc.step())
                        else:
                            scl.activation(p1_sg[:, a * 512:(a + 1) * 512], PF[a][:, 0:512], AF.Silu, scale=DEQ).then_inc(sc.h, sc.step())
                    silu_done[g_idx] = sc.val
                    @block.vector
                    def _(v, a=a, g_idx=g_idx, mm_up=mm_up):
                        v.wait_ge(mm.h, mm_up)
                        v.wait_ge(sc.h, silu_done[g_idx])
                        if g_idx >= 2:
                            v.wait_ge(gp.h, gp.val)
                        v.scalar_tensor_tensor(p1_h[:, a * 512:(a + 1) * 512],
                                               p1_sg[:, a * 512:(a + 1) * 512], DEQ,
                                               PF[2 + a][:, 0:512], ALU.mult, ALU.mult).then_inc(ve.h, ve.step())
                    stt_done[g_idx] = ve.val
                    if _os.environ.get("K_P1_NOST"):
                        @block.sync
                        def _(s, g_idx=g_idx):
                            s.wait_ge(ve.h, stt_done[g_idx])
                            s.nop().then_inc(gp.h, gp.step(16))
                    else:
                        @block.gpsimd
                        def _(g, c=c, mt=mt, a=a, g_idx=g_idx):
                            g.wait_ge(gp.h, gp.val)
                            g.wait_ge(ve.h, stt_done[g_idx])
                            g.dma_start(out=h0sp[mt, :, c * 512:(c + 1) * 512],
                                        in_=p1_h[:, a * 512:(a + 1) * 512]).then_inc(gp.h, gp.step(16))
                    hst_done[g_idx] = gp.val
                xnf_free[c] = mm.val

        wgwu_es.close()
        if upto == "h0sp":
            _finish(block, sems)
            return nc, es

        phase_barrier(block, sems)
        with (
            nc.sbuf_tensor("wd_sb", [128, (FF0 // 128) * D], BF16) as wd_sb,
            nc.sbuf_tensor("p2_h", [128, 2 * (FF0 // 128) * 512], BF16) as p2_h,
            nc.sbuf_tensor("p2_g", [128, 2 * 512], F32) as p2_g,
        ):
            @block.gpsimd
            def _(g):
                for k in range(FF0 // 128):
                    g.dma_start(out=wd_sb[:, k * D:(k + 1) * D], in_=wdT[k * 128:(k + 1) * 128, :]).then_inc(wl.h, wl.step(16))
            p2_w_ld = wl.val

            h_free = {}
            g_done = {}
            g_store = {}
            for c in range(NCH):
                cb = c % 2
                @block.sync
                def _(s, c=c, cb=cb):
                    s.wait_ge(ld.h, ld.val)
                    if c >= 2:
                        s.wait_ge(mm.h, h_free[c - 2])
                    for part in range(4):
                        s.dma_start(out=p2_h[:, (cb * 24 + part * 6) * 512:(cb * 24 + (part + 1) * 6) * 512].rearrange("p (k t) -> p k t", k=6),
                                    in_=h0sp[part * 6:(part + 1) * 6, :, c * 512:(c + 1) * 512].rearrange("k p t -> p k t")).then_inc(ld.h, ld.step(16))

                ld_chunk = ld.val
                for mt in range(NB):
                    a = mt % 2
                    g_idx = c * NB + mt
                    @block.tensor
                    def _(te, cb=cb, mt=mt, a=a, g_idx=g_idx, ld_chunk=ld_chunk):
                        if g_idx == 0:
                            te.wait_ge(wl.h, p2_w_ld)
                        if mt == 0:
                            te.wait_ge(ld.h, ld_chunk)
                        if g_idx >= 2:
                            te.wait_ge(sc.h, g_done[g_idx - 2])
                        for k in range(FF0 // 128):
                            ins = te.matmul(PF[a][:, 0:512], wd_sb[:, k * D + mt * 128: k * D + (mt + 1) * 128],
                                            p2_h[:, (cb * 24 + k) * 512:(cb * 24 + k + 1) * 512],
                                            start=(k == 0), stop=(k == FF0 // 128 - 1))
                            if k == FF0 // 128 - 1:
                                ins.then_inc(mm.h, mm.step())
                    mm_y = mm.val
                    @block.scalar
                    def _(scl, a=a, g_idx=g_idx, mm_y=mm_y):
                        scl.wait_ge(mm.h, mm_y)
                        if g_idx >= 2:
                            scl.wait_ge(st.h, st.val)
                        scl.activation(p2_g[:, a * 512:(a + 1) * 512], PF[a][:, 0:512], AF.Silu, scale=DEQ).then_inc(sc.h, sc.step())
                    g_done[g_idx] = sc.val
                    @block.sync
                    def _(s, c=c, mt=mt, a=a, g_idx=g_idx):
                        s.wait_ge(st.h, st.val)
                        s.wait_ge(sc.h, g_done[g_idx])
                        s.dma_start(out=g0fm[mt, :, c * 512:(c + 1) * 512],
                                    in_=p2_g[:, a * 512:(a + 1) * 512]).then_inc(st.h, st.step(16))
                    g_store[g_idx] = st.val
                h_free[c] = mm.val

        if upto == "g0fm":
            _finish(block, sems)
            return nc, es

        phase_barrier(block, sems)
        with (
            nc.sbuf_tensor("ctx_sb", [128, HEADS * 130], BF16) as ctx_sb,
            nc.sbuf_tensor("diagk_sb", [128, HEADS * NT], F32) as diagk_sb,
            nc.sbuf_tensor("stt0", [128, 16], F32) as stt0,
            nc.sbuf_tensor("projTs", [128, PM], BF16) as projTs,
            nc.sbuf_tensor("gmax_sb", [1, 8], F32) as gmax_sb,
            nc.sbuf_tensor("cc_sb", [128, 128], F32) as cc_sb,
        ):
            with (
                nc.sbuf_tensor("k_fm", [128, NB * T], BF16) as k_fm,
                nc.sbuf_tensor("v0_tm", [128, NT * HEADS * 65], BF16) as v0_tm,
            ):
                with (
                    nc.sbuf_tensor("wq_sb", [128, NB * D], BF16) as wq_sb,
                    nc.sbuf_tensor("wk_sb", [128, NB * D], BF16) as wk_sb,
                    nc.sbuf_tensor("p3_xnf", [128, 2 * NB * 512], BF16) as p3_xnf,
                    nc.sbuf_tensor("p3_q", [128, 2 * 512], BF16) as p3_q,
                    nc.sbuf_tensor("projTf", [128, PM], F32) as projTf,
                ):
                    @block.gpsimd
                    def _(g):
                        for k in range(NB):
                            g.dma_start(out=wq_sb[:, k * D:(k + 1) * D], in_=wq[k * 128:(k + 1) * 128, :]).then_inc(wl.h, wl.step(16))
                            g.dma_start(out=wk_sb[:, k * D:(k + 1) * D], in_=wk[k * 128:(k + 1) * 128, :]).then_inc(wl.h, wl.step(16))
                    p3_w_ld = wl.val

                    @block.sync
                    def _(s):
                        s.dma_start(out=projTf[0:64, :], in_=projT).then_inc(ld.h, ld.step(16))
                        s.dma_start(out=projTf[64:128, :], in_=projT).then_inc(ld.h, ld.step(16))
                    @block.vector
                    def _(v):
                        v.wait_ge(ld.h, ld.val)
                        v.tensor_scalar(projTs[:], projTf[:], NZ, None, ALU.mult).then_inc(ve.h, ve.step())
                        v.memset(v0_tm[:], 1.0).then_inc(ve.h, ve.step())

                    qev_done = {}
                    qst_done = {}
                    for c in range(NCH):
                        cb = c % 2
                        ld_before = ld.val
                        @block.sync
                        def _(s, c=c, cb=cb):
                            s.wait_ge(ld.h, ld.val)
                            if c >= 2:
                                s.wait_ge(mm.h, qev_done[(c - 2, "mmend")])
                            for j in range(NB):
                                s.dma_start(out=p3_xnf[:, (cb * NB + j) * 512:(cb * NB + j + 1) * 512],
                                            in_=xn0_tm[c * 512:(c + 1) * 512, j * 128:(j + 1) * 128],
                                            transpose=True).then_inc(ld.h, ld.step(16))
                        ld_chunk = ld.val
                        for mt in range(NB):
                            a = mt % 2
                            g_idx = c * NB + mt
                            @block.tensor
                            def _(te, c=c, cb=cb, mt=mt, a=a, g_idx=g_idx, ld_chunk=ld_chunk):
                                if g_idx == 0:
                                    te.wait_ge(wl.h, p3_w_ld)
                                if mt == 0:
                                    te.wait_ge(ld.h, ld_chunk)
                                if g_idx >= 1:
                                    te.wait_ge(ve.h, qev_done[g_idx - 1])
                                for k in range(NB):
                                    ins = te.matmul(PF[a][:, 0:512], wq_sb[:, k * D + mt * 128: k * D + (mt + 1) * 128],
                                                    p3_xnf[:, (cb * NB + k) * 512:(cb * NB + k + 1) * 512],
                                                    start=(k == 0), stop=(k == NB - 1))
                                    if k == NB - 1:
                                        ins.then_inc(mm.h, mm.step())
                                for k in range(NB):
                                    ins = te.matmul(PF[2 + a][:, 0:512], wk_sb[:, k * D + mt * 128: k * D + (mt + 1) * 128],
                                                    p3_xnf[:, (cb * NB + k) * 512:(cb * NB + k + 1) * 512],
                                                    start=(k == 0), stop=(k == NB - 1))
                                    if k == NB - 1:
                                        ins.then_inc(mm.h, mm.step())
                            mm_q = mm.val - 1
                            mm_k = mm.val
                            @block.vector
                            def _(v, c=c, mt=mt, a=a, g_idx=g_idx, mm_q=mm_q, mm_k=mm_k):
                                v.wait_ge(mm.h, mm_q)
                                v.wait_ge(gp.h, gp.val)
                                v.tensor_copy(p3_q[:, a * 512:(a + 1) * 512], PF[a][:, 0:512]).then_inc(ve.h, ve.step())
                                v.wait_ge(mm.h, mm_k)
                                v.tensor_copy(k_fm[:, mt * T + c * 512: mt * T + (c + 1) * 512], PF[2 + a][:, 0:512]).then_inc(ve.h, ve.step())
                            qev_done[g_idx] = ve.val
                            @block.gpsimd
                            def _(g, c=c, mt=mt, a=a, g_idx=g_idx):
                                g.wait_ge(gp.h, gp.val)
                                g.wait_ge(ve.h, qev_done[g_idx] - 1)
                                g.dma_start(out=qfm_sp[mt, :, c * 512:(c + 1) * 512],
                                            in_=p3_q[:, a * 512:(a + 1) * 512]).then_inc(gp.h, gp.step(16))
                            qst_done[g_idx] = gp.val
                        qev_done[(c, "mmend")] = mm.val

                phase_barrier(block, sems)
                with (
                    nc.sbuf_tensor("wv_sb", [128, NB * D], BF16) as wv_sb,
                    nc.sbuf_tensor("p3b_xnf", [128, 2 * NB * 512], BF16) as p3b_xnf,
                ):
                    @block.gpsimd
                    def _(g):
                        for k in range(NB):
                            g.dma_start(out=wv_sb[:, k * D:(k + 1) * D], in_=wv[k * 128:(k + 1) * 128, :]).then_inc(wl.h, wl.step(16))
                    p3b_w_ld = wl.val

                    vev_done = {}
                    for c in range(NCH):
                        cb = c % 2
                        @block.sync
                        def _(s, c=c, cb=cb):
                            s.wait_ge(ld.h, ld.val)
                            if c >= 2:
                                s.wait_ge(mm.h, vev_done[(c - 2, "mmend")])
                            for j in range(NB):
                                s.dma_start(out=p3b_xnf[:, (cb * NB + j) * 512:(cb * NB + j + 1) * 512],
                                            in_=xn0_tm[c * 512:(c + 1) * 512, j * 128:(j + 1) * 128],
                                            transpose=True).then_inc(ld.h, ld.step(16))
                        ld_chunk = ld.val
                        for tt in range(4):
                            for nh in range(2):
                                a = (tt * 2 + nh) % 2
                                g_idx = c * 8 + tt * 2 + nh
                                @block.tensor
                                def _(te, c=c, cb=cb, tt=tt, nh=nh, a=a, g_idx=g_idx, ld_chunk=ld_chunk):
                                    if g_idx == 0:
                                        te.wait_ge(wl.h, p3b_w_ld)
                                    if tt == 0 and nh == 0:
                                        te.wait_ge(ld.h, ld_chunk)
                                    if g_idx >= 2:
                                        te.wait_ge(ve.h, vev_done[g_idx - 2])
                                    for k in range(NB):
                                        ins = te.matmul(PF[a][:, 0:512],
                                                        p3b_xnf[:, (cb * NB + k) * 512 + tt * 128: (cb * NB + k) * 512 + (tt + 1) * 128],
                                                        wv_sb[:, k * D + nh * 512: k * D + (nh + 1) * 512],
                                                        start=(k == 0), stop=(k == NB - 1))
                                        if k == NB - 1:
                                            ins.then_inc(mm.h, mm.step())
                                mm_v = mm.val
                                @block.vector
                                def _(v, c=c, tt=tt, nh=nh, a=a, mm_v=mm_v):
                                    v.wait_ge(mm.h, mm_v)
                                    base = (c * 4 + tt) * HEADS * 65 + nh * 8 * 65
                                    dst = v0_tm[:, base: base + 8 * 65].rearrange("p (h e) -> p h e", h=8)[:, :, 0:64]
                                    v.tensor_copy(dst, PF[a][:, 0:512].rearrange("p (h e) -> p h e", h=8)).then_inc(ve.h, ve.step())
                                vev_done[g_idx] = ve.val
                        vev_done[(c, "mmend")] = mm.val

                phase_barrier(block, sems)
                with (
                    nc.sbuf_tensor("k2_sb", [128, T], F32) as k2_sb,
                    nc.sbuf_tensor("kp_f", [128, 2 * PM], F32) as kp_f,
                    nc.sbuf_tensor("kp_b", [128, 2 * PM], BF16) as kp_b,
                    nc.sbuf_tensor("ctxe", [128, PM], BF16) as ctxe,
                ):
                    @block.vector
                    def _(v):
                        v.memset(cc_sb[:], -1e30).then_inc(ve.h, ve.step())
                    for h in range(HEADS):
                        r0 = (h % 2) * 64
                        blk = h // 2
                        @block.vector
                        def _(v, h=h, r0=r0, blk=blk):
                            if h > 0:
                                v.wait_ge(mm.h, mm.val)
                            v.drain()
                            v.tensor_tensor(k2_sb[r0:r0 + 64, :], k_fm[r0:r0 + 64, blk * T:(blk + 1) * T],
                                            k_fm[r0:r0 + 64, blk * T:(blk + 1) * T], ALU.mult).then_inc(ve.h, ve.step())
                        sq_done = ("ve", ve.val)
                        for tt in range(NT):
                            a = tt % 2
                            dbank = 4 + (tt // 8) % 2
                            @block.tensor
                            def _(te, h=h, r0=r0, blk=blk, tt=tt, a=a, dbank=dbank, sq_done=sq_done):
                                if tt == 0:
                                    te.wait_ge(ve.h, sq_done[1])
                                te.wait_ge(ve.h, ve.val)
                                te.matmul(PF[a][:, 0:PM], k_fm[r0:r0 + 64, blk * T + tt * 128: blk * T + (tt + 1) * 128],
                                          projTs[r0:r0 + 64, :], start=True, stop=True).then_inc(mm.h, mm.step())
                                if tt % 8 == 0 and (h * NT + tt) >= 16:
                                    te.wait_ge(ve.h, ve.val)
                                te.matmul(PF[dbank][:, (tt % 8):(tt % 8) + 1], k2_sb[r0:r0 + 64, tt * 128:(tt + 1) * 128],
                                          dcol[r0:r0 + 64, :], start=True, stop=True).then_inc(mm.h, mm.step())
                            mm_dd = mm.val - 1
                            mm_dg = mm.val
                            @block.vector
                            def _(v, h=h, tt=tt, a=a, dbank=dbank, mm_dd=mm_dd, mm_dg=mm_dg):
                                v.wait_ge(mm.h, mm_dd)
                                v.reduce_max(stt0[:, 2 + a:3 + a], PF[a][:, 0:PM], axis=AX.X)
                                v.drain()
                                if h == 0 and tt == 0:
                                    v.tensor_copy(stt0[:, 0:1], stt0[:, 2:3]).then_inc(ve.h, ve.step())
                                else:
                                    v.tensor_tensor(stt0[:, 0:1], stt0[:, 0:1], stt0[:, 2 + a:3 + a], ALU.max).then_inc(ve.h, ve.step())
                                if tt % 8 == 7:
                                    v.wait_ge(mm.h, mm_dg)
                                    v.tensor_copy(diagk_sb[:, h * NT + tt - 7: h * NT + tt + 1], PF[dbank][:, 0:8]).then_inc(ve.h, ve.step())
                    @block.tensor
                    def _(te):
                        te.wait_ge(ve.h, ve.val)
                        te.transpose(PF[0][0:1, 0:128], stt0[:, 0:1], identf[:]).then_inc(mm.h, mm.step())
                    @block.vector
                    def _(v):
                        v.wait_ge(mm.h, mm.val)
                        v.reduce_max(gmax_sb[0:1, 0:1], PF[0][0:1, 0:128], axis=AX.X)
                        v.drain()
                        v.tensor_copy(cc_sb[0:1, 0:1], gmax_sb[0:1, 0:1]).then_inc(ve.h, ve.step())
                    @block.sync
                    def _(s):
                        s.wait_ge(ve.h, ve.val)
                        s.dma_start(out=cc_in, in_=cc_sb[:]).then_inc(st.h, st.step(16))
                    if _os.environ.get("K_NOCC"):
                        @block.gpsimd
                        def _(g):
                            g.wait_ge(st.h, st.val)
                            g.dma_start(out=cc_out, in_=cc_in).then_inc(ccs.h, 16)
                            ccs.step(16)
                    else:
                        @block.gpsimd
                        def _(g):
                            g.wait_ge(st.h, st.val)
                            g.collective_compute(
                                "AllReduce", ALU.max,
                                replica_groups=[[0, 1, 2, 3, 4, 5, 6, 7]],
                                ins=[cc_in.opt()], outs=[cc_out.opt()],
                            ).then_inc(ccs.h)
                            ccs.step()
                    @block.sync
                    def _(s):
                        s.wait_ge(ccs.h, ccs.val)
                        s.dma_start(out=gmax_sb[0:1, 0:8], in_=cc_out[0:1, 0:8]).then_inc(ld.h, ld.step(16))
                    @block.tensor
                    def _(te):
                        te.wait_ge(ld.h, ld.val)
                        te.matmul(PF[1][:, 0:1], onesrow[:], gmax_sb[0:1, 0:1], start=True, stop=True).then_inc(mm.h, mm.step())
                    @block.vector
                    def _(v):
                        v.wait_ge(mm.h, mm.val)
                        v.tensor_scalar(stt0[:, 1:2], PF[1][:, 0:1], -1.0, LNRATIO, ALU.mult, ALU.add).then_inc(ve.h, ve.step())
                        v.drain()

                    if upto == "dbg_gmax":
                        @block.sync
                        def _(s):
                            s.wait_ge(ve.h, ve.val)
                            s.dma_start(out=dbg[0:1, 0:8], in_=gmax_sb[0:1, 0:8]).then_inc(st.h, st.step(16))
                        _finish(block, sems)
                        return nc, es

                    kp_done = {}
                    ctxev = {}
                    prev_ctx_ev = None
                    for h in range(HEADS):
                        r0 = (h % 2) * 64
                        blk = h // 2
                        mm0, sc0, ve0 = mm.val, sc.val, ve.val
                        dd_of = lambda tt: mm0 + 2 * tt + 1
                        ctx_of = lambda tt: mm0 + 2 * tt + 2
                        bias_of = lambda tt: ve0 + 2 * tt + 1
                        cvt_of = lambda tt: ve0 + 2 * tt + 2
                        exp_of = lambda tt: sc0 + tt + 1
                        @block.tensor
                        def _(te, h=h, r0=r0, blk=blk, pce=prev_ctx_ev, ve0=ve0):
                            for tt in range(NT):
                                a = tt % 2
                                if tt == 0:
                                    te.wait_ge(ve.h, ve0)
                                if tt >= 2:
                                    te.wait_ge(sc.h, exp_of(tt - 2))
                                te.matmul(PF[a][:, 0:PM], k_fm[r0:r0 + 64, blk * T + tt * 128: blk * T + (tt + 1) * 128],
                                          projTs[r0:r0 + 64, :], start=True, stop=True).then_inc(mm.h, mm.step())
                                te.wait_ge(ve.h, cvt_of(tt))
                                if tt == 0 and pce is not None:
                                    te.wait_ge(ve.h, pce)
                                te.matmul(PF[2][0:65, 0:PM], v0_tm[:, (tt * HEADS + h) * 65: (tt * HEADS + h) * 65 + 65],
                                          kp_b[:, a * PM:(a + 1) * PM],
                                          start=(tt == 0), stop=(tt == NT - 1)).then_inc(mm.h, mm.step())
                        @block.vector
                        def _(v, h=h):
                            for tt in range(NT):
                                a = tt % 2
                                if tt >= 2:
                                    v.wait_ge(sc.h, exp_of(tt - 2))
                                v.tensor_scalar(stt0[:, 4 + a:5 + a], diagk_sb[:, h * NT + tt: h * NT + tt + 1],
                                                -1.0, stt0[:, 1:2], ALU.mult, ALU.add).then_inc(ve.h, ve.step())
                                v.wait_ge(sc.h, exp_of(tt))
                                if tt >= 2:
                                    v.wait_ge(mm.h, ctx_of(tt - 2))
                                v.tensor_scalar(kp_b[:, a * PM:(a + 1) * PM], kp_f[:, a * PM:(a + 1) * PM],
                                                RATIO * KEPS, None, ALU.add).then_inc(ve.h, ve.step())
                        @block.scalar
                        def _(scl, h=h):
                            for tt in range(NT):
                                a = tt % 2
                                scl.wait_ge(mm.h, dd_of(tt))
                                scl.wait_ge(ve.h, bias_of(tt))
                                if tt >= 2:
                                    scl.wait_ge(ve.h, cvt_of(tt - 2))
                                scl.activation(kp_f[:, a * PM:(a + 1) * PM], PF[a][:, 0:PM], AF.Exp,
                                               bias=stt0[:, 4 + a:5 + a]).then_inc(sc.h, sc.step())
                        assert mm.val == mm0 + 2 * NT and ve.val == ve0 + 2 * NT and sc.val == sc0 + NT
                        mm_ctx = mm.val
                        @block.vector
                        def _(v, h=h, mm_ctx=mm_ctx):
                            v.wait_ge(mm.h, mm_ctx)
                            v.tensor_copy(ctxe[0:65, 0:PM], PF[2][0:65, 0:PM]).then_inc(ve.h, ve.step())
                        ctx_ev = ve.val
                        @block.tensor
                        def _(te, h=h, ctx_ev=ctx_ev):
                            te.wait_ge(ve.h, ctx_ev)
                            te.transpose(PB[0][0:128, 0:65], ctxe[0:65, 0:128], identb[0:65, 0:65]).then_inc(mm.h, mm.step())
                            te.transpose(PB[1][0:128, 0:65], ctxe[0:65, 128:256], identb[0:65, 0:65]).then_inc(mm.h, mm.step())
                        mm_tr = mm.val
                        @block.vector
                        def _(v, h=h, mm_tr=mm_tr):
                            v.wait_ge(mm.h, mm_tr)
                            v.tensor_copy(ctx_sb[:, h * 130: h * 130 + 65], PB[0][:, 0:65])
                            v.drain()
                            v.tensor_copy(ctx_sb[:, h * 130 + 65: h * 130 + 130], PB[1][:, 0:65]).then_inc(ve.h, ve.step())
                        prev_ctx_ev = ve.val

                    if upto == "dbg_ctx":
                        @block.sync
                        def _(s):
                            s.wait_ge(ve.h, ve.val)
                            s.dma_start(out=dbg[:, 0:HEADS * 130], in_=ctx_sb[:]).then_inc(st.h, st.step(16))
                        _finish(block, sems)
                        return nc, es

            _emit_p6(dict(nc=nc, block=block, sems=sems, PF=PF, PB=PB, identf=identf,
                          identb=identb, onesrow=onesrow, dcol=dcol, cst=cst, projTs=projTs,
                          ctx_sb=ctx_sb, stt0=stt0, x_in=x_in, wo0=wo0, bo0=bo0,
                          qfm_sp=qfm_sp, g0fm=g0fm, x1_tm=x1_tm, xn1_tm=xn1_tm))

        if upto in ("x1_tm", "xn1_tm"):
            _finish(block, sems)
            return nc, es

        phase_barrier(block, sems)
        v1_es = ExitStack()
        v1_tm = v1_es.enter_context(nc.sbuf_tensor("v1_tm", [128, NT * HEADS * 65], BF16, side="right"))
        _emit_p7(dict(nc=nc, block=block, sems=sems, ccs=ccs, PF=PF, PB=PB, identf=identf,
                      identb=identb, onesrow=onesrow, onescol=onescol, cst=cst,
                      xn1_tm=xn1_tm, kaug_sp=kaug_sp, qaug_sp=qaug_sp, v1_tm=v1_tm,
                      wkv=wkv, gkv=gkv, wkdec=wkdec, wvdec=wvdec, wq1=wq1, wqdec=wqdec,
                      wkrope=wkrope, wqrope=wqrope, cos4_d=cos4_d, sin4_d=sin4_d, rot_d=rot_d,
                      upto=upto, dbg=dbg))
        if upto in ("kaug_sp", "qaug_sp"):
            _finish(block, sems)
            return nc, es

        E2 = dict(nc=nc, block=block, sems=sems, PF=PF, PB=PB, identf=identf, identb=identb,
                  onesrow=onesrow, onescol=onescol, cst=cst, masks_d=masks_d,
                  xn1_tm=xn1_tm, x1_tm=x1_tm, kaug_sp=kaug_sp, qaug_sp=qaug_sp, v1_tm=v1_tm,
                  s1fm=s1fm, h1sp=h1sp, wo1=wo1, wfcT=wfcT, wprT=wprT, y_out=y_out,
                  upto=upto, dbg=dbg, es=es, v1_es=v1_es)
        r = _emit_p8_p11(E2)
        if r:
            _finish(block, sems)
            return nc, es

        _finish(block, sems)
    return nc, es


def _emit_p8_p11(E):
    (nc, block, sems, PF, PB, identf, identb, onesrow, onescol, cst, masks_d,
     xn1_tm, x1_tm, kaug_sp, qaug_sp, v1_tm, s1fm, h1sp, wo1, wfcT, wprT, y_out,
     upto, dbg, es) = (
        E["nc"], E["block"], E["sems"], E["PF"], E["PB"], E["identf"], E["identb"],
        E["onesrow"], E["onescol"], E["cst"], E["masks_d"], E["xn1_tm"], E["x1_tm"],
        E["kaug_sp"], E["qaug_sp"], E["v1_tm"], E["s1fm"], E["h1sp"], E["wo1"],
        E["wfcT"], E["wprT"], E["y_out"], E["upto"], E["dbg"], E["es"])
    ld, wl, mm, ve, sc, gp, st = (sems[k] for k in ["ld", "wl", "mm", "ve", "sc", "gp", "st"])

    phase_barrier(block, sems)
    attn1_es = ExitStack()
    attn1 = attn1_es.enter_context(nc.sbuf_tensor("attn1", [128, NB * T], BF16))
    p8 = ExitStack()
    _a = p8.enter_context
    ka_sb = _a(nc.sbuf_tensor("p8_ka", [97, 2 * T], BF16))
    qa_sb = _a(nc.sbuf_tensor("p8_qa", [97, 2 * T], BF16))
    pT = _a(nc.sbuf_tensor("p8_pT", [128, 3 * 512], BF16))
    mask_sb = _a(nc.sbuf_tensor("p8_mask", [128, 4 * 512], BF16))
    dinv = _a(nc.sbuf_tensor("p8_dinv", [1, 512], F32))
    tmpo = _a(nc.sbuf_tensor("p8_tmpo", [64, 512], F32))

    @block.sync
    def _(s):
        s.wait_ge(ld.h, ld.val)
        s.dma_start(out=mask_sb[:], in_=masks_d).then_inc(ld.h, ld.step(16))
        s.dma_start(out=ka_sb[:, 0:T], in_=kaug_sp[0]).then_inc(ld.h, ld.step(16))
        s.dma_start(out=qa_sb[:, 0:T], in_=qaug_sp[0]).then_inc(ld.h, ld.step(16))
    hd_ld = {0: ld.val}
    hd_mm_done = {}
    for h in range(HEADS):
        hb = h % 2
        r0 = (h % 2) * 64
        if h + 1 < HEADS:
            @block.sync
            def _(s, h=h):
                s.wait_ge(ld.h, ld.val)
                if h >= 1:
                    s.wait_ge(mm.h, hd_mm_done[h - 1])
                s.dma_start(out=ka_sb[:, ((h + 1) % 2) * T:((h + 1) % 2 + 1) * T], in_=kaug_sp[h + 1]).then_inc(ld.h, ld.step(16))
                s.dma_start(out=qa_sb[:, ((h + 1) % 2) * T:((h + 1) % 2 + 1) * T], in_=qaug_sp[h + 1]).then_inc(ld.h, ld.step(16))
            hd_ld[h + 1] = ld.val
        if h == 0:
            prev_evac = None
        for c in range(NCH):
            nt8 = 4 * c + 4
            mm0, sc0, ve0 = mm.val, ve.val, sc.val
            sc0 = sc.val
            ve0 = ve.val
            mm0 = mm.val
            mm_s_of = lambda i: mm0 + (1 if i == 0 else 2 * i)
            pv_of = lambda i: mm0 + 2 * i + 3 if i < nt8 - 1 else mm0 + 2 * nt8
            exp_of = lambda i: sc0 + i + 1
            mask_of = lambda i: ve0 + (i - 4 * c) + 1
            @block.tensor
            def _(te, h=h, hb=hb, c=c, nt8=nt8, prev_ev=prev_evac):
                def s_mm(i):
                    sb3 = [0, 1, 4][i % 3]
                    if i == 0 and c == 0:
                        te.wait_ge(ld.h, hd_ld[h])
                    if i >= 3:
                        te.wait_ge(sc.h, exp_of(i - 3))
                    te.matmul(PF[sb3][:, 0:512], ka_sb[:, hb * T + i * 128: hb * T + (i + 1) * 128],
                              qa_sb[:, hb * T + c * 512: hb * T + (c + 1) * 512],
                              start=True, stop=True).then_inc(mm.h, mm.step())
                def pv_mm(i):
                    if i >= 4 * c:
                        te.wait_ge(ve.h, mask_of(i))
                    else:
                        te.wait_ge(sc.h, exp_of(i))
                    if i == 0 and prev_ev is not None:
                        te.wait_ge(ve.h, max(prev_ev))
                    te.matmul(PF[2][0:65, 0:512], v1_tm[:, i * HEADS * 65 + h * 65: i * HEADS * 65 + (h + 1) * 65],
                              pT[:, (i % 3) * 512:(i % 3 + 1) * 512],
                              start=(i == 0), stop=(i == nt8 - 1)).then_inc(mm.h, mm.step())
                s_mm(0)
                for i in range(1, nt8):
                    s_mm(i)
                    pv_mm(i - 1)
                pv_mm(nt8 - 1)
            @block.scalar
            def _(scl, h=h, c=c, nt8=nt8):
                for i in range(nt8):
                    scl.wait_ge(mm.h, mm_s_of(i))
                    if i >= 3:
                        scl.wait_ge(mm.h, pv_of(i - 3))
                    scl.activation(pT[:, (i % 3) * 512:(i % 3 + 1) * 512], PF[[0, 1, 4][i % 3]][:, 0:512],
                                   AF.Exp).then_inc(sc.h, sc.step())
            assert sc.val == sc0 + nt8
            @block.vector
            def _(v, c=c, nt8=nt8):
                for i in range(4 * c, nt8):
                    v.wait_ge(sc.h, exp_of(i))
                    vv = i - 4 * c
                    v.tensor_tensor(pT[:, (i % 3) * 512:(i % 3 + 1) * 512], pT[:, (i % 3) * 512:(i % 3 + 1) * 512],
                                    mask_sb[:, vv * 512:(vv + 1) * 512], ALU.mult)
                    v.drain()
                    v.nop().then_inc(ve.h, ve.step())
            assert ve.val == ve0 + 4
            assert mm.val == mm0 + 2 * nt8
            mm_o = mm.val
            @block.vector
            def _(v, mm_o=mm_o):
                v.wait_ge(mm.h, mm_o)
                v.reciprocal(dinv[0:1, :], PF[2][64:65, 0:512])
                v.drain()
                v.nop().then_inc(ve.h, ve.step())
            rec_done = ve.val
            @block.tensor
            def _(te, rec_done=rec_done):
                te.wait_ge(ve.h, rec_done)
                te.matmul(PF[3][0:64, 0:512], onesrow[0:1, 0:64], dinv[0:1, :], start=True, stop=True).then_inc(mm.h, mm.step())
            mm_b = mm.val
            @block.vector
            def _(v, mm_o=mm_o):
                v.wait_ge(mm.h, mm_o)
                v.tensor_copy(tmpo[:], PF[2][0:64, 0:512])
                v.drain()
                v.nop().then_inc(ve.h, ve.step())
            cp_done = ve.val
            @block.vector
            def _(v, h=h, r0=r0, c=c, mm_b=mm_b, cp_done=cp_done):
                v.wait_ge(mm.h, mm_b)
                v.tensor_tensor(attn1[r0:r0 + 64, (h // 2) * T + c * 512:(h // 2) * T + (c + 1) * 512],
                                tmpo[:], PF[3][0:64, 0:512], ALU.mult).then_inc(ve.h, ve.step())
            prev_evac = (cp_done, rec_done)
        hd_mm_done[h] = mm.val
    p8.close()
    E["v1_es"].close()
    del ka_sb, qa_sb, pT, mask_sb, dinv, tmpo

    phase_barrier(block, sems)
    p9 = ExitStack()
    wo1_sb = p9.enter_context(nc.sbuf_tensor("wo1_sb", [128, NB * D], BF16))
    s1_ev = p9.enter_context(nc.sbuf_tensor("p9_s1", [128, NB * 512], F32))
    @block.gpsimd
    def _(g):
        g.wait_ge(wl.h, wl.val)
        for k in range(NB):
            g.dma_start(out=wo1_sb[:, k * D:(k + 1) * D], in_=wo1[k * 128:(k + 1) * 128, :]).then_inc(wl.h, wl.step(16))
    p9_w_ld = wl.val
    for c in range(NCH):
        mm0, ve0 = mm.val, ve.val
        mm_of = lambda mt: mm0 + mt + 1
        ev_of = lambda mt: ve0 + mt + 1
        @block.tensor
        def _(te, c=c, mm0=mm0, ve0=ve0):
            for mt in range(NB):
                a = mt % 2
                if c == 0 and mt == 0:
                    te.wait_ge(wl.h, p9_w_ld)
                if mt >= 2:
                    te.wait_ge(ve.h, ev_of(mt - 2))
                elif c > 0:
                    te.wait_ge(ve.h, ve0)
                for k in range(NB):
                    ins = te.matmul(PF[a][:, 0:512], wo1_sb[:, k * D + mt * 128: k * D + (mt + 1) * 128],
                                    attn1[:, k * T + c * 512:(k * T) + (c + 1) * 512],
                                    start=(k == 0), stop=(k == NB - 1))
                    if k == NB - 1:
                        ins.then_inc(mm.h, mm.step())
        @block.vector
        def _(v, c=c, stv=st.val):
            for mt in range(NB):
                a = mt % 2
                v.wait_ge(mm.h, mm_of(mt))
                if mt == 0:
                    v.wait_ge(st.h, stv)
                v.drain()
                v.tensor_copy(s1_ev[:, mt * 512:(mt + 1) * 512], PF[a][:, 0:512]).then_inc(ve.h, ve.step())
        @block.sync
        def _(s, c=c):
            s.wait_ge(st.h, st.val)
            for mt in range(NB):
                a = mt % 2
                s.wait_ge(ve.h, ev_of(mt))
                s.dma_start(out=s1fm[mt, :, c * 512:(c + 1) * 512], in_=s1_ev[:, mt * 512:(mt + 1) * 512]).then_inc(st.h, st.step(16))
        assert mm.val == mm0 + NB and ve.val == ve0 + NB
    p9.close()
    attn1_es.close()
    if upto == "s1fm":
        return True

    phase_barrier(block, sems)
    p10 = ExitStack()
    _a = p10.enter_context
    wfc_sb = _a(nc.sbuf_tensor("wfc_sb", [128, NB * FF1], BF16))
    xnf1 = _a(nc.sbuf_tensor("p10_xnf", [128, NB * 512], BF16))
    r_f = _a(nc.sbuf_tensor("p10_r", [128, 2 * 512], F32))
    h_b = _a(nc.sbuf_tensor("p10_h", [128, 2 * 512], BF16))
    @block.gpsimd
    def _(g):
        g.wait_ge(wl.h, wl.val)
        for k in range(NB):
            g.dma_start(out=wfc_sb[:, k * FF1:(k + 1) * FF1], in_=wfcT[k * 128:(k + 1) * 128, :]).then_inc(wl.h, wl.step(16))
    p10_w_ld = wl.val
    for c in range(NCH):
        @block.sync
        def _(s, c=c):
            s.wait_ge(ld.h, ld.val)
            s.wait_ge(mm.h, mm.val)
            for j in range(NB):
                s.dma_start(out=xnf1[:, j * 512:(j + 1) * 512],
                            in_=xn1_tm[c * 512:(c + 1) * 512, j * 128:(j + 1) * 128],
                            transpose=True).then_inc(ld.h, ld.step(16))
        ld_chunk = ld.val
        for mt in range(FF1 // 128):
            a = mt % 2
            g_idx = c * 32 + mt
            @block.tensor
            def _(te, c=c, mt=mt, a=a, g_idx=g_idx, ld_chunk=ld_chunk):
                if g_idx == 0:
                    te.wait_ge(wl.h, p10_w_ld)
                if mt == 0:
                    te.wait_ge(ld.h, ld_chunk)
                if g_idx >= 2:
                    te.wait_ge(sc.h, sc.val)
                for k in range(NB):
                    ins = te.matmul(PF[a][:, 0:512], wfc_sb[:, k * FF1 + mt * 128: k * FF1 + (mt + 1) * 128],
                                    xnf1[:, k * 512:(k + 1) * 512], start=(k == 0), stop=(k == NB - 1))
                    if k == NB - 1:
                        ins.then_inc(mm.h, mm.step())
            mm_pre = mm.val
            @block.scalar
            def _(scl, a=a, g_idx=g_idx, mm_pre=mm_pre):
                scl.wait_ge(mm.h, mm_pre)
                if g_idx >= 2:
                    scl.wait_ge(ve.h, ve.val)
                scl.activation(r_f[:, a * 512:(a + 1) * 512], PF[a][:, 0:512], AF.Relu).then_inc(sc.h, sc.step())
            relu_done = sc.val
            @block.vector
            def _(v, a=a, g_idx=g_idx, relu_done=relu_done):
                v.wait_ge(sc.h, relu_done)
                if g_idx >= 2:
                    v.wait_ge(gp.h, gp.val)
                v.tensor_tensor(h_b[:, a * 512:(a + 1) * 512], r_f[:, a * 512:(a + 1) * 512],
                                r_f[:, a * 512:(a + 1) * 512], ALU.mult).then_inc(ve.h, ve.step())
            sqr_done = ve.val
            @block.gpsimd
            def _(g, c=c, mt=mt, a=a, sqr_done=sqr_done):
                g.wait_ge(gp.h, gp.val)
                g.wait_ge(ve.h, sqr_done)
                g.dma_start(out=h1sp[mt, :, c * 512:(c + 1) * 512], in_=h_b[:, a * 512:(a + 1) * 512]).then_inc(gp.h, gp.step(16))
    p10.close()
    if upto == "h1sp":
        return True

    phase_barrier(block, sems)
    p11 = ExitStack()
    _a = p11.enter_context
    wpr_sb = _a(nc.sbuf_tensor("wpr_sb", [128, (FF1 // 128) * D], BF16))
    h1f = _a(nc.sbuf_tensor("p11_h", [128, (FF1 // 128) * 512], BF16))
    yb = _a(nc.sbuf_tensor("p11_y", [128, 2 * 512], F32))
    memb = _a(nc.sbuf_tensor("p11_mem", [128, 2 * 512], F32))
    sb_ = _a(nc.sbuf_tensor("p11_s", [128, 2 * 512], F32))
    accb = _a(nc.sbuf_tensor("p11_acc", [128, 2 * 512], F32))
    g1b = _a(nc.sbuf_tensor("p11_g1", [128, 2 * 512], F32))
    s1c = _a(nc.sbuf_tensor("p11_s1", [128, NB * 512], F32))
    sgb = _a(nc.sbuf_tensor("p11_sg", [128, 2 * 512], F32))
    x2t = _a(nc.sbuf_tensor("p11_x2", [128, 4 * D], F32))
    xrt = _a(nc.sbuf_tensor("p11_xr", [128, 4 * D], F32))
    @block.gpsimd
    def _(g):
        g.wait_ge(wl.h, wl.val)
        for k in range(FF1 // 128):
            g.dma_start(out=wpr_sb[:, k * D:(k + 1) * D], in_=wprT[k * 128:(k + 1) * 128, :]).then_inc(wl.h, wl.step(16))
    p11_w_ld = wl.val
    for c in range(NCH):
        @block.sync
        def _(s, c=c):
            s.wait_ge(ld.h, ld.val)
            s.wait_ge(mm.h, mm.val)
            s.wait_ge(ve.h, ve.val)
            for part in range(4):
                s.dma_start(out=h1f[:, part * 8 * 512:(part + 1) * 8 * 512].rearrange("p (k t) -> p k t", k=8),
                            in_=h1sp[part * 8:(part + 1) * 8, :, c * 512:(c + 1) * 512].rearrange("k p t -> p k t")).then_inc(ld.h, ld.step(16))
            s.dma_start(out=s1c[:].rearrange("p (k t) -> p k t", k=NB),
                        in_=s1fm[:, :, c * 512:(c + 1) * 512].rearrange("k p t -> p k t")).then_inc(ld.h, ld.step(16))
            for tt in range(4):
                s.dma_start(out=xrt[:, tt * D:(tt + 1) * D],
                            in_=x1_tm[(c * 4 + tt) * 128:(c * 4 + tt + 1) * 128, :]).then_inc(ld.h, ld.step(16))
        ld_chunk = ld.val
        for mt in range(NB):
            a = mt % 2
            g_idx = c * NB + mt
            @block.tensor
            def _(te, c=c, mt=mt, a=a, g_idx=g_idx, ld_chunk=ld_chunk):
                if g_idx == 0:
                    te.wait_ge(wl.h, p11_w_ld)
                if mt == 0:
                    te.wait_ge(ld.h, ld_chunk)
                if g_idx >= 2:
                    te.wait_ge(ve.h, ve.val)
                for k in range(FF1 // 128):
                    ins = te.matmul(PF[a][:, 0:512], wpr_sb[:, k * D + mt * 128: k * D + (mt + 1) * 128],
                                    h1f[:, k * 512:(k + 1) * 512], start=(k == 0), stop=(k == FF1 // 128 - 1))
                    if k == FF1 // 128 - 1:
                        ins.then_inc(mm.h, mm.step())
            mm_y = mm.val
            @block.vector
            def _(v, a=a, mm_y=mm_y):
                v.wait_ge(mm.h, mm_y)
                v.tensor_copy(yb[:, a * 512:(a + 1) * 512], PF[a][:, 0:512])
                v.drain()
                v.tensor_copy(memb[:, a * 512:(a + 1) * 512], yb[:, a * 512:(a + 1) * 512]).then_inc(ve.h, ve.step())
                for step in range(5):
                    v.drain()
                    v.tensor_scalar(sb_[:, a * 512:(a + 1) * 512], memb[:, a * 512:(a + 1) * 512],
                                    1.0, None, ALU.is_gt)
                    v.drain()
                    if step == 0:
                        v.tensor_copy(accb[:, a * 512:(a + 1) * 512], sb_[:, a * 512:(a + 1) * 512])
                    else:
                        v.tensor_tensor(accb[:, a * 512:(a + 1) * 512], accb[:, a * 512:(a + 1) * 512],
                                        sb_[:, a * 512:(a + 1) * 512], ALU.add)
                    if step < 4:
                        v.scalar_tensor_tensor(g1b[:, a * 512:(a + 1) * 512], sb_[:, a * 512:(a + 1) * 512], -1.0,
                                               yb[:, a * 512:(a + 1) * 512], ALU.mult, ALU.add)
                        v.drain()
                        v.scalar_tensor_tensor(memb[:, a * 512:(a + 1) * 512], memb[:, a * 512:(a + 1) * 512], 0.9,
                                               g1b[:, a * 512:(a + 1) * 512], ALU.mult, ALU.add)
                v.drain()
                v.nop().then_inc(ve.h, ve.step())
            snn_v = ve.val
            @block.scalar
            def _(scl, a=a, snn_v=snn_v):
                scl.wait_ge(ve.h, snn_v)
                scl.activation(g1b[:, a * 512:(a + 1) * 512], accb[:, a * 512:(a + 1) * 512], AF.Silu,
                               scale=0.2).then_inc(sc.h, sc.step())
            g1_done = sc.val
            @block.vector
            def _(v, mt=mt, a=a, g1_done=g1_done):
                v.wait_ge(sc.h, g1_done)
                v.tensor_tensor(sgb[:, a * 512:(a + 1) * 512], g1b[:, a * 512:(a + 1) * 512],
                                s1c[:, mt * 512:(mt + 1) * 512], ALU.mult).then_inc(ve.h, ve.step())
            sg_done = ve.val
            for tt in range(4):
                tb = 2 + tt % 2
                @block.tensor
                def _(te, mt=mt, a=a, tt=tt, tb=tb, sg_done=sg_done):
                    te.wait_ge(ve.h, sg_done if tt == 0 else ve.val)
                    te.transpose(PF[tb][:, 0:128], sgb[:, a * 512 + tt * 128: a * 512 + (tt + 1) * 128],
                                 identf[:]).then_inc(mm.h, mm.step())
                trd = mm.val
                @block.vector
                def _(v, mt=mt, tt=tt, tb=tb, trd=trd, stv=st.val):
                    if mt == 0 and tt == 0:
                        v.wait_ge(st.h, stv)
                    v.wait_ge(mm.h, trd)
                    v.tensor_copy(x2t[:, tt * D + mt * 128: tt * D + (mt + 1) * 128], PF[tb][:, 0:128]).then_inc(ve.h, ve.step())
        for tt in range(4):
            i_tile = c * 4 + tt
            @block.vector
            def _(v, tt=tt):
                v.wait_ge(ve.h, ve.val)
                v.tensor_tensor(x2t[:, tt * D:(tt + 1) * D], x2t[:, tt * D:(tt + 1) * D],
                                xrt[:, tt * D:(tt + 1) * D], ALU.add).then_inc(ve.h, ve.step())
            add_done = ve.val
            @block.sync
            def _(s, i_tile=i_tile, tt=tt, add_done=add_done):
                s.wait_ge(st.h, st.val)
                s.wait_ge(ve.h, add_done)
                s.dma_start(out=y_out[i_tile * 128:(i_tile + 1) * 128, :], in_=x2t[:, tt * D:(tt + 1) * D]).then_inc(st.h, st.step(16))
    p11.close()
    return False


def _emit_p7(E):
    (nc, block, sems, PF, PB, identf, identb, onesrow, onescol, cst,
     xn1_tm, kaug_sp, qaug_sp, v1_tm) = (
        E["nc"], E["block"], E["sems"], E["PF"], E["PB"], E["identf"], E["identb"],
        E["onesrow"], E["onescol"], E["cst"], E["xn1_tm"], E["kaug_sp"], E["qaug_sp"], E["v1_tm"])
    wkv, gkv, wkdec, wvdec, wq1, wqdec, wkrope, wqrope = (
        E["wkv"], E["gkv"], E["wkdec"], E["wvdec"], E["wq1"], E["wqdec"], E["wkrope"], E["wqrope"])
    cos4_d, sin4_d, rot_d = E["cos4_d"], E["sin4_d"], E["rot_d"]
    ld, wl, mm, ve, sc, gp, st = (sems[k] for k in ["ld", "wl", "mm", "ve", "sc", "gp", "st"])

    p7 = ExitStack()
    _a = p7.enter_context
    xnf = _a(nc.sbuf_tensor("p7_xnf", [128, NB * 512], BF16))
    wkv_sb = _a(nc.sbuf_tensor("wkv_sb", [128, NB * 256], BF16))
    wkdec_sb = _a(nc.sbuf_tensor("wkdec_sb", [128, 2 * D], BF16))
    wvdec_sb = _a(nc.sbuf_tensor("wvdec_sb", [128, 2 * D], BF16))
    wq1_sb = _a(nc.sbuf_tensor("wq1_sb", [128, NB * 512], BF16))
    wqdec_sb = _a(nc.sbuf_tensor("wqdec_sb", [128, 4 * D], BF16))
    wkrope_sb = _a(nc.sbuf_tensor("wkrope_sb", [128, NB * 512], BF16))
    wqrope_sb = _a(nc.sbuf_tensor("wqrope_sb", [128, 4 * 512], BF16))
    gkv_sb = _a(nc.sbuf_tensor("gkv_sb", [128, 2], F32))
    cos_sb = _a(nc.sbuf_tensor("cos_sb", [128, T], BF16))
    sin_sb = _a(nc.sbuf_tensor("sin_sb", [128, T], BF16))
    rot_sb = _a(nc.sbuf_tensor("rot_sb", [128, 128], BF16))
    kvc_f = _a(nc.sbuf_tensor("p7_kvc", [128, 2 * 512], F32))
    sqbuf = _a(nc.sbuf_tensor("p7_sq", [128, 2 * 512], F32))
    kvcn_b = _a(nc.sbuf_tensor("p7_kvcn", [128, 2 * 512], BF16))
    qc_b = _a(nc.sbuf_tensor("p7_qc", [128, 4 * 512], BF16))
    kr_b = _a(nc.sbuf_tensor("p7_kr", [128, 512], BF16))
    t1_f = _a(nc.sbuf_tensor("p7_t1", [128, 512], F32))
    t2_f = _a(nc.sbuf_tensor("p7_t2", [128, 512], F32))
    kstage = _a(nc.sbuf_tensor("p7_kst", [128, HEADS * 512], BF16))
    qstage = _a(nc.sbuf_tensor("p7_qst", [128, HEADS * 512], BF16))
    rows = _a(nc.sbuf_tensor("p7_rows", [1, 2 * 512], F32))
    maxk = _a(nc.sbuf_tensor("p7_maxk", [1, 64], F32))

    @block.gpsimd
    def _(g):
        g.wait_ge(wl.h, wl.val)
        for k in range(NB):
            g.dma_start(out=wkv_sb[:, k * 256:(k + 1) * 256], in_=wkv[k * 128:(k + 1) * 128, :]).then_inc(wl.h, wl.step(16))
            g.dma_start(out=wq1_sb[:, k * 512:(k + 1) * 512], in_=wq1[k * 128:(k + 1) * 128, :]).then_inc(wl.h, wl.step(16))
            g.dma_start(out=wkrope_sb[:, k * 512:(k + 1) * 512], in_=wkrope[k * 128:(k + 1) * 128, :]).then_inc(wl.h, wl.step(16))
        for k in range(2):
            g.dma_start(out=wkdec_sb[:, k * D:(k + 1) * D], in_=wkdec[k * 128:(k + 1) * 128, :]).then_inc(wl.h, wl.step(16))
            g.dma_start(out=wvdec_sb[:, k * D:(k + 1) * D], in_=wvdec[k * 128:(k + 1) * 128, :]).then_inc(wl.h, wl.step(16))
        for k in range(4):
            g.dma_start(out=wqdec_sb[:, k * D:(k + 1) * D], in_=wqdec[k * 128:(k + 1) * 128, :]).then_inc(wl.h, wl.step(16))
            g.dma_start(out=wqrope_sb[:, k * 512:(k + 1) * 512], in_=wqrope[k * 128:(k + 1) * 128, :]).then_inc(wl.h, wl.step(16))
    p7_w_ld = wl.val
    @block.sync
    def _(s):
        s.wait_ge(ld.h, ld.val)
        s.dma_start(out=gkv_sb[:], in_=gkv).then_inc(ld.h, ld.step(16))
        s.dma_start(out=cos_sb[:], in_=cos4_d).then_inc(ld.h, ld.step(16))
        s.dma_start(out=sin_sb[:], in_=sin4_d).then_inc(ld.h, ld.step(16))
        s.dma_start(out=rot_sb[:], in_=rot_d).then_inc(ld.h, ld.step(16))
    consts_ld = ld.val

    @block.vector
    def _(v):
        v.memset(v1_tm[:], 1.0).then_inc(ve.h, ve.step())
        v.memset(maxk[:], -1e30).then_inc(ve.h, ve.step())

    for c in range(NCH):
        @block.sync
        def _(s, c=c):
            s.wait_ge(ld.h, ld.val)
            s.wait_ge(mm.h, mm.val)
            for j in range(NB):
                s.dma_start(out=xnf[:, j * 512:(j + 1) * 512],
                            in_=xn1_tm[c * 512:(c + 1) * 512, j * 128:(j + 1) * 128],
                            transpose=True).then_inc(ld.h, ld.step(16))
        ld_chunk = ld.val

        for m in range(2):
            @block.tensor
            def _(te, c=c, m=m, ld_chunk=ld_chunk):
                if c == 0 and m == 0:
                    te.wait_ge(wl.h, p7_w_ld)
                    te.wait_ge(ld.h, consts_ld)
                if m == 0:
                    te.wait_ge(ld.h, ld_chunk)
                te.wait_ge(ve.h, ve.val)
                for k in range(NB):
                    ins = te.matmul(PF[m][:, 0:512], wkv_sb[:, k * 256 + m * 128: k * 256 + (m + 1) * 128],
                                    xnf[:, k * 512:(k + 1) * 512], start=(k == 0), stop=(k == NB - 1))
                    if k == NB - 1:
                        ins.then_inc(mm.h, mm.step())
            mm_kv = mm.val
            @block.vector
            def _(v, m=m, mm_kv=mm_kv):
                v.wait_ge(mm.h, mm_kv)
                v.tensor_copy(kvc_f[:, m * 512:(m + 1) * 512], PF[m][:, 0:512]).then_inc(ve.h, ve.step())
        kv_ev = ve.val
        @block.vector
        def _(v, kv_ev=kv_ev):
            v.drain()
            v.tensor_tensor(sqbuf[:], kvc_f[:], kvc_f[:], ALU.mult).then_inc(ve.h, ve.step())
        sq_done = ve.val
        @block.tensor
        def _(te, sq_done=sq_done):
            te.wait_ge(ve.h, sq_done)
            te.matmul(PF[2][0:1, 0:512], onescol[:], sqbuf[:, 0:512], start=True, stop=False)
            te.matmul(PF[2][0:1, 0:512], onescol[:], sqbuf[:, 512:1024], start=False, stop=True).then_inc(mm.h, mm.step())
        ms_done = mm.val
        @block.scalar
        def _(scl, ms_done=ms_done):
            scl.wait_ge(mm.h, ms_done)
            scl.activation(rows[0:1, 0:512], PF[2][0:1, 0:512], AF.Ln, bias=cst[0:1, 1:2], scale=1.0 / 256)
            scl.drain()
            scl.activation(rows[0:1, 512:1024], rows[0:1, 0:512], AF.Exp, scale=-0.5).then_inc(sc.h, sc.step())
        msr_done = sc.val
        @block.vector
        def _(v, msr_done=msr_done):
            v.wait_ge(sc.h, msr_done)
            v.nop().then_inc(ve.h, ve.step())
        rstd_done = ve.val
        @block.tensor
        def _(te, rstd_done=rstd_done):
            te.wait_ge(ve.h, rstd_done)
            te.matmul(PF[3][:, 0:512], onesrow[:], rows[0:1, 512:1024], start=True, stop=True).then_inc(mm.h, mm.step())
        bc_done = mm.val
        for m in range(2):
            @block.vector
            def _(v, m=m, bc_done=bc_done):
                v.wait_ge(mm.h, bc_done)
                v.scalar_tensor_tensor(kvcn_b[:, m * 512:(m + 1) * 512], kvc_f[:, m * 512:(m + 1) * 512],
                                       gkv_sb[:, m:m + 1], PF[3][:, 0:512], ALU.mult, ALU.mult).then_inc(ve.h, ve.step())
        kvcn_done = ve.val

        for mt in range(NB):
            a = mt % 2
            @block.tensor
            def _(te, mt=mt, a=a, kvcn_done=kvcn_done):
                te.wait_ge(ve.h, kvcn_done if mt < 2 else ve.val)
                for k in range(2):
                    ins = te.matmul(PF[a][:, 0:512], wkdec_sb[:, k * D + mt * 128: k * D + (mt + 1) * 128],
                                    kvcn_b[:, k * 512:(k + 1) * 512], start=(k == 0), stop=(k == 1))
                    if k == 1:
                        ins.then_inc(mm.h, mm.step())
            mm_kd = mm.val
            @block.vector
            def _(v, mt=mt, a=a, mm_kd=mm_kd, stv=st.val):
                if mt == 0:
                    v.wait_ge(st.h, stv)
                v.wait_ge(mm.h, mm_kd)
                v.tensor_copy(kstage[0:64, (2 * mt) * 512:(2 * mt + 1) * 512], PF[a][0:64, 0:512])
                v.tensor_copy(kstage[0:64, (2 * mt + 1) * 512:(2 * mt + 2) * 512], PF[a][64:128, 0:512]).then_inc(ve.h, ve.step())
        for tt in range(4):
            for nh in range(2):
                a = (tt * 2 + nh) % 2
                @block.tensor
                def _(te, c=c, tt=tt, nh=nh, a=a):
                    te.wait_ge(ve.h, ve.val)
                    for k in range(2):
                        ins = te.matmul(PF[a][:, 0:512], kvcn_b[:, k * 512 + tt * 128: k * 512 + (tt + 1) * 128],
                                        wvdec_sb[:, k * D + nh * 512: k * D + (nh + 1) * 512],
                                        start=(k == 0), stop=(k == 1))
                        if k == 1:
                            ins.then_inc(mm.h, mm.step())
                mm_v = mm.val
                @block.vector
                def _(v, c=c, tt=tt, nh=nh, a=a, mm_v=mm_v):
                    v.wait_ge(mm.h, mm_v)
                    base = (c * 4 + tt) * HEADS * 65 + nh * 8 * 65
                    dst = v1_tm[:, base: base + 8 * 65].rearrange("p (h e) -> p h e", h=8)[:, :, 0:64]
                    v.tensor_copy(dst, PF[a][:, 0:512].rearrange("p (h e) -> p h e", h=8)).then_inc(ve.h, ve.step())

        for m in range(4):
            a = m % 2
            @block.tensor
            def _(te, m=m, a=a):
                te.wait_ge(ve.h, ve.val)
                for k in range(NB):
                    ins = te.matmul(PF[a][:, 0:512], wq1_sb[:, k * 512 + m * 128: k * 512 + (m + 1) * 128],
                                    xnf[:, k * 512:(k + 1) * 512], start=(k == 0), stop=(k == NB - 1))
                    if k == NB - 1:
                        ins.then_inc(mm.h, mm.step())
            mm_qc = mm.val
            @block.vector
            def _(v, m=m, a=a, mm_qc=mm_qc):
                v.wait_ge(mm.h, mm_qc)
                v.tensor_scalar(qc_b[:, m * 512:(m + 1) * 512], PF[a][:, 0:512], QSCALE, None, ALU.mult).then_inc(ve.h, ve.step())
        qc_done = ve.val
        for mt in range(NB):
            a = mt % 2
            @block.tensor
            def _(te, mt=mt, a=a, qc_done=qc_done):
                te.wait_ge(ve.h, qc_done if mt < 2 else ve.val)
                for k in range(4):
                    ins = te.matmul(PF[a][:, 0:512], wqdec_sb[:, k * D + mt * 128: k * D + (mt + 1) * 128],
                                    qc_b[:, k * 512:(k + 1) * 512], start=(k == 0), stop=(k == 3))
                    if k == 3:
                        ins.then_inc(mm.h, mm.step())
            mm_qd = mm.val
            @block.vector
            def _(v, mt=mt, a=a, mm_qd=mm_qd):
                v.wait_ge(mm.h, mm_qd)
                v.tensor_copy(qstage[0:64, (2 * mt) * 512:(2 * mt + 1) * 512], PF[a][0:64, 0:512])
                v.tensor_copy(qstage[0:64, (2 * mt + 1) * 512:(2 * mt + 2) * 512], PF[a][64:128, 0:512]).then_inc(ve.h, ve.step())

        for which, wsb, nk, stage in (("k", wkrope_sb, NB, kstage), ("q", wqrope_sb, 4, qstage)):
            src_b = xnf if which == "k" else qc_b
            for m in range(4):
                a = m % 2
                @block.tensor
                def _(te, which=which, wsb=wsb, nk=nk, src_b=src_b, m=m, a=a):
                    te.wait_ge(ve.h, ve.val)
                    for k in range(nk):
                        ins = te.matmul(PF[a][:, 0:512], wsb[:, k * 512 + m * 128: k * 512 + (m + 1) * 128],
                                        src_b[:, k * 512:(k + 1) * 512], start=(k == 0), stop=(k == nk - 1))
                        if k == nk - 1:
                            ins.then_inc(mm.h, mm.step())
                mm_r = mm.val
                @block.vector
                def _(v, a=a, mm_r=mm_r):
                    v.wait_ge(mm.h, mm_r)
                    v.tensor_copy(kr_b[:], PF[a][:, 0:512]).then_inc(ve.h, ve.step())
                kr_done = ve.val
                @block.tensor
                def _(te, a=a, kr_done=kr_done):
                    te.wait_ge(ve.h, kr_done)
                    te.matmul(PF[4][:, 0:512], rot_sb[:], kr_b[:], start=True, stop=True).then_inc(mm.h, mm.step())
                rot_done = mm.val
                @block.vector
                def _(v, c=c, m=m, stage=stage, rot_done=rot_done):
                    v.tensor_tensor(t1_f[:], kr_b[:], cos_sb[:, c * 512:(c + 1) * 512], ALU.mult)
                    v.wait_ge(mm.h, rot_done)
                    v.tensor_tensor(t2_f[:], PF[4][:, 0:512], sin_sb[:, c * 512:(c + 1) * 512], ALU.mult)
                    v.drain()
                    for hh in range(4):
                        h = 4 * m + hh
                        v.tensor_tensor(stage[64:96, h * 512:(h + 1) * 512],
                                        t1_f[32 * hh:32 * (hh + 1), :], t2_f[32 * hh:32 * (hh + 1), :], ALU.add)
                    v.drain()
                    v.nop().then_inc(ve.h, ve.step())

        @block.vector
        def _(v):
            v.memset(kstage[96:97, :], 1.0).then_inc(ve.h, ve.step())
        for which, stage in (("k", kstage), ("q", qstage)):
            mm0, sc0, ve0 = mm.val, sc.val, ve.val
            nrm_of = lambda h: mm0 + h + 1
            nrow_of = lambda h: sc0 + h + 1
            order7 = []
            for _h in range(HEADS):
                order7.append(("sq", _h))
                if _h >= 4:
                    order7.append(("post", _h - 4))
            for _h in range(HEADS - 4, HEADS):
                order7.append(("post", _h))
            sq_pos = {}
            post_pos = {}
            for _j, (_kind, _h) in enumerate(order7):
                (sq_pos if _kind == "sq" else post_pos)[_h] = ve0 + _j + 1
            sq_of = lambda h: sq_pos[h]
            post_of = lambda h: post_pos[h]
            @block.vector
            def _(v, which=which, stage=stage, ve0=ve0, order7=tuple(order7)):
                for kind, h in order7:
                    if kind == "sq":
                        if h >= 2:
                            v.wait_ge(mm.h, nrm_of(h - 2))
                        v.drain()
                        v.tensor_tensor(sqbuf[0:96, (h % 2) * 512:(h % 2) * 512 + 512], stage[0:96, h * 512:(h + 1) * 512],
                                        stage[0:96, h * 512:(h + 1) * 512], ALU.mult).then_inc(ve.h, ve.step())
                    else:
                        v.wait_ge(sc.h, nrow_of(h))
                        if which == "k":
                            v.reduce_max(maxk[0:1, 32 + h:33 + h], rows[0:1, (h % 2) * 512:(h % 2) * 512 + 512], axis=AX.X)
                            v.drain()
                            v.tensor_tensor(maxk[0:1, h:h + 1], maxk[0:1, h:h + 1], maxk[0:1, 32 + h:33 + h], ALU.max)
                            v.drain()
                            v.tensor_scalar(maxk[0:1, 48 + h:49 + h], maxk[0:1, h:h + 1], -1.0, None, ALU.mult)
                            v.drain()
                            v.nop().then_inc(ve.h, ve.step())
                        else:
                            v.tensor_scalar(qstage[96:97, h * 512:(h + 1) * 512], rows[0:1, (h % 2) * 512:(h % 2) * 512 + 512],
                                            maxk[0:1, 48 + h:49 + h], CQ_MARGIN, ALU.mult, ALU.add).then_inc(ve.h, ve.step())
            @block.tensor
            def _(te, which=which, ve0=ve0, sc0=sc0):
                for h in range(HEADS):
                    te.wait_ge(ve.h, sq_of(h))
                    if h >= 2:
                        te.wait_ge(sc.h, nrow_of(h - 2))
                    else:
                        te.wait_ge(sc.h, sc0)
                    te.matmul(PF[2 + h % 2][0:1, 0:512], onescol[0:96, :],
                              sqbuf[0:96, (h % 2) * 512:(h % 2) * 512 + 512], start=True, stop=True).then_inc(mm.h, mm.step())
            @block.scalar
            def _(scl, which=which, ve0=ve0):
                for h in range(HEADS):
                    scl.wait_ge(mm.h, nrm_of(h))
                    if h >= 2:
                        scl.wait_ge(ve.h, post_of(h - 2))
                    scl.activation(rows[0:1, (h % 2) * 512:(h % 2) * 512 + 512], PF[2 + h % 2][0:1, 0:512], AF.Ln)
                    scl.drain()
                    scl.activation(rows[0:1, (h % 2) * 512:(h % 2) * 512 + 512],
                                   rows[0:1, (h % 2) * 512:(h % 2) * 512 + 512], AF.Exp, scale=0.5).then_inc(sc.h, sc.step())
            assert ve.val == ve0 + 32 and mm.val == mm0 + 16 and sc.val == sc0 + 16
        stage_done = ve.val
        @block.sync
        def _(s, c=c, stage_done=stage_done):
            s.wait_ge(st.h, st.val)
            s.wait_ge(ve.h, stage_done)
            for h in range(HEADS):
                s.dma_start(out=kaug_sp[h, :, c * 512:(c + 1) * 512], in_=kstage[0:QK + 1, h * 512:(h + 1) * 512]).then_inc(st.h, st.step(16))
                s.dma_start(out=qaug_sp[h, :, c * 512:(c + 1) * 512], in_=qstage[0:QK + 1, h * 512:(h + 1) * 512]).then_inc(st.h, st.step(16))
    p7.close()


def _emit_p6(E):
    (nc, block, sems, PF, PB, identf, identb, onesrow, dcol, cst,
     projTs, ctx_sb, stt0, x_in, wo0, bo0, qfm_sp, g0fm, x1_tm, xn1_tm) = (
        E["nc"], E["block"], E["sems"], E["PF"], E["PB"], E["identf"], E["identb"],
        E["onesrow"], E["dcol"], E["cst"], E["projTs"], E["ctx_sb"], E["stt0"],
        E["x_in"], E["wo0"], E["bo0"], E["qfm_sp"], E["g0fm"], E["x1_tm"], E["xn1_tm"])
    ld, wl, mm, ve, sc, gp, st = (sems[k] for k in ["ld", "wl", "mm", "ve", "sc", "gp", "st"])
    phase_barrier(block, sems)
    p6es = ExitStack()
    _a = p6es.enter_context
    p6_qf = _a(nc.sbuf_tensor("p6_qf", [128, 2 * NB * 512], BF16))
    p6_q2 = _a(nc.sbuf_tensor("p6_q2", [128, 512], F32))
    p6_qpf = _a(nc.sbuf_tensor("p6_qpf", [128, 2 * PM], F32))
    p6_qpb = _a(nc.sbuf_tensor("p6_qpb", [128, 2 * PM], BF16))
    p6_qTs = _a(nc.sbuf_tensor("p6_qTs", [128, 2 * HEADS * 512], BF16))
    p6_dinv = _a(nc.sbuf_tensor("p6_dinv", [1, 1024], F32))
    p6_tmpo = _a(nc.sbuf_tensor("p6_tmpo", [64, 1024], F32))
    p6_attn = _a(nc.sbuf_tensor("p6_attn", [128, NB * 512], BF16))
    wo_sb = _a(nc.sbuf_tensor("wo_sb", [128, NB * D], BF16))
    bo_sb = _a(nc.sbuf_tensor("bo_sb", [128, 8], F32))
    p6_g0 = _a(nc.sbuf_tensor("p6_g0", [128, NB * 512], F32))
    p6_s0 = _a(nc.sbuf_tensor("p6_s0", [128, 2 * 512], F32))
    p6_sg = _a(nc.sbuf_tensor("p6_sg", [128, 2 * 512], F32))
    p6_x1 = _a(nc.sbuf_tensor("p6_x1", [128, 4 * D], F32))
    p6_xr = _a(nc.sbuf_tensor("p6_xr", [128, 4 * D], F32))
    p6_xc = _a(nc.sbuf_tensor("p6_xc", [128, D], F32))
    p6_sq = _a(nc.sbuf_tensor("p6_sq", [128, D], F32))
    p6_xn = _a(nc.sbuf_tensor("p6_xn", [128, 2 * D], BF16))
    p6_ln = _a(nc.sbuf_tensor("p6_ln", [128, 8], F32))
    if True:
        @block.gpsimd
        def _(g):
            for k in range(NB):
                g.dma_start(out=wo_sb[:, k * D:(k + 1) * D], in_=wo0[k * 128:(k + 1) * 128, :]).then_inc(wl.h, wl.step(16))
        p6_w_ld = wl.val
        @block.sync
        def _(s):
            s.dma_start(out=bo_sb[:], in_=bo0).then_inc(ld.h, ld.step(16))

        for c in range(NCH):
            cb = c % 2
            chunk_start_mm = mm.val
            @block.sync
            def _(s, c=c, cb=cb):
                s.wait_ge(ld.h, ld.val)
                if c > 0:
                    s.wait_ge(ve.h, ve.val)
                s.dma_start(out=p6_qf[:, cb * NB * 512:(cb + 1) * NB * 512].rearrange("p (k t) -> p k t", k=NB),
                            in_=qfm_sp[:, :, c * 512:(c + 1) * 512].rearrange("k p t -> p k t")).then_inc(ld.h, ld.step(16))
                s.dma_start(out=p6_g0[:].rearrange("p (k t) -> p k t", k=NB),
                            in_=g0fm[:, :, c * 512:(c + 1) * 512].rearrange("k p t -> p k t")).then_inc(ld.h, ld.step(16))
                for tt in range(4):
                    s.dma_start(out=p6_xr[:, tt * D:(tt + 1) * D],
                                in_=x_in[(c * 4 + tt) * 128:(c * 4 + tt + 1) * 128, :]).then_inc(ld.h, ld.step(16))
            ld_chunk = ld.val

            for h in range(HEADS):
                r0 = (h % 2) * 64
                blk = h // 2
                dbank = 4 + h % 2
                mm0, sc0, ve0 = mm.val, sc.val, ve.val
                sq_of = ve0 + 1
                bias_of = lambda tt: ve0 + 1 + tt + 1
                cvt_of = lambda tt: ve0 + 5 + tt + 1
                copy_of = lambda tt: ve0 + 9 + tt + 1
                diag_of = mm0 + 4
                dd_of = lambda tt: mm0 + 4 + tt + 1
                tr_of = lambda tt: mm0 + 8 + tt + 1
                exp_of = lambda tt: sc0 + tt + 1
                @block.tensor
                def _(te, c=c, h=h, r0=r0, dbank=dbank, ve0=ve0, sq_of=sq_of, sc0=sc0):
                    te.wait_ge(ve.h, sq_of)
                    te.wait_ge(sc.h, sc0)
                    for tt in range(4):
                        ins = te.matmul(PF[dbank][:, tt:tt + 1], p6_q2[r0:r0 + 64, tt * 128:(tt + 1) * 128],
                                        dcol[r0:r0 + 64, :], start=True, stop=True)
                        if tt == 3:
                            ins.then_inc(mm.h, mm.step(4))
                    for tt in range(4):
                        a = tt % 2
                        if tt >= 2:
                            te.wait_ge(sc.h, exp_of(tt - 2))
                        te.matmul(PF[a][:, 0:PM], p6_qf[r0:r0 + 64, (cb * NB + blk) * 512 + tt * 128:(cb * NB + blk) * 512 + (tt + 1) * 128],
                                  projTs[r0:r0 + 64, :], start=True, stop=True).then_inc(mm.h, mm.step())
                    for tt in range(4):
                        a = tt % 2
                        te.wait_ge(ve.h, cvt_of(tt))
                        if tt >= 2:
                            te.wait_ge(ve.h, copy_of(tt - 2))
                        te.transpose(PB[a][:, 0:128], p6_qpb[:, a * PM: a * PM + 128], identb[:])
                        te.transpose(PB[a][:, 128:256], p6_qpb[:, a * PM + 128: (a + 1) * PM], identb[:]).then_inc(mm.h, mm.step())
                @block.vector
                def _(v, c=c, cb=cb, h=h, r0=r0, blk=blk, dbank=dbank, ld_chunk=ld_chunk):
                    if h == 0:
                        v.wait_ge(ld.h, ld_chunk)
                    else:
                        v.wait_ge(mm.h, mm0)
                    v.drain()
                    v.tensor_tensor(p6_q2[r0:r0 + 64, :], p6_qf[r0:r0 + 64, (cb * NB + blk) * 512:(cb * NB + blk + 1) * 512],
                                    p6_qf[r0:r0 + 64, (cb * NB + blk) * 512:(cb * NB + blk + 1) * 512], ALU.mult).then_inc(ve.h, ve.step())
                    for tt in range(4):
                        a = tt % 2
                        v.wait_ge(mm.h, max(dd_of(tt), diag_of))
                        if tt >= 2:
                            v.wait_ge(sc.h, exp_of(tt - 2))
                        v.reduce_max(stt0[:, 6 + a:7 + a], PF[a][:, 0:PM], axis=AX.X, negate=True)
                        v.tensor_scalar(stt0[:, 4 + a:5 + a], PF[dbank][:, tt:tt + 1], -1.0, LNRATIO, ALU.mult, ALU.add).then_inc(ve.h, ve.step())
                    for tt in range(4):
                        a = tt % 2
                        v.wait_ge(sc.h, exp_of(tt))
                        if tt >= 2:
                            v.wait_ge(mm.h, tr_of(tt - 2))
                        v.tensor_scalar(p6_qpb[:, a * PM:(a + 1) * PM], p6_qpf[:, a * PM:(a + 1) * PM],
                                        stt0[:, 8 + a:9 + a], RATIO * KEPS, ALU.mult, ALU.add).then_inc(ve.h, ve.step())
                    for tt in range(4):
                        a = tt % 2
                        v.wait_ge(mm.h, tr_of(tt))
                        v.tensor_copy(p6_qTs[:, h * 512 + tt * 128: h * 512 + (tt + 1) * 128], PB[a][:, 0:128])
                        v.drain()
                        v.tensor_copy(p6_qTs[:, (HEADS + h) * 512 + tt * 128: (HEADS + h) * 512 + (tt + 1) * 128],
                                      PB[a][:, 128:256]).then_inc(ve.h, ve.step())
                @block.scalar
                def _(scl, h=h):
                    for tt in range(4):
                        a = tt % 2
                        scl.wait_ge(ve.h, bias_of(tt))
                        if tt >= 2:
                            scl.wait_ge(ve.h, cvt_of(tt - 2))
                        scl.activation(stt0[:, 8 + a:9 + a], stt0[:, 6 + a:7 + a], AF.Exp)
                        scl.activation(p6_qpf[:, a * PM:(a + 1) * PM], PF[a][:, 0:PM], AF.Exp,
                                       bias=stt0[:, 4 + a:5 + a]).then_inc(sc.h, sc.step())
                assert mm.val == mm0 + 12 and ve.val == ve0 + 13 and sc.val == sc0 + 4

            stageA_ve = ve.val
            mm0b, ve0b = mm.val, ve.val
            orderB = []
            for _h in range(HEADS):
                orderB.append(("out", _h))
                if _h >= 1:
                    orderB.append(("bc", _h - 1))
            orderB.append(("bc", HEADS - 1))
            out_pos, bc_pos = {}, {}
            for _j, (_k, _h) in enumerate(orderB):
                (out_pos if _k == "out" else bc_pos)[_h] = mm0b + _j + 1
            rec_of = lambda h: ve0b + 3 * h + 1
            cp_of = lambda h: ve0b + 3 * h + 2
            ttb_of = lambda h: ve0b + 3 * h + 3
            @block.tensor
            def _(te, c=c, ve0b=ve0b, orderB=tuple(orderB)):
                for kind, h in orderB:
                    if kind == "out":
                        ob = [2, 5][h % 2]
                        if h == 0:
                            te.wait_ge(ve.h, ve0b)
                        if h >= 2:
                            te.wait_ge(ve.h, cp_of(h - 2))
                        te.matmul(PF[ob][0:65, 0:512], ctx_sb[:, h * 130: h * 130 + 65],
                                  p6_qTs[:, h * 512:(h + 1) * 512], start=True, stop=False)
                        te.matmul(PF[ob][0:65, 0:512], ctx_sb[:, h * 130 + 65: h * 130 + 130],
                                  p6_qTs[:, (HEADS + h) * 512:(HEADS + h + 1) * 512], start=False, stop=True).then_inc(mm.h, mm.step())
                    else:
                        bb = [0, 1][h % 2]
                        te.wait_ge(ve.h, rec_of(h))
                        if h >= 2:
                            te.wait_ge(ve.h, ttb_of(h - 2))
                        te.matmul(PF[bb][0:64, 0:512], onesrow[0:1, 0:64], p6_dinv[0:1, (h % 2) * 512:(h % 2 + 1) * 512],
                                  start=True, stop=True).then_inc(mm.h, mm.step())
            @block.vector
            def _(v, c=c):
                for h in range(HEADS):
                    ob = [2, 5][h % 2]
                    bb = [0, 1][h % 2]
                    r0h = (h % 2) * 64
                    v.wait_ge(mm.h, out_pos[h])
                    if h >= 2:
                        v.wait_ge(mm.h, bc_pos[h - 2])
                    v.reciprocal(p6_dinv[0:1, (h % 2) * 512:(h % 2 + 1) * 512], PF[ob][64:65, 0:512])
                    v.drain()
                    v.nop().then_inc(ve.h, ve.step())
                    v.tensor_copy(p6_tmpo[:, (h % 2) * 512:(h % 2 + 1) * 512], PF[ob][0:64, 0:512])
                    v.drain()
                    v.nop().then_inc(ve.h, ve.step())
                    v.wait_ge(mm.h, bc_pos[h])
                    v.tensor_tensor(p6_attn[r0h:r0h + 64, (h // 2) * 512:(h // 2 + 1) * 512],
                                    p6_tmpo[:, (h % 2) * 512:(h % 2 + 1) * 512], PF[bb][0:64, 0:512],
                                    ALU.mult).then_inc(ve.h, ve.step())
            assert mm.val == mm0b + 2 * HEADS and ve.val == ve0b + 3 * HEADS

            stageB_ve = ve.val
            mm0c, ve0c = mm.val, ve.val
            ordC_t = []
            for _mt in range(NB):
                ordC_t.append(("wo", _mt, 0))
                if _mt >= 1:
                    for _tt in range(4):
                        ordC_t.append(("tr", _mt - 1, _tt))
            for _tt in range(4):
                ordC_t.append(("tr", NB - 1, _tt))
            wo_pos, tr_pos = {}, {}
            for _j, (_k, _mt, _tt) in enumerate(ordC_t):
                if _k == "wo":
                    wo_pos[_mt] = mm0c + _j + 1
                else:
                    tr_pos[(_mt, _tt)] = mm0c + _j + 1
            ordC_v = []
            for _mt in range(NB):
                ordC_v.append(("sg", _mt, 0))
                if _mt >= 1:
                    for _tt in range(4):
                        ordC_v.append(("ev", _mt - 1, _tt))
            for _tt in range(4):
                ordC_v.append(("ev", NB - 1, _tt))
            sg_pos, ev_pos = {}, {}
            for _j, (_k, _mt, _tt) in enumerate(ordC_v):
                if _k == "sg":
                    sg_pos[_mt] = ve0c + _j + 1
                else:
                    ev_pos[(_mt, _tt)] = ve0c + _j + 1
            @block.tensor
            def _(te, c=c, ordC_t=tuple(ordC_t), ve0c=ve0c):
                for kind, mt, tt in ordC_t:
                    if kind == "wo":
                        a = mt % 2
                        if mt == 0:
                            te.wait_ge(ve.h, ve0c)
                            te.wait_ge(wl.h, p6_w_ld)
                        if mt >= 2:
                            te.wait_ge(ve.h, sg_pos[mt - 2])
                        for k in range(NB):
                            ins = te.matmul(PF[a][:, 0:512], wo_sb[:, k * D + mt * 128: k * D + (mt + 1) * 128],
                                            p6_attn[:, k * 512:(k + 1) * 512],
                                            start=(k == 0), stop=(k == NB - 1))
                            if k == NB - 1:
                                ins.then_inc(mm.h, mm.step())
                    else:
                        a = mt % 2
                        tb = 2 + tt % 2
                        te.wait_ge(ve.h, sg_pos[mt])
                        prev = (mt, tt - 2) if tt >= 2 else ((mt - 1, tt + 2) if mt >= 1 else None)
                        if prev is not None and prev in ev_pos:
                            te.wait_ge(ve.h, ev_pos[prev])
                        te.transpose(PF[tb][:, 0:128], p6_sg[:, a * 512 + tt * 128: a * 512 + (tt + 1) * 128],
                                     identf[:]).then_inc(mm.h, mm.step())
            @block.vector
            def _(v, c=c, ordC_v=tuple(ordC_v), stv=st.val):
                first_ev = True
                for kind, mt, tt in ordC_v:
                    if kind == "sg":
                        a = mt % 2
                        v.wait_ge(mm.h, wo_pos[mt])
                        if mt >= 2:
                            v.wait_ge(mm.h, tr_pos[(mt - 2, 3)])
                        v.tensor_scalar(p6_s0[:, a * 512:(a + 1) * 512], PF[a][:, 0:512], bo_sb[:, mt:mt + 1], None, ALU.add)
                        v.drain()
                        v.tensor_tensor(p6_sg[:, a * 512:(a + 1) * 512], p6_s0[:, a * 512:(a + 1) * 512],
                                        p6_g0[:, mt * 512:(mt + 1) * 512], ALU.mult).then_inc(ve.h, ve.step())
                    else:
                        tb = 2 + tt % 2
                        if first_ev:
                            v.wait_ge(st.h, stv)
                            first_ev = False
                        v.wait_ge(mm.h, tr_pos[(mt, tt)])
                        v.tensor_copy(p6_x1[:, tt * D + mt * 128: tt * D + (mt + 1) * 128], PF[tb][:, 0:128]).then_inc(ve.h, ve.step())
            assert mm.val == mm0c + NB + 4 * NB and ve.val == ve0c + NB + 4 * NB

            for tt in range(4):
                i_tile = c * 4 + tt
                @block.vector
                def _(v, c=c, tt=tt):
                    v.wait_ge(ve.h, ve.val)
                    v.wait_ge(st.h, st.val)
                    v.tensor_tensor(p6_x1[:, tt * D:(tt + 1) * D], p6_x1[:, tt * D:(tt + 1) * D],
                                    p6_xr[:, tt * D:(tt + 1) * D], ALU.add).then_inc(ve.h, ve.step())
                    v.drain()
                    v.reduce_sum(p6_ln[:, 0:1], p6_x1[:, tt * D:(tt + 1) * D], axis=AX.X)
                    v.drain()
                    v.tensor_scalar(p6_ln[:, 1:2], p6_ln[:, 0:1], 1.0 / D, None, ALU.mult)
                    v.drain()
                    v.tensor_scalar(p6_xc[:], p6_x1[:, tt * D:(tt + 1) * D], p6_ln[:, 1:2], None, ALU.subtract).then_inc(ve.h, ve.step())
                xc_done = ve.val
                @block.vector
                def _(v, xc_done=xc_done):
                    v.drain()
                    v.tensor_tensor(p6_sq[:], p6_xc[:], p6_xc[:], ALU.mult)
                    v.drain()
                    v.reduce_sum(p6_ln[:, 2:3], p6_sq[:], axis=AX.X)
                    v.drain()
                    v.tensor_scalar(p6_ln[:, 3:4], p6_ln[:, 2:3], 1.0 / D, None, ALU.mult).then_inc(ve.h, ve.step())
                var_done = ve.val
                @block.scalar
                def _(scl, var_done=var_done):
                    scl.wait_ge(ve.h, var_done)
                    scl.activation(p6_ln[:, 4:5], p6_ln[:, 3:4], AF.Ln, bias=cst[:, 0:1])
                    scl.drain()
                    scl.activation(p6_ln[:, 5:6], p6_ln[:, 4:5], AF.Exp, scale=-0.5).then_inc(sc.h, sc.step())
                sd_done = sc.val
                @block.vector
                def _(v, tt=tt, sd_done=sd_done):
                    v.wait_ge(sc.h, sd_done)
                    v.tensor_scalar(p6_xn[:, (tt % 2) * D:(tt % 2 + 1) * D], p6_xc[:], p6_ln[:, 5:6], None, ALU.mult).then_inc(ve.h, ve.step())
                xn_done = ve.val
                @block.sync
                def _(s, i_tile=i_tile, tt=tt, xn_done=xn_done):
                    s.wait_ge(st.h, st.val)
                    s.wait_ge(ve.h, xn_done)
                    s.dma_start(out=x1_tm[i_tile * 128:(i_tile + 1) * 128, :], in_=p6_x1[:, tt * D:(tt + 1) * D]).then_inc(st.h, st.step(16))
                    s.dma_start(out=xn1_tm[i_tile * 128:(i_tile + 1) * 128, :], in_=p6_xn[:, (tt % 2) * D:(tt % 2 + 1) * D]).then_inc(st.h, st.step(16))
    p6es.close()


def phase_barrier(block, sems):
    vals = [(s.h, s.val) for s in sems.values() if s.val > 0]
    for eng in ["tensor", "vector", "scalar", "gpsimd", "sync"]:
        @getattr(block, eng)
        def _(e):
            for h, v in vals:
                e.wait_ge(h, v)


def _finish(block, sems):
    st, ld = sems["st"], sems["ld"]
    @block.sync
    def _(s):
        s.wait_ge(st.h, st.val)
        s.wait_ge(ld.h, ld.val)


def _prep_inputs(inputs, core):
    d = {}
    d["x"] = np.ascontiguousarray(inputs["L0_attn_x"][core])
    d["wq"] = inputs["L0_attn_wq"]
    d["wk"] = inputs["L0_attn_wk"]
    d["wv"] = inputs["L0_attn_wv"]
    d["wo0"] = inputs["L0_attn_wo"]
    d["bo0"] = np.ascontiguousarray(inputs["L0_attn_bo"].reshape(8, 128).T)
    d["projT"] = np.ascontiguousarray(inputs["L0_attn_proj"].T)
    d["wgT"] = np.ascontiguousarray(inputs["L0_ffn_w_gate"].T)
    d["wuT"] = np.ascontiguousarray(inputs["L0_ffn_w_up"].T)
    d["wdT"] = np.ascontiguousarray(inputs["L0_ffn_w_down"].T)
    d["wkv"] = inputs["L1_attn_W_kv"]
    d["gkv"] = np.ascontiguousarray(inputs["L1_attn_g_kv"].reshape(2, 128).T)
    d["wkdec"] = inputs["L1_attn_W_kdec"]
    d["wvdec"] = inputs["L1_attn_W_vdec"]
    d["wq1"] = inputs["L1_attn_W_q"]
    d["wqdec"] = inputs["L1_attn_W_qdec"]
    d["wkrope"] = inputs["L1_attn_W_krope"]
    d["wqrope"] = inputs["L1_attn_W_qrope"]
    d["wo1"] = inputs["L1_attn_W_o"]
    d["wfcT"] = np.ascontiguousarray(inputs["L1_ffn_W_fc"].T)
    d["wprT"] = np.ascontiguousarray(inputs["L1_ffn_W_proj"].T)
    return d


def kernel(**inputs):
    nc, es = build_program()
    in_maps = [_prep_inputs(inputs, core) for core in range(8)]
    res = run_bass_kernel_spmd(nc, in_maps, core_ids=list(range(8)))
    return np.stack([res.results[i]["y"] for i in range(8)], axis=0)
